# version 10
# baseline (speedup 1.0000x reference)
import contextlib
import numpy as np
import ml_dtypes
import concourse.bass as bass
import concourse.mybir as mybir
from concourse.bass_utils import run_bass_kernel_spmd

F32 = mybir.dt.float32
BF16 = mybir.dt.bfloat16
AF = mybir.ActivationFunctionType
OP = mybir.AluOpType
BF = ml_dtypes.bfloat16

N_CORES = 8
TOK = 2048
DM = 2048
EPS = 1e-5
H = 32
D = 64
BS = 256
NB = TOK // BS

FUSED = True



def _hadamard(n):
    h = np.array([[1.0]], dtype=np.float32)
    while h.shape[0] < n:
        h = np.block([[h, h], [h, -h]])
    return h.astype(np.float32)


def _decay():
    base = 1.0 / 2.0 ** (8.0 / H)
    rate = base ** (np.arange(H, dtype=np.float32) + 1.0)
    slope = (rate * np.float32(1.0 + 1e-05)).astype(np.float32)
    r = np.arange(BS, dtype=np.float32) + 1.0
    qd = np.exp(-slope[:, None] * r[None, :]).astype(np.float32)
    kd = np.exp(-slope[:, None] * (BS - r[None, :])).astype(np.float32)
    de = slope[:, None, None] * (r[:, None] - r[None, :])[None, :, :]
    with np.errstate(over="ignore"):
        dd = np.where(de >= 0, np.exp(-de), 0.0).astype(np.float32)
    bd = np.exp(-slope * BS).astype(np.float32)
    return qd, kd, dd, bd


def _pp(vec, chunks):
    return np.ascontiguousarray(np.asarray(vec, dtype=np.float32).reshape(chunks, 128).T)


def _make_consts():
    qd, kd, dd, bd = _decay()
    ddT = dd.transpose(0, 2, 1).reshape(H, 2, 128, BS)
    ddT = np.ascontiguousarray(ddT.transpose(2, 0, 1, 3)).reshape(128, H * 2 * BS)
    qd_tab = np.empty((128, 16 * BS), dtype=np.float32)
    for c in range(16):
        for half in range(2):
            h = 2 * c + half
            qd_tab[half * 64:(half + 1) * 64, c * BS:(c + 1) * BS] = qd[h][None, :]
    kd_tab = np.empty((128, 2, DM), dtype=np.float32)
    for h in range(H):
        for par in range(2):
            kd_tab[:, par, h * 64:(h + 1) * 64] = kd[h, par * 128:(par + 1) * 128][:, None]
    bd_tab = np.empty((128, 16 * 64), dtype=np.float32)
    for ch in range(16):
        for half in range(2):
            bd_tab[half * 64:(half + 1) * 64, ch * 64:(ch + 1) * 64] = bd[2 * ch + half]
    return {
        "h128": _hadamard(128).astype(BF),
        "ddT_tab": np.ascontiguousarray(ddT),
        "qd_tab": qd_tab,
        "kd_tab": kd_tab,
        "bd_tab": np.ascontiguousarray(bd_tab),
    }



ENGS = ("sync", "gpsimd", "scalar", "vector", "tensor")


class Prog:

    def __init__(self, nc, stack, tag="", sem_stack=None, sem_handles=None,
                 sem_counts=None):
        self.nc = nc
        self.stack = stack
        self.sem_stack = sem_stack if sem_stack is not None else stack
        self.tag = tag
        self.items = {e: [] for e in ENGS}
        self.chain_prev = {}
        self.sem_handles = sem_handles if sem_handles is not None else {}
        self.sem_counts = sem_counts if sem_counts is not None else {}

    def sem(self, name):
        if name not in self.sem_handles:
            h = self.sem_stack.enter_context(self.nc.semaphore(name))
            self.sem_handles[name] = h
            self.sem_counts[name] = 0
        return self.sem_handles[name]

    def count(self, name):
        self.sem(name)
        return self.sem_counts[name]

    CHAIN_ENGS = ("vector", "scalar", "gpsimd")

    def op(self, eng, build, wait=(), inc=None, inc_by=1, dma=False):
        dma = dma or inc_by == 16
        chained = (eng in self.CHAIN_ENGS) and not dma
        ret_tok_none = inc is None
        if chained and inc is None:
            inc = "seq_" + eng
        wait_list = [(s, v) for (s, v) in wait if v > 0]
        if chained:
            prev = self.chain_prev.get(eng)
            if prev is not None:
                wait_list = wait_list + [prev]
        wait_vals = [(self.sem(s), v) for (s, v) in wait_list]
        tok = None
        inc_pair = None
        if inc is not None:
            self.sem(inc)
            self.sem_counts[inc] += inc_by
            tok = self.sem_counts[inc]
            inc_pair = (self.sem_handles[inc], inc_by)
        if chained:
            self.chain_prev[eng] = (inc, tok)

        def emit(engine):
            for (h, v) in wait_vals:
                engine.wait_ge(h, v)
            ins = build(engine)
            if inc_pair is not None:
                ins.then_inc(inc_pair[0], inc_pair[1])
        emit.meta = (wait_list, inc, inc_by if inc else 0)
        self.items[eng].append(emit)
        return None if ret_tok_none else tok

    def wait(self, eng, sem, val):
        if val <= 0:
            return
        h = self.sem(sem)
        fn = lambda engine: engine.wait_ge(h, val)
        fn.meta = ([(sem, val)], None, 0)
        self.items[eng].append(fn)

    @contextlib.contextmanager
    def scope(self):
        with contextlib.ExitStack() as st:
            outer = self.stack
            self.stack = st
            try:
                yield
            finally:
                self.stack = outer

    def sb(self, nm, shape, dt):
        return self.stack.enter_context(self.nc.sbuf_tensor(self.tag + nm, shape, dt))

    def psb(self, nm, shape, dt=F32):
        return self.stack.enter_context(self.nc.psum_tensor(self.tag + nm, shape, dt))

    def emit(self):
        with self.nc.Block() as block:
            for e in ENGS:
                items = self.items[e]
                if not items:
                    continue

                def runner(engine, items=items):
                    for fn in items:
                        fn(engine)
                getattr(block, e)(runner)


class Tracker:

    def __init__(self):
        self.t = {}

    def gate(self, key):
        return self.t.get(key, None)

    def set(self, key, sem, val):
        self.t[key] = (sem, val)


def _w(*pairs):
    return [pr for pr in pairs if pr is not None and pr[1] > 0]



def ln_sweep(p, src_dram, dst_sb, dst_dram, ntok, extra_scale_pp=None):
    nt = ntok // 512
    xz = [p.sb(f"ln_xz{i}", [128, 16 * 512], F32) for i in range(2)]
    xb = [p.sb(f"ln_xb{i}", [128, 16 * 512], BF16) for i in range(2)]
    sq = [p.sb(f"ln_sq{i}", [128, 16 * 512], BF16) for i in range(2)]
    ones_bf = p.sb("ln_ones_bf", [128, 1], BF16)
    ones32 = p.sb("ln_ones32", [1, 128], F32)
    st = [p.sb(f"ln_stat{i}", [1, 512], F32) for i in range(4)]
    out_tile = ([p.sb(f"ln_out{i}", [128, 16 * 512], BF16) for i in range(2)]
                if dst_sb is None else None)
    ps_sum = [p.psb(f"ln_ps_sum{i}", [1, 512]) for i in range(2)]
    ps_sq = [p.psb(f"ln_ps_sq{i}", [1, 512]) for i in range(2)]
    ps_b = [p.psb(f"ln_ps_b{i}", [128, 1024]) for i in range(2)]

    srcv = src_dram.rearrange("(c p) t -> p c t", p=128)
    dstv = dst_dram.rearrange("(c p) t -> p c t", p=128) if dst_dram is not None else None

    p.op("vector", lambda v: v.memset(ones_bf[:], 1.0))
    ini = p.op("vector", lambda v: v.memset(ones32[:], 1.0), inc="ln_init")

    apply_tok = {0: 0, 1: 0}
    mm_tok = {}
    state = {"last_apply": 0}

    def stage_a(tt):
        si = tt % 2
        t0 = tt * 512
        xb_r = xb[si][:].rearrange("p (c t) -> p c t", c=16)
        sq_r = sq[si][:].rearrange("p (c t) -> p c t", c=16)
        ld = p.op("gpsimd", lambda g, si=si, t0=t0: g.dma_start(
            out=xz[si][:], in_=srcv[:, :, t0:t0 + 512]),
            wait=_w(("ln_apply", apply_tok[si])), inc="ln_ld", inc_by=16)
        cp = p.op("vector", lambda v, si=si: v.tensor_copy(xb[si][:], xz[si][:]),
                  wait=_w(("ln_ld", ld), ("ln_init", ini)), inc="ln_v")
        sqk = p.op("scalar", lambda s, si=si: s.activation(sq[si][:], xb[si][:], AF.Square),
                   wait=_w(("ln_v", cp)), inc="ln_a")
        for c in range(16):
            p.op("tensor", lambda te, c=c, si=si, xb_r=xb_r: te.matmul(
                ps_sum[si][:, :], ones_bf[:], xb_r[:, c, :],
                start=(c == 0), stop=(c == 15)),
                wait=_w(("ln_a", sqk), ("ln_stat2", p.count("ln_stat2"))) if c == 0 else ())
        mm = None
        for c in range(16):
            mm = p.op("tensor", lambda te, c=c, si=si, sq_r=sq_r: te.matmul(
                ps_sq[si][:, :], ones_bf[:], sq_r[:, c, :],
                start=(c == 0), stop=(c == 15)),
                inc=("ln_mm" if c == 15 else None))
        mm_tok[tt] = mm

    def stage_b(tt):
        si = tt % 2
        t0 = tt * 512
        xz_r = xz[si][:].rearrange("p (c t) -> p c t", c=16)
        xb_r = xb[si][:].rearrange("p (c t) -> p c t", c=16)
        sq_r = sq[si][:].rearrange("p (c t) -> p c t", c=16)
        mm = p.count("ln_mm")
        p.op("vector", lambda v, si=si: v.tensor_scalar(
            st[0][:], ps_sum[si][0:1, :], 1.0 / DM, None, OP.mult),
            wait=_w(("ln_mm", mm)))
        p.op("vector", lambda v, si=si: v.tensor_scalar(
            st[1][:], ps_sum[si][0:1, :], -1.0 / DM, None, OP.mult))
        p.op("vector", lambda v: v.tensor_tensor(st[2][:], st[0][:], st[0][:], OP.mult))
        p.op("vector", lambda v, si=si: v.scalar_tensor_tensor(
            st[3][:], ps_sq[si][0:1, :], 1.0 / DM, st[2][:], OP.mult, OP.subtract))
        stat2 = p.op("vector", lambda v: v.tensor_scalar(
            st[2][:], st[3][:], EPS, None, OP.add), inc="ln_stat2")
        p.op("scalar", lambda s: s.activation(st[3][:], st[2][:], AF.Ln),
             wait=_w(("ln_stat2", stat2)))
        sqr = p.op("scalar", lambda s: s.activation(st[2][:], st[3][:], AF.Exp, scale=-0.5),
                   inc="ln_a")
        nmr = p.op("vector", lambda v: v.tensor_tensor(st[1][:], st[1][:], st[2][:], OP.mult),
                   wait=_w(("ln_a", sqr)), inc="ln_v")
        p.op("tensor", lambda te, si=si: te.matmul(
            ps_b[si][:, 0:512], ones32[:], st[2][:], start=True, stop=True),
            wait=_w(("ln_v", nmr), ("ln_apply", apply_tok[si])))
        mm2 = p.op("tensor", lambda te, si=si: te.matmul(
            ps_b[si][:, 512:1024], ones32[:], st[1][:], start=True, stop=True),
            inc="ln_mm")
        for c in range(16):
            dst_r = (dst_sb[:].rearrange("p (c t) -> p c t", c=16)[:, c, t0:t0 + 512]
                     if dst_sb is not None
                     else out_tile[si][:].rearrange("p (c t) -> p c t", c=16)[:, c, :])
            p.op("vector", lambda v, c=c, si=si, xb_r=xb_r, xz_r=xz_r: v.tensor_tensor(
                xb_r[:, c, :], xz_r[:, c, :], ps_b[si][:, 0:512], OP.mult),
                wait=_w(("ln_mm", mm2)) if c == 0 else ())
            if extra_scale_pp is None:
                tk = p.op("vector", lambda v, c=c, si=si, dst_r=dst_r, xb_r=xb_r: v.tensor_tensor(
                    dst_r, xb_r[:, c, :], ps_b[si][:, 512:1024], OP.add),
                    inc=("ln_apply" if c == 15 else None))
            else:
                p.op("vector", lambda v, c=c, si=si, xb_r=xb_r, sq_r=sq_r: v.tensor_tensor(
                    sq_r[:, c, :], xb_r[:, c, :], ps_b[si][:, 512:1024], OP.add))
                tk = p.op("vector", lambda v, c=c, dst_r=dst_r, sq_r=sq_r: v.tensor_scalar(
                    dst_r, sq_r[:, c, :], extra_scale_pp[:, c:c + 1], None, OP.mult),
                    inc=("ln_apply" if c == 15 else None))
            if tk is not None:
                apply_tok[si] = tk
                state["last_apply"] = tk
        if dst_dram is not None:
            src_t = (dst_sb[:].rearrange("p (c t) -> p c t", c=16)[:, :, t0:t0 + 512]
                     if dst_sb is not None
                     else out_tile[si][:].rearrange("p (c t) -> p c t", c=16))
            p.op("scalar", lambda s, t0=t0, src_t=src_t: s.dma_start(
                out=dstv[:, :, t0:t0 + 512], in_=src_t),
                wait=_w(("ln_apply", apply_tok[si])), inc="ln_st", inc_by=16)

    stage_a(0)
    for tt in range(nt):
        if tt + 1 < nt:
            stage_a(tt + 1)
        stage_b(tt)
    if dst_dram is not None:
        p.wait("scalar", "ln_st", p.count("ln_st"))
        return ("ln_st", p.count("ln_st"))
    return ("ln_apply", state["last_apply"])


def wt_stream(p, trk, wT_dram, oc0, nchunk, wT, prev_use_tok, ifc=16,
              eng="gpsimd"):
    wT_r = wT[:].rearrange("p (c o) -> p c o", c=ifc)
    src = wT_dram.rearrange("(c p) o -> p c o", p=128)[:, :, oc0 * 128:(oc0 + nchunk) * 128]
    tok = p.op(eng, lambda g: g.dma_start(out=wT_r[:, :, :], in_=src),
               wait=_w(prev_use_tok), inc="wld", inc_by=16)
    return ("wld", tok)



def build_p1(p, t):
    xn_sb = p.sb("xn_sb", [128, 16 * TOK], BF16)
    with p.scope():
        ln_done = ln_sweep(p, t["x0T"], xn_sb, t["xn"], TOK)
    xn_r = xn_sb[:].rearrange("p (c t) -> p c t", c=16)

    wT = [p.sb(f"wT{i}", [128, 16 * 512], BF16) for i in range(2)]
    outA = [p.sb(f"outA{i}", [128, TOK], BF16) for i in range(2)]
    outB = [p.sb(f"outB{i}", [128, 512], BF16) for i in range(4)]
    ps_A = [p.psb(f"ps_A{i}", [128, 512]) for i in range(4)]
    ps_B = [p.psb(f"ps_B{i}", [128, 512]) for i in range(4)]

    qkv = t["w_qkv"]
    qkfm = t["qk_fm"].rearrange("(c p) t -> p c t", p=128)
    kvtm = t["kv_tm"].rearrange("(c p) f -> p c f", p=128)

    trk = Tracker()
    for g in range(12):
        a_side = g < 8
        b_side = g >= 4
        wtr = wt_stream(p, trk, qkv, 4 * g, 4, wT[g % 2],
                        trk.gate(("wT", g % 2)) or (ln_done if g < 2 else None))
        wb = wT[g % 2]
        if a_side:
            for j in range(4):
                oc = 4 * g + j
                pa = ps_A[j]
                oa = outA[j % 2]
                for tt in range(4):
                    for c in range(16):
                        w0 = ()
                        if c == 0:
                            w0 = _w(wtr if (tt == 0 and j == 0) else None,
                                    ln_done if (g == 0 and tt == 0) else None,
                                    trk.gate(("psA", j)))
                        p.op("tensor", lambda te, j=j, c=c, tt=tt, pa=pa, wb=wb: te.matmul(
                            pa[:, :], wb[:, c * 512 + j * 128:c * 512 + (j + 1) * 128],
                            xn_r[:, c, tt * 512:(tt + 1) * 512],
                            start=(c == 0), stop=(c == 15)),
                            wait=w0, inc=("mmA" if c == 15 else None))
                    last_mm = ("mmA", p.count("mmA"))
                    ev = p.op("scalar", lambda s, pa=pa, oa=oa, tt=tt: s.activation(
                        oa[:, tt * 512:(tt + 1) * 512], pa[:, :], AF.Silu),
                        wait=_w(last_mm, trk.gate(("outA", j % 2)) if tt == 0 else None),
                        inc="evA")
                    trk.set(("psA", j), "evA", ev)
                st = p.op("scalar", lambda s, oc=oc, oa=oa: s.dma_start(
                    out=qkfm[:, oc, :], in_=oa[:, :]),
                    wait=_w(("evA", p.count("evA"))), inc="stA", inc_by=16)
                trk.set(("outA", j % 2), "stA", st)
        if b_side:
            of0 = 4 * g * 128 - 2048
            for tc in range(16):
                pb = ps_B[tc % 4]
                ob = outB[tc % 4]
                for c in range(16):
                    w0 = ()
                    if c == 0:
                        w0 = _w(wtr if tc == 0 else None,
                                trk.gate(("psB", tc % 4)))
                    p.op("tensor", lambda te, c=c, tc=tc, pb=pb, wb=wb: te.matmul(
                        pb[:, :], xn_r[:, c, tc * 128:(tc + 1) * 128],
                        wb[:, c * 512:(c + 1) * 512],
                        start=(c == 0), stop=(c == 15)),
                        wait=w0, inc=("mmB" if c == 15 else None))
                last_mm = ("mmB", p.count("mmB"))
                ev = p.op("scalar", lambda s, pb=pb, ob=ob: s.activation(
                    ob[:, :], pb[:, :], AF.Silu),
                    wait=_w(last_mm, trk.gate(("outB", tc % 4))), inc="evB")
                trk.set(("psB", tc % 4), "evB", ev)
                st = p.op("scalar", lambda s, tc=tc, of0=of0, ob=ob: s.dma_start(
                    out=kvtm[:, tc, of0:of0 + 512], in_=ob[:, :]),
                    wait=_w(("evB", ev)), inc="stB", inc_by=16)
                trk.set(("outB", tc % 4), "stB", st)
        mm_gate = ("mmA", p.count("mmA")) if a_side else ("mmB", p.count("mmB"))
        if a_side and b_side:
            mm_gate = ("mmB", p.count("mmB"))
        trk.set(("wT", g % 2), *mm_gate)

    p.wait("sync", "stA", p.count("stA"))
    p.wait("sync", "stB", p.count("stB"))




def build_p2(p, t):
    ddT = p.sb("ddT", [128, H * 2 * BS], F32)
    qdt = p.sb("qdt", [128, 16 * BS], F32)
    kdt = p.sb("kdt", [128, 2 * DM], F32)
    bdt = p.sb("bdt", [128, 1024], F32)
    q_blk = p.sb("q_blk", [128, 16 * BS], BF16)
    k_blk = p.sb("k_blk", [128, 16 * BS], BF16)
    qqd = p.sb("qqd", [128, 16 * BS], BF16)
    kv_blk = p.sb("kv_blk", [128, 2 * 4096], BF16)
    kkd = p.sb("kkd", [128, 2 * DM], BF16)
    sT = [p.sb(f"sT{i}", [128, 512], BF16) for i in range(3)]
    attn_sb = p.sb("attn_sb", [128, 16 * BS], BF16)
    S_f32 = p.sb("S_f32", [128, 1024], F32)
    S_tmp = p.sb("S_tmp", [128, 1024], F32)
    S_bf = p.sb("S_bf", [128, 1024], BF16)
    ps_aw = [p.psb(f"ps_aw{i}", [128, 512]) for i in range(3)]
    ps_att = [p.psb(f"ps_att{i}", [128, 512]) for i in range(2)]
    ps_S = p.psb("ps_S", [128, 1024])

    qkv_v = t["qk_fm"].rearrange("(c p) t -> p c t", p=128)
    kv_v = t["kv_tm"].rearrange("(c p) f -> p c f", p=128)
    attn_v = t["attn"].rearrange("(c p) t -> p c t", p=128)

    q_r = q_blk[:].rearrange("p (c t) -> p c t", c=16)
    k_r = k_blk[:].rearrange("p (c t) -> p c t", c=16)
    qqd_r = qqd[:].rearrange("p (c t) -> p c t", c=16)
    kv_r = kv_blk[:].rearrange("p (kc f) -> p kc f", kc=2)
    kkd_r = kkd[:].rearrange("p (kc f) -> p kc f", kc=2)
    kdt_r = kdt[:].rearrange("p (kc f) -> p kc f", kc=2)
    at_r = attn_sb[:].rearrange("p (c t) -> p c t", c=16)

    tab = None
    for (dst, src) in ((ddT, t["ddT_tab"]), (qdt, t["qd_tab"]),
                       (kdt, t["kd_tab"].rearrange("p a f -> p (a f)")), (bdt, t["bd_tab"])):
        tab = p.op("sync", lambda s, dst=dst, src=src: s.dma_start(out=dst[:], in_=src),
                   inc="p2_tab", inc_by=16)
    p.op("vector", lambda v: v.memset(S_f32[:], 0.0))
    s_bf_tok = p.op("vector", lambda v: v.memset(S_bf[:], 0.0), inc="p2_sbf")

    blk_done = 0
    sT_pe_tok = {0: 0, 1: 0, 2: 0}
    ev_tok = 0
    st_tok = 0
    for tb in range(NB):
        t0 = tb * BS
        p.op("gpsimd", lambda g, t0=t0: g.dma_start(
            out=q_blk[:], in_=qkv_v[:, 0:16, t0:t0 + BS]),
            wait=_w(("p2_blk", blk_done)), inc="p2_ld", inc_by=16)
        p.op("gpsimd", lambda g, t0=t0: g.dma_start(
            out=k_blk[:], in_=qkv_v[:, 16:32, t0:t0 + BS]), inc="p2_ld", inc_by=16)
        ld = p.op("gpsimd", lambda g, tb=tb: g.dma_start(
            out=kv_blk[:].rearrange("p (kc f) -> p kc f", kc=2),
            in_=kv_v[:, 2 * tb:2 * tb + 2, :]), inc="p2_ld", inc_by=16)
        qq = p.op("vector", lambda v: v.tensor_tensor(qqd[:], q_blk[:], qdt[:], OP.mult),
                  wait=_w(("p2_ld", ld), ("p2_tab", tab)), inc="p2_v")
        kk = None
        for kc in range(2):
            kk = p.op("vector", lambda v, kc=kc: v.tensor_tensor(
                kkd_r[:, kc, 0:DM], kv_r[:, kc, 0:DM], kdt_r[:, kc, :], OP.mult),
                inc="p2_v")
        sT_dve = {0: 0, 1: 0, 2: 0}
        att_ev = {0: ev_tok, 1: ev_tok}
        for h in range(H):
            ch, hb = h // 2, (h % 2) * 64
            bi = h % 3
            stb = sT[bi]
            paw = ps_aw[bi]
            pat = ps_att[(h // 2) % 2]
            for kc in range(2):
                w0 = ()
                if kc == 0:
                    w0 = _w(("p2_ld", ld) if h == 0 else None,
                            ("p2_v", sT_dve[bi]))
                p.op("tensor", lambda te, ch=ch, hb=hb, kc=kc, paw=paw: te.matmul(
                    paw[:, kc * BS:(kc + 1) * BS],
                    k_r[hb:hb + 64, ch, kc * 128:(kc + 1) * 128],
                    q_r[hb:hb + 64, ch, :],
                    start=(kc == 0), stop=(kc == 1)),
                    wait=w0, inc=("p2_mm" if kc == 1 else None))
            aw_tok = p.count("p2_mm")
            sT_dve[bi] = p.op("vector", lambda v, h=h, stb=stb, paw=paw: v.tensor_tensor(
                stb[:], paw[:], ddT[:, h * 512:(h + 1) * 512], OP.mult),
                wait=_w(("p2_mm", aw_tok), ("p2_mm2", sT_pe_tok[bi])), inc="p2_v")
            for kc in range(2):
                w0 = ()
                if kc == 0:
                    w0 = _w(("p2_v", sT_dve[bi]),
                            ("p2_ev", att_ev[(h // 2) % 2]) if h % 2 == 0 else None,
                            ("p2_sbf", s_bf_tok))
                p.op("tensor", lambda te, h=h, hb=hb, kc=kc, stb=stb, pat=pat: te.matmul(
                    pat[hb:hb + 64, 0:BS],
                    kv_r[:, kc, 2048 + h * 64:2048 + (h + 1) * 64],
                    stb[:, kc * BS:(kc + 1) * BS],
                    start=(kc == 0), stop=False),
                    wait=w0)
            sT_pe_tok[bi] = p.op("tensor", lambda te, h=h, hb=hb, ch=ch, pat=pat: te.matmul(
                pat[hb:hb + 64, 0:BS],
                S_bf[hb:hb + 64, ch * 64:(ch + 1) * 64],
                qqd_r[hb:hb + 64, ch, :],
                start=False, stop=True),
                wait=_w(("p2_v", qq)) if h == 0 else (), inc="p2_mm2")
            if h % 2 == 1:
                att_ev[(h // 2) % 2] = ev_tok = p.op("vector", lambda v, ch=ch, pat=pat: v.tensor_copy(
                    at_r[:, ch, :], pat[:, 0:BS]),
                    wait=_w(("p2_mm2", sT_pe_tok[bi]), ("p2_st", st_tok)), inc="p2_ev")
        smm = None
        for h in range(H):
            for kc in range(2):
                w0 = ()
                if h == 0 and kc == 0:
                    w0 = _w(("p2_v", kk), ("p2_supd", p.count("p2_supd")))
                hb2, ch2 = (h % 2) * 64, h // 2
                smm = p.op("tensor", lambda te, h=h, kc=kc, hb2=hb2, ch2=ch2: te.matmul(
                    ps_S[hb2:hb2 + 64, ch2 * 64:(ch2 + 1) * 64],
                    kkd_r[:, kc, h * 64:(h + 1) * 64],
                    kv_r[:, kc, 2048 + h * 64:2048 + (h + 1) * 64],
                    start=(kc == 0), stop=(kc == 1)),
                    wait=w0, inc=("p2_mm" if (h == H - 1 and kc == 1) else None))
        p.op("vector", lambda v: v.tensor_tensor(S_tmp[:], S_f32[:], bdt[:], OP.mult),
             wait=_w(("p2_mm", smm)))
        up = p.op("vector", lambda v: v.tensor_tensor(S_f32[:], S_tmp[:], ps_S[:, :], OP.add),
                  inc="p2_supd")
        s_bf_tok = p.op("vector", lambda v: v.tensor_copy(S_bf[:], S_f32[:]),
                        inc="p2_sbf")
        st_tok = p.op("scalar", lambda s, t0=t0: s.dma_start(
            out=attn_v[:, :, t0:t0 + BS], in_=at_r[:, :, :]),
            wait=_w(("p2_ev", ev_tok)), inc="p2_st", inc_by=16)
        blk_done = p.op("vector", lambda v: v.engine_nop(),
                        wait=_w(("p2_mm", p.count("p2_mm")),
                                ("p2_mm2", p.count("p2_mm2")),
                                ("p2_sbf", s_bf_tok),
                                ("p2_st", st_tok)),
                        inc="p2_blk")
    p.wait("sync", "p2_st", p.count("p2_st"))




def build_p3(p, t):
    at = p.sb("at", [128, 16 * 512], BF16)
    sq = p.sb("sq", [128, 16 * 512], BF16)
    ones_bf = p.sb("ones_bf", [128, 1], BF16)
    ones32 = p.sb("ones32", [1, 128], F32)
    st0 = p.sb("st0", [1, 512], F32)
    st1 = p.sb("st1", [1, 512], F32)
    nw = p.sb("nw", [128, 16], F32)
    xn_sb = p.sb("xn_sb", [128, 16 * TOK], BF16)
    G = p.sb("G", [128, 16 * TOK], BF16)
    gt = p.sb("gt", [128, 512], BF16)
    wT = [p.sb(f"wT{i}", [128, 16 * 512], BF16) for i in range(2)]
    x0t = [p.sb(f"x0t{i}", [128, 512], F32) for i in range(2)]
    x1t = [p.sb(f"x1t{i}", [128, 512], F32) for i in range(2)]
    ps_sq = p.psb("ps_sq", [1, 512])
    ps_b = p.psb("ps_b", [128, 512])
    ps = [p.psb(f"ps{i}", [128, 512]) for i in range(4)]

    at_r = at[:].rearrange("p (c t) -> p c t", c=16)
    sq_r = sq[:].rearrange("p (c t) -> p c t", c=16)
    xn_r = xn_sb[:].rearrange("p (c t) -> p c t", c=16)
    G_r = G[:].rearrange("p (c t) -> p c t", c=16)
    attn_v = t["attn"].rearrange("(c p) t -> p c t", p=128)
    xn_v = t["xn"].rearrange("(c p) t -> p c t", p=128)
    x0_v = t["x0T"].rearrange("(c p) t -> p c t", p=128)
    x1_v = t["x1"].rearrange("(c p) t -> p c t", p=128)

    p.op("vector", lambda v: v.memset(ones_bf[:], 1.0))
    ini = p.op("vector", lambda v: v.memset(ones32[:], 1.0), inc="p3_init")
    nwt = p.op("sync", lambda s: s.dma_start(out=nw[:], in_=t["nw_pp"]), inc="p3_tab", inc_by=16)
    xnld = p.op("sync", lambda s: s.dma_start(out=xn_sb[:].rearrange("p (c t) -> p c t", c=16),
                                              in_=xn_v[:, :, :]), inc="p3_tab", inc_by=16)

    app_tok = 0
    for tt in range(4):
        t0 = tt * 512
        ld = p.op("gpsimd", lambda g, t0=t0: g.dma_start(
            out=at[:], in_=attn_v[:, :, t0:t0 + 512]),
            wait=_w(("p3_app", app_tok)), inc="p3_ld", inc_by=16)
        sqk = p.op("scalar", lambda s: s.activation(sq[:], at[:], AF.Square),
                   wait=_w(("p3_ld", ld)), inc="p3_a")
        mm = None
        for c in range(16):
            mm = p.op("tensor", lambda te, c=c: te.matmul(
                ps_sq[:, :], ones_bf[:], sq_r[:, c, :],
                start=(c == 0), stop=(c == 15)),
                wait=_w(("p3_a", sqk), ("p3_st2", p.count("p3_st2")), ("p3_init", ini)) if c == 0 else (),
                inc=("p3_mm" if c == 15 else None))
        v1 = p.op("vector", lambda v: v.tensor_scalar(
            st0[:], ps_sq[0:1, :], 1.0 / DM, EPS, OP.mult, OP.add),
            wait=_w(("p3_mm", mm)), inc="p3_v")
        a1 = p.op("scalar", lambda s: s.activation(st1[:], st0[:], AF.Sqrt),
                  wait=_w(("p3_v", v1)), inc="p3_a")
        st2 = p.op("vector", lambda v: v.reciprocal(st0[:], st1[:]),
                   wait=_w(("p3_a", a1)), inc="p3_st2")
        bmm = p.op("tensor", lambda te: te.matmul(ps_b[:, :], ones32[:], st0[:],
                                                  start=True, stop=True),
                   wait=_w(("p3_st2", st2), ("p3_app", app_tok)), inc="p3_mm")
        for c in range(16):
            tk = p.op("vector", lambda v, c=c, t0=t0: v.scalar_tensor_tensor(
                G_r[:, c, t0:t0 + 512], at_r[:, c, :], nw[:, c:c + 1], ps_b[:, :],
                OP.mult, OP.mult),
                wait=_w(("p3_mm", bmm), ("p3_tab", nwt)) if c == 0 else (),
                inc=("p3_app" if c == 15 else None))
            if tk is not None:
                app_tok = tk

    trk = Tracker()
    gdone = 0
    for g in range(4):
        wtr = wt_stream(p, trk, t["w_gate"], 4 * g, 4, wT[g % 2],
                        trk.gate(("wT", g % 2)))
        wb = wT[g % 2]
        for j in range(4):
            oc = 4 * g + j
            pa = ps[j]
            for tt in range(4):
                for c in range(16):
                    w0 = ()
                    if c == 0:
                        w0 = _w(wtr if (tt == 0 and j == 0) else None,
                                ("p3_tab", xnld) if (g == 0 and j == 0 and tt == 0) else None,
                                trk.gate(("ps", j)))
                    p.op("tensor", lambda te, j=j, c=c, tt=tt, pa=pa, wb=wb: te.matmul(
                        pa[:, :], wb[:, c * 512 + j * 128:c * 512 + (j + 1) * 128],
                        xn_r[:, c, tt * 512:(tt + 1) * 512],
                        start=(c == 0), stop=(c == 15)),
                        wait=w0, inc=("p3_mmg" if c == 15 else None))
                ev = p.op("scalar", lambda s, pa=pa: s.activation(gt[:], pa[:, :], AF.Sigmoid),
                          wait=_w(("p3_mmg", p.count("p3_mmg")), ("p3_gg", p.count("p3_gg")),
                                  ("p3_app", app_tok) if (g == 0 and j == 0 and tt == 0) else None),
                          inc="p3_ev")
                gdone = p.op("vector", lambda v, oc=oc, tt=tt: v.tensor_tensor(
                    G_r[:, oc, tt * 512:(tt + 1) * 512], gt[:],
                    G_r[:, oc, tt * 512:(tt + 1) * 512], OP.mult),
                    wait=_w(("p3_ev", ev)), inc="p3_gg")
                trk.set(("ps", j), "p3_gg", gdone)
        trk.set(("wT", g % 2), "p3_mmg", p.count("p3_mmg"))

    gg_all = ("p3_gg", p.count("p3_gg"))
    for g in range(4):
        wtr = wt_stream(p, trk, t["w_out0"], 4 * g, 4, wT[g % 2],
                        trk.gate(("wT", g % 2)))
        wb = wT[g % 2]
        for j in range(4):
            oc = 4 * g + j
            pa = ps[j]
            for tt in range(4):
                xl = p.op("gpsimd", lambda gp, oc=oc, tt=tt, j=j: gp.dma_start(
                    out=x0t[tt % 2][:], in_=x0_v[:, oc, tt * 512:(tt + 1) * 512]),
                    wait=_w(trk.gate(("x0t", tt % 2))), inc="p3_xld", inc_by=16)
                for c in range(16):
                    w0 = ()
                    if c == 0:
                        w0 = _w(wtr if (tt == 0 and j == 0) else None,
                                gg_all if (g == 0 and j == 0 and tt == 0) else None,
                                trk.gate(("ps", j)))
                    p.op("tensor", lambda te, j=j, c=c, tt=tt, pa=pa, wb=wb: te.matmul(
                        pa[:, :], wb[:, c * 512 + j * 128:c * 512 + (j + 1) * 128],
                        G_r[:, c, tt * 512:(tt + 1) * 512],
                        start=(c == 0), stop=(c == 15)),
                        wait=w0, inc=("p3_mmo" if c == 15 else None))
                ev = p.op("vector", lambda v, pa=pa, tt=tt: v.tensor_tensor(
                    x1t[tt % 2][:], pa[:, :], x0t[tt % 2][:], OP.add),
                    wait=_w(("p3_mmo", p.count("p3_mmo")), ("p3_xld", xl),
                            trk.gate(("x1t", tt % 2))), inc="p3_evo")
                trk.set(("ps", j), "p3_evo", ev)
                stt = p.op("scalar", lambda s, oc=oc, tt=tt: s.dma_start(
                    out=x1_v[:, oc, tt * 512:(tt + 1) * 512], in_=x1t[tt % 2][:]),
                    wait=_w(("p3_evo", ev)), inc="p3_sto", inc_by=16)
                trk.set(("x1t", tt % 2), "p3_sto", stt)
                trk.set(("x0t", tt % 2), "p3_evo", ev)
        trk.set(("wT", g % 2), "p3_mmo", p.count("p3_mmo"))

    p.wait("sync", "p3_sto", p.count("p3_sto"))




def _bfly(p, ping, pong, nch, span, first_wait, sems=("p4_cp", "p4_g")):
    import math
    nst = int(math.log2(nch))
    v_tok, g_tok = first_wait
    bufs = [ping, pong]
    for s in range(nst):
        dist = nch >> (s + 1)
        cur = bufs[s % 2][:].rearrange("p (g two d) -> p g two d", two=2, d=dist * span)
        nxt = bufs[(s + 1) % 2][:].rearrange("p (g two d) -> p g two d", two=2, d=dist * span)
        a, b = cur[:, :, 0, :], cur[:, :, 1, :]
        w = _w((sems[0], v_tok), (sems[1], g_tok))
        v_tok = p.op("vector", lambda v, a=a, b=b, nxt=nxt: v.tensor_tensor(
            nxt[:, :, 0, :], a, b, OP.add), wait=w, inc=sems[0])
        v_tok = p.op("vector", lambda v, a=a, b=b, nxt=nxt: v.tensor_tensor(
            nxt[:, :, 1, :], a, b, OP.subtract), inc=sems[0])
    return nst % 2, v_tok, g_tok


def build_p4a(p, t):
    xsu = p.sb("xsu", [128, 16 * TOK], BF16)
    su = p.sb("su", [128, 16], F32)
    sut = p.op("sync", lambda s: s.dma_start(out=su[:], in_=t["su_ug_pp"]), inc="p4_tab", inc_by=16)
    p.wait("vector", "p4_tab", sut)
    with p.scope():
        ln_done = ln_sweep(p, t["x1"], xsu, None, TOK, extra_scale_pp=su)
    h128 = p.sb("h128s", [128, 128], BF16)
    h1t = p.op("sync", lambda s: s.dma_start(out=h128[:], in_=t["h128"]), wait=_w(ln_done),
               inc="p4_tab", inc_by=16)
    ping = p.sb("ping", [128, 16 * 512], BF16)
    pong = p.sb("pong", [128, 16 * 512], BF16)
    ps = [p.psb(f"ps{i}", [128, 512]) for i in range(4)]
    xsu_r = xsu[:].rearrange("p (c t) -> p c t", c=16)
    ping_r = ping[:].rearrange("p (c t) -> p c t", c=16)
    pong_r = pong[:].rearrange("p (c t) -> p c t", c=16)
    x1h_v = t["x1h"].rearrange("(c p) t -> p c t", p=128)

    st_tok = 0
    for tt in range(4):
        t0 = tt * 512
        cp = None
        for c in range(16):
            w0 = ()
            if c == 0:
                w0 = _w(ln_done, ("p4_tab", h1t), ("p4_st", st_tok))
            else:
                w0 = _w(("p4_cp", max(0, p.count("p4_cp") - 3)),)
            p.op("tensor", lambda te, c=c, t0=t0: te.matmul(
                ps[c % 4][:, :], h128[:], xsu_r[:, c, t0:t0 + 512],
                start=True, stop=True), wait=w0, inc="p4_mm")
            cp = p.op("vector", lambda v, c=c: v.tensor_copy(ping_r[:, c, :], ps[c % 4][:, :]),
                      wait=_w(("p4_mm", p.count("p4_mm"))), inc="p4_cp")
        fin, vt, gt = _bfly(p, ping, pong, 16, 512, (cp, p.count("p4_g")))
        assert fin == 0
        st_tok = p.op("scalar", lambda s, t0=t0: s.dma_start(
            out=x1h_v[:, :, t0:t0 + 512], in_=ping_r[:, :, :]),
            wait=_w(("p4_cp", vt), ("p4_g", gt)), inc="p4_st", inc_by=16)
    p.wait("sync", "p4_st", p.count("p4_st"))



def build_p4b(p, t):
    x1h_sb = p.sb("x1h_sb", [128, 16 * TOK], BF16)
    wsu = p.sb("wsu", [128, 64], F32)
    wsg = p.sb("wsg", [128, 64], F32)
    sud = p.sb("sud", [128, 64], F32)
    wT_u = [p.sb(f"wT_u{i}", [128, 16 * 512], BF16) for i in range(2)]
    wT_g = [p.sb(f"wT_g{i}", [128, 16 * 512], BF16) for i in range(2)]
    gt = p.sb("gt", [128, 512], BF16)
    ut = p.sb("ut", [128, 512], BF16)
    ht = [p.sb(f"ht{i}", [128, 512], BF16) for i in range(2)]
    ps_u = [p.psb(f"ps_u{i}", [128, 512]) for i in range(4)]
    ps_g = [p.psb(f"ps_g{i}", [128, 512]) for i in range(4)]

    x1h_v = t["x1h"].rearrange("(c p) t -> p c t", p=128)
    h_v = t["hbuf"].rearrange("(c p) t -> p c t", p=128)
    x1h_r = x1h_sb[:].rearrange("p (c t) -> p c t", c=16)

    for nm, dst in (("wsu_pp", wsu), ("wsg_pp", wsg), ("su_dp_pp", sud)):
        p.op("sync", lambda s, nm=nm, dst=dst: s.dma_start(out=dst[:], in_=t[nm]),
             inc="p4_tab", inc_by=16)
    ldx = p.op("sync", lambda s: s.dma_start(
        out=x1h_sb[:].rearrange("p (c t) -> p c t", c=16), in_=x1h_v[:, :, :]),
        inc="p4_tab", inc_by=16)
    tab4b = ldx
    rs = float(1.0 / np.sqrt(2048.0))
    p.op("vector", lambda v: v.tensor_tensor(wsu[:], wsu[:], sud[:], OP.mult),
         wait=_w(("p4_tab", tab4b)))
    p.op("vector", lambda v: v.tensor_scalar(wsu[:], wsu[:], rs, None, OP.mult))
    pr = p.op("vector", lambda v: v.tensor_scalar(wsg[:], wsg[:], rs, None, OP.mult),
              inc="p4_pr")

    trk = Tracker()
    for g in range(16):
        wtr_u = wt_stream(p, trk, t["w_up"], 4 * g, 4, wT_u[g % 2],
                          trk.gate(("wTu", g % 2)))
        wtr_g = wt_stream(p, trk, t["w_wgate"], 4 * g, 4, wT_g[g % 2],
                          trk.gate(("wTg", g % 2)))
        wu, wg = wT_u[g % 2], wT_g[g % 2]
        for j in range(4):
            oc = 4 * g + j
            pu, pg = ps_u[j], ps_g[j]
            for tt in range(4):
                for c in range(16):
                    w0 = ()
                    if c == 0:
                        w0 = _w(wtr_g if (tt == 0 and j == 0) else None,
                                ("p4_pr", pr) if (g == 0 and j == 0 and tt == 0) else None,
                                trk.gate(("ps", j)))
                    p.op("tensor", lambda te, j=j, c=c, tt=tt, pu=pu, wu=wu: te.matmul(
                        pu[:, :], wu[:, c * 512 + j * 128:c * 512 + (j + 1) * 128],
                        x1h_r[:, c, tt * 512:(tt + 1) * 512],
                        start=(c == 0), stop=(c == 15)), wait=w0)
                for c in range(16):
                    p.op("tensor", lambda te, j=j, c=c, tt=tt, pg=pg, wg=wg: te.matmul(
                        pg[:, :], wg[:, c * 512 + j * 128:c * 512 + (j + 1) * 128],
                        x1h_r[:, c, tt * 512:(tt + 1) * 512],
                        start=(c == 0), stop=(c == 15)),
                        inc=("p4_mm" if c == 15 else None))
                mmt = p.count("p4_mm")
                e1 = p.op("scalar", lambda s, pg=pg, oc=oc: s.activation(
                    gt[:], pg[:, :], AF.Silu, scale=wsg[:, oc:oc + 1]),
                    wait=_w(("p4_mm", mmt), ("p4_h", p.count("p4_h"))), inc="p4_ev")
                e2 = p.op("vector", lambda v, pu=pu, oc=oc: v.tensor_scalar(
                    ut[:], pu[:, :], wsu[:, oc:oc + 1], None, OP.mult),
                    wait=_w(("p4_mm", mmt), ("p4_h", p.count("p4_h"))), inc="p4_ev")
                hv = p.op("vector", lambda v, tt=tt: v.tensor_tensor(
                    ht[tt % 2][:], gt[:], ut[:], OP.mult),
                    wait=_w(("p4_ev", p.count("p4_ev")), trk.gate(("ht", tt % 2))),
                    inc="p4_h")
                trk.set(("ps", j), "p4_h", hv)
                stt = p.op("scalar", lambda s, oc=oc, tt=tt: s.dma_start(
                    out=h_v[:, oc, tt * 512:(tt + 1) * 512], in_=ht[tt % 2][:]),
                    wait=_w(("p4_h", hv)), inc="p4_sth", inc_by=16)
                trk.set(("ht", tt % 2), "p4_sth", stt)
        mm_gate = ("p4_mm", p.count("p4_mm"))
        trk.set(("wTu", g % 2), *mm_gate)
        trk.set(("wTg", g % 2), *mm_gate)
    p.wait("sync", "p4_sth", p.count("p4_sth"))



def build_p4c(p, t):
    h128 = p.sb("h128s", [128, 128], BF16)
    ping = p.sb("ping", [128, 64 * 512], BF16)
    pong = p.sb("pong", [128, 64 * 512], BF16)
    ps = [p.psb(f"ps{i}", [128, 512]) for i in range(4)]
    h1t = p.op("sync", lambda s: s.dma_start(out=h128[:], in_=t["h128"]), inc="p4_tab", inc_by=16)
    h_v = t["hbuf"].rearrange("(c p) t -> p c t", p=128)
    hp_v = t["hpbuf"].rearrange("(c p) t -> p c t", p=128)
    ping_r = ping[:].rearrange("p (c t) -> p c t", c=64)
    pong_r = pong[:].rearrange("p (c t) -> p c t", c=64)

    st_tok = 0
    for tt in range(4):
        t0 = tt * 512
        ld = p.op("gpsimd", lambda g, t0=t0: g.dma_start(
            out=ping_r[:, :, :], in_=h_v[:, :, t0:t0 + 512]),
            wait=_w(("p4_st", st_tok)), inc="p4_ld", inc_by=16)
        cp = None
        for c in range(64):
            w0 = _w(("p4_ld", ld), ("p4_tab", h1t)) if c == 0 else _w(("p4_cp", max(0, p.count("p4_cp") - 3)),)
            p.op("tensor", lambda te, c=c: te.matmul(
                ps[c % 4][:, :], h128[:], ping_r[:, c, :],
                start=True, stop=True), wait=w0, inc="p4_mm")
            cp = p.op("vector", lambda v, c=c: v.tensor_copy(ping_r[:, c, :], ps[c % 4][:, :]),
                      wait=_w(("p4_mm", p.count("p4_mm"))), inc="p4_cp")
        fin, vt, gt = _bfly(p, ping, pong, 64, 512, (cp, p.count("p4_g")))
        assert fin == 0
        st_tok = p.op("scalar", lambda s, t0=t0: s.dma_start(
            out=hp_v[:, :, t0:t0 + 512], in_=ping_r[:, :, :]),
            wait=_w(("p4_cp", vt), ("p4_g", gt)), inc="p4_st", inc_by=16)
    p.wait("sync", "p4_st", p.count("p4_st"))



def build_p4d(p, t):
    wsd = p.sb("wsd", [128, 16], F32)
    hp_sb = p.sb("hp_sb", [128, 64 * 1024], BF16)
    wT = [p.sb(f"wT{i}", [128, 64 * 128], BF16) for i in range(2)]
    x1t = [p.sb(f"x1t{i}", [128, 512], F32) for i in range(2)]
    x2t = [p.sb(f"x2t{i}", [128, 512], F32) for i in range(2)]
    ps = [p.psb(f"ps{i}", [128, 512]) for i in range(4)]

    hp_v = t["hpbuf"].rearrange("(c p) t -> p c t", p=128)
    x1_v = t["x1"].rearrange("(c p) t -> p c t", p=128)
    x2_v = t["x2"].rearrange("(c p) t -> p c t", p=128)
    hp_r = hp_sb[:].rearrange("p (c t) -> p c t", c=64)

    wsdt = p.op("sync", lambda s: s.dma_start(out=wsd[:], in_=t["wsd_pp"]), inc="p4_tab", inc_by=16)
    rs = float(1.0 / np.sqrt(8192.0))
    pr = p.op("vector", lambda v: v.tensor_scalar(wsd[:], wsd[:], rs, None, OP.mult),
              wait=_w(("p4_tab", wsdt)), inc="p4_pr")

    trk = Tracker()
    for sup in range(2):
        s0 = sup * 1024
        ldh = p.op("gpsimd", lambda g, s0=s0: g.dma_start(
            out=hp_r[:, :, :], in_=hp_v[:, :, s0:s0 + 1024]),
            wait=_w(trk.gate("hp")), inc="p4_ld", inc_by=16)
        for oc in range(16):
            wtr = wt_stream(p, trk, t["w_down"], oc, 1, wT[oc % 2],
                            trk.gate(("wT", oc % 2)), ifc=64)
            wb = wT[oc % 2]
            for sp in range(2):
                pi = (oc * 2 + sp) % 4
                for c in range(64):
                    w0 = ()
                    if c == 0:
                        w0 = _w(wtr if sp == 0 else None, ("p4_ld", ldh) if sp == 0 else None,
                                ("p4_pr", pr), trk.gate(("ps", pi)))
                    p.op("tensor", lambda te, c=c, sp=sp, pa=ps[pi], wb=wb: te.matmul(
                        pa[:, :], wb[:, c * 128:(c + 1) * 128],
                        hp_r[:, c, sp * 512:(sp + 1) * 512],
                        start=(c == 0), stop=(c == 63)),
                        wait=w0, inc=("p4_mm" if c == 63 else None))
                xl = p.op("gpsimd", lambda g, oc=oc, sp=sp, s0=s0: g.dma_start(
                    out=x1t[sp][:], in_=x1_v[:, oc, s0 + sp * 512:s0 + (sp + 1) * 512]),
                    wait=_w(trk.gate(("x1t", sp))), inc="p4_xld", inc_by=16)
                ev = p.op("vector", lambda v, oc=oc, sp=sp, pi=pi: v.scalar_tensor_tensor(
                    x2t[sp][:], ps[pi][:, :], wsd[:, oc:oc + 1], x1t[sp][:],
                    OP.mult, OP.add),
                    wait=_w(("p4_mm", p.count("p4_mm")), ("p4_xld", xl),
                            trk.gate(("x2t", sp))), inc="p4_ev")
                trk.set(("ps", pi), "p4_ev", ev)
                trk.set(("x1t", sp), "p4_ev", ev)
                stt = p.op("scalar", lambda s, oc=oc, sp=sp, s0=s0: s.dma_start(
                    out=x2_v[:, oc, s0 + sp * 512:s0 + (sp + 1) * 512], in_=x2t[sp][:]),
                    wait=_w(("p4_ev", ev)), inc="p4_stx", inc_by=16)
                trk.set(("x2t", sp), "p4_stx", stt)
            trk.set(("wT", oc % 2), "p4_mm", p.count("p4_mm"))
        trk.set("hp", "p4_mm", p.count("p4_mm"))
    p.wait("sync", "p4_stx", p.count("p4_stx"))




def plan_par(p, engs):
    store = {}
    for eng in engs:
        def f(e, eng=eng):
            pid = e.partition_id()
            store[eng] = e.snap(pid % 2, min_val=0, max_val=1)
            return e.nop()
        p.op(eng, f)
    return store


def _half_view(dram_ap, p128=128):
    return dram_ap.rearrange("(c p) (h u) -> h p c u", p=p128, h=2)



def build_p5(p, t):
    xn3_sb = p.sb("xn3_sb", [128, 16 * TOK], BF16)
    with p.scope():
        ln_done = ln_sweep(p, t["x2"], xn3_sb, t["xn3"], TOK)
    xn3_r = xn3_sb[:].rearrange("p (c t) -> p c t", c=16)

    wv = [p.sb(f"wv{i}", [128, 16 * 512], BF16) for i in range(2)]
    ones1 = p.sb("ones1", [1, 128], BF16)
    bias_bf = p.sb("bias_bf", [1, 512], BF16)
    bias_f32 = p.sb("bias_f32", [1, 512], F32)
    vt = [p.sb(f"vt{i}", [128, 512], BF16) for i in range(2)]
    wqk = p.sb("wqk", [128, 16 * 128], BF16)
    bqk = p.sb("bqk", [128, 1], F32)
    gq = p.sb("gq", [128, 1], F32)
    bq = p.sb("bq", [128, 1], F32)
    gk = p.sb("gk", [128, 1], F32)
    bk = p.sb("bk", [128, 1], F32)
    qkt = p.sb("qkt", [128, 512], BF16)
    q3s = p.sb("q3s", [128, TOK], BF16)
    k3s = p.sb("k3s", [128, TOK], BF16)
    ps = [p.psb(f"ps{i}", [128, 512]) for i in range(4)]

    v3_v = t["v3"].rearrange("(c p) f -> p c f", p=128)
    trk = Tracker()

    init5 = p.op("vector", lambda v: v.memset(ones1[:], 1.0), inc="p5_ini")
    tab5 = None
    for nm, dst in (("b_qk_pp", bqk), ("gq_pp", gq), ("bq_pp", bq), ("gk_pp", gk), ("bk_pp", bk)):
        tab5 = p.op("sync", lambda s, nm=nm, dst=dst: s.dma_start(out=dst[:], in_=t[nm]),
             wait=_w(ln_done), inc="p5_tab", inc_by=16)

    wqk_tok = p.op("gpsimd", lambda g: g.dma_start(
        out=wqk[:].rearrange("p (c o) -> p c o", c=16),
        in_=t["w_qk"].rearrange("(c p) o -> p c o", p=128)),
        wait=_w(ln_done), inc="p5_wld", inc_by=16)
    for tt in range(4):
        mm = None
        for c in range(16):
            w0 = _w(("p5_wld", wqk_tok), ln_done, ("p5_tab", tab5),
                    ("p5_qaff", p.count("p5_qaff"))) if c == 0 else ()
            mm = p.op("tensor", lambda te, c=c, tt=tt: te.matmul(
                ps[0][:, :], wqk[:, c * 128:(c + 1) * 128],
                xn3_r[:, c, tt * 512:(tt + 1) * 512],
                start=(c == 0), stop=(c == 15)),
                wait=w0, inc=("p5_mm" if c == 15 else None))
        sl = p.op("scalar", lambda s: s.activation(qkt[:], ps[0][:, :], AF.Silu, bias=bqk[:, 0:1]),
                  wait=_w(("p5_mm", mm)), inc="p5_ev")
        p.op("vector", lambda v, tt=tt: v.tensor_scalar(
            q3s[:, tt * 512:(tt + 1) * 512], qkt[:], gq[:, 0:1], bq[:, 0:1], OP.mult, OP.add),
            wait=_w(("p5_ev", sl)))
        p.op("vector", lambda v, tt=tt: v.tensor_scalar(
            k3s[:, tt * 512:(tt + 1) * 512], qkt[:], gk[:, 0:1], bk[:, 0:1], OP.mult, OP.add),
            inc="p5_qaff")
    qa = p.count("p5_qaff")
    p.op("scalar", lambda s: s.dma_start(out=t["q3"], in_=q3s[:]),
         wait=_w(("p5_qaff", qa)), inc="p5_st", inc_by=16)
    p.op("scalar", lambda s: s.dma_start(out=t["k3"], in_=k3s[:]),
         wait=_w(("p5_qaff", qa)), inc="p5_st", inc_by=16)

    for ofs in range(8):
        wb = wv[ofs % 2]
        wld = p.op("gpsimd", lambda g, ofs=ofs, wb=wb: g.dma_start(
            out=wb[:].rearrange("p (c o) -> p c o", c=16),
            in_=t["w_hid"].rearrange("(c p) o -> p c o", p=128)[:, :, ofs * 512:(ofs + 1) * 512]),
            wait=_w(trk.gate(("wv", ofs % 2)) or (ln_done if ofs < 2 else None)),
            inc="p5_wld", inc_by=16)
        bl = p.op("gpsimd", lambda g, ofs=ofs: g.dma_start(
            out=bias_f32[:], in_=t["b_hid_row"][0:1, ofs * 512:(ofs + 1) * 512]),
            wait=_w(("p5_bc", p.count("p5_bc"))), inc="p5_wld", inc_by=16)
        bc = p.op("vector", lambda v: v.tensor_copy(bias_bf[:], bias_f32[:]),
                  wait=_w(("p5_wld", p.count("p5_wld")), ("p5_ini", init5),
                          ("p5_vmm", p.count("p5_vmm"))), inc="p5_bc")
        for tc in range(16):
            pa = ps[tc % 4]
            for c in range(16):
                w0 = ()
                if c == 0:
                    w0 = _w(("p5_wld", p.count("p5_wld")), ln_done if ofs == 0 else None,
                            ("p5_bc", bc), trk.gate(("psv", tc % 4)))
                p.op("tensor", lambda te, c=c, tc=tc, pa=pa, wb=wb: te.matmul(
                    pa[:, :], xn3_r[:, c, tc * 128:(tc + 1) * 128],
                    wb[:, c * 512:(c + 1) * 512],
                    start=(c == 0), stop=False), wait=w0)
            mm = p.op("tensor", lambda te, pa=pa: te.matmul(
                pa[:, :], ones1[:], bias_bf[:], start=False, stop=True),
                inc="p5_vmm")
            ev = p.op("scalar", lambda s, pa=pa, tc=tc: s.activation(
                vt[tc % 2][:], pa[:, :], AF.Silu),
                wait=_w(("p5_vmm", mm), trk.gate(("vt", tc % 2))), inc="p5_ev")
            trk.set(("psv", tc % 4), "p5_ev", ev)
            stt = p.op("scalar", lambda s, tc=tc, ofs=ofs: s.dma_start(
                out=v3_v[:, tc, ofs * 512:(ofs + 1) * 512], in_=vt[tc % 2][:]),
                wait=_w(("p5_ev", ev)), inc="p5_st", inc_by=16)
            trk.set(("vt", tc % 2), "p5_st", stt)
        trk.set(("wv", ofs % 2), "p5_vmm", p.count("p5_vmm"))
    p.wait("sync", "p5_st", p.count("p5_st"))



def build_p6a(p, t):
    par = plan_par(p, ["gpsimd"])
    xn3o = p.sb("xn3o", [128, 16 * 1024], BF16)
    whg = [p.sb(f"whg{i}", [128, 16 * 128], BF16) for i in range(2)]
    bg = p.sb("bg", [128, 32], F32)
    gt = [p.sb(f"gt{i}", [128, 512], BF16) for i in range(2)]
    ps = [p.psb(f"ps{i}", [128, 512]) for i in range(4)]
    xn3_r = xn3o[:].rearrange("p (c t) -> p c t", c=16)
    g3_v = t["gate3"].rearrange("(c p) t -> p c t", p=128)
    xn3h = _half_view(t["xn3"])

    bgt = p.op("sync", lambda s: s.dma_start(out=bg[:], in_=t["b_hidg_pp"]), inc="p6_tab", inc_by=16)
    ldx = p.op("gpsimd", lambda g: g.dma_start(
        out=xn3o[:].rearrange("p (c t) -> p c t", c=16), in_=xn3h[par["gpsimd"]]),
        inc="p6_ld", inc_by=16)
    trk = Tracker()
    for oc in range(32):
        wb = whg[oc % 2]
        wld = p.op("gpsimd", lambda g, oc=oc, wb=wb: g.dma_start(
            out=wb[:].rearrange("p (c o) -> p c o", c=16),
            in_=t["w_hid"].rearrange("(c p) o -> p c o", p=128)[:, :, 4096 + oc * 128:4096 + (oc + 1) * 128]),
            wait=_w(trk.gate(("whg", oc % 2))), inc="p6_wld", inc_by=16)
        for ts in range(2):
            pi = (oc * 2 + ts) % 4
            mm = None
            for c in range(16):
                w0 = ()
                if c == 0:
                    w0 = _w(("p6_wld", wld), ("p6_ld", ldx) if oc == 0 else None,
                            ("p6_tab", bgt) if oc == 0 else None,
                            trk.gate(("ps", pi)))
                mm = p.op("tensor", lambda te, c=c, ts=ts, pi=pi, wb=wb: te.matmul(
                    ps[pi][:, :], wb[:, c * 128:(c + 1) * 128],
                    xn3_r[:, c, ts * 512:(ts + 1) * 512],
                    start=(c == 0), stop=(c == 15)),
                    wait=w0, inc=("p6_mm" if c == 15 else None))
            ev = p.op("scalar", lambda s, oc=oc, ts=ts, pi=pi: s.activation(
                gt[ts][:], ps[pi][:, :], AF.Silu, bias=bg[:, oc:oc + 1]),
                wait=_w(("p6_mm", mm), trk.gate(("gt", ts))), inc="p6_ev")
            trk.set(("ps", pi), "p6_ev", ev)
            stt = p.op("scalar", lambda s, oc=oc, ts=ts: s.dma_start(
                out=g3_v[:, oc, ts * 512:(ts + 1) * 512], in_=gt[ts][:]),
                wait=_w(("p6_ev", ev)), inc="p6_st", inc_by=16)
            trk.set(("gt", ts), "p6_st", stt)
        trk.set(("whg", oc % 2), "p6_mm", p.count("p6_mm"))
    p.wait("sync", "p6_st", p.count("p6_st"))



def build_p6b(p, t):
    par = plan_par(p, ["gpsimd"])
    v_sb = p.sb("v_sb", [128, 16 * 4096], BF16)
    k_sb = p.sb("k_sb", [128, TOK], BF16)
    q_sb = p.sb("q_sb", [128, 1024], BF16)
    sT = p.sb("sT", [128, 16 * 1024], BF16)
    rt = p.sb("rt", [128, 512], BF16)
    gc = [p.sb(f"gc{i}", [128, 1024], BF16) for i in range(2)]
    avt = [p.sb(f"avt{i}", [128, 512], BF16) for i in range(2)]
    ps = [p.psb(f"ps{i}", [128, 512]) for i in range(4)]

    v_r = v_sb[:].rearrange("p (c f) -> p c f", c=16)
    sT_r = sT[:].rearrange("p (c q) -> p c q", c=16)
    v3_v = t["v3"].rearrange("(c p) f -> p c f", p=128)
    g3_v = t["gate3"].rearrange("(c p) t -> p c t", p=128)
    avg_v = t["avg"].rearrange("(c p) t -> p c t", p=128)
    q3h = t["q3"].rearrange("p (h u) -> h p u", h=2)

    ldv = p.op("gpsimd", lambda g: g.dma_start(out=v_r[:, :, :], in_=v3_v[:, :, :]),
               inc="p6_ld", inc_by=16)
    ldk = p.op("gpsimd", lambda g: g.dma_start(out=k_sb[:], in_=t["k3"]),
               inc="p6_ld", inc_by=16)
    ldq = p.op("gpsimd", lambda g: g.dma_start(out=q_sb[:], in_=q3h[par["gpsimd"]]),
               inc="p6_ld", inc_by=16)

    rsc = float(1.0 / (2048.0 * 2048.0))
    sdone = 0
    for tc in range(16):
        for qs in range(2):
            pa = ps[(tc * 2 + qs) % 4]
            gate_sd = max(0, p.count("p6_sd") - 2)
            mm = p.op("tensor", lambda te, tc=tc, qs=qs, pa=pa: te.matmul(
                pa[:, :], k_sb[:, tc * 128:(tc + 1) * 128],
                q_sb[:, qs * 512:(qs + 1) * 512], start=True, stop=True),
                wait=_w(("p6_ld", ldq) if tc == 0 and qs == 0 else None,
                        ("p6_sd", gate_sd)),
                inc="p6_mm")
            rv = p.op("scalar", lambda s, pa=pa: s.activation(
                rt[:], pa[:, :], AF.Relu, scale=rsc),
                wait=_w(("p6_mm", mm), ("p6_sd", p.count("p6_sd"))), inc="p6_rv")
            sdone = p.op("vector", lambda v, tc=tc, qs=qs, pa=pa: v.tensor_tensor(
                sT_r[:, tc, qs * 512:(qs + 1) * 512], pa[:, :], rt[:], OP.mult),
                wait=_w(("p6_rv", rv)), inc="p6_sd")
    trk = Tracker()
    for ec in range(32):
        gl = p.op("gpsimd", lambda g, ec=ec: g.dma_start(
            out=gc[ec % 2][:], in_=g3_v[:, ec, :]),
            wait=_w(trk.gate(("gc", ec % 2))), inc="p6_gld", inc_by=16)
        for qs in range(2):
            pi = (ec * 2 + qs) % 4
            pa = ps[pi]
            mm = None
            for tc in range(16):
                w0 = ()
                if tc == 0:
                    w0 = _w(("p6_sd", p.count("p6_sd")) if ec == 0 else None,
                            trk.gate(("psb", pi)))
                mm = p.op("tensor", lambda te, tc=tc, qs=qs, ec=ec, pa=pa: te.matmul(
                    pa[:, :], v_r[:, tc, ec * 128:(ec + 1) * 128],
                    sT_r[:, tc, qs * 512:(qs + 1) * 512],
                    start=(tc == 0), stop=(tc == 15)),
                    wait=w0, inc=("p6_mm" if tc == 15 else None))
            ev = p.op("vector", lambda v, ec=ec, qs=qs, pa=pa: v.tensor_tensor(
                avt[qs][:], pa[:, :], gc[ec % 2][:, qs * 512:(qs + 1) * 512], OP.mult),
                wait=_w(("p6_mm", mm), ("p6_gld", gl), trk.gate(("avt", qs))),
                inc="p6_ev")
            trk.set(("psb", pi), "p6_ev", ev)
            stt = p.op("scalar", lambda s, ec=ec, qs=qs: s.dma_start(
                out=avg_v[:, ec, qs * 512:(qs + 1) * 512], in_=avt[qs][:]),
                wait=_w(("p6_ev", ev)), inc="p6_st", inc_by=16)
            trk.set(("avt", qs), "p6_st", stt)
        trk.set(("gc", ec % 2), "p6_ev", p.count("p6_ev"))
    p.wait("sync", "p6_st", p.count("p6_st"))



def build_p6c(p, t):
    par = plan_par(p, ["gpsimd"])
    avg_sb = p.sb("avg_sb", [128, 32 * 1024], BF16)
    w1 = [p.sb(f"w1{i}", [128, 32 * 128], BF16) for i in range(2)]
    bo = p.sb("bo", [128, 16], F32)
    xn3o = p.sb("xn3o", [128, 16 * 1024], BF16)
    x2o = p.sb("x2o", [128, 16 * 1024], F32)
    tmp = p.sb("tmp", [128, 512], F32)
    x3t = [p.sb(f"x3t{i}", [128, 512], F32) for i in range(2)]
    ps = [p.psb(f"ps{i}", [128, 512]) for i in range(4)]

    avg_r = avg_sb[:].rearrange("p (c t) -> p c t", c=32)
    avg_v = t["avg"].rearrange("(c p) t -> p c t", p=128)
    x3_v = t["x3o"].rearrange("(c p) t -> p c t", p=128)
    xn3_r = xn3o[:].rearrange("p (c t) -> p c t", c=16)
    x2_r = x2o[:].rearrange("p (c t) -> p c t", c=16)
    xn3h = _half_view(t["xn3"])
    x2h = _half_view(t["x2"])

    bot = p.op("sync", lambda s: s.dma_start(out=bo[:], in_=t["b_out1_pp"]), inc="p6_tab", inc_by=16)
    lda = p.op("gpsimd", lambda g: g.dma_start(out=avg_r[:, :, :], in_=avg_v[:, :, :]),
               inc="p6_ld", inc_by=16)
    p.op("gpsimd", lambda g: g.dma_start(
        out=xn3o[:].rearrange("p (c t) -> p c t", c=16), in_=xn3h[par["gpsimd"]]),
        inc="p6_ld", inc_by=16)
    ldx = p.op("gpsimd", lambda g: g.dma_start(
        out=x2o[:].rearrange("p (c t) -> p c t", c=16), in_=x2h[par["gpsimd"]]),
        inc="p6_ld", inc_by=16)
    trk = Tracker()
    for dc in range(16):
        wb = w1[dc % 2]
        wld = p.op("gpsimd", lambda g, dc=dc, wb=wb: g.dma_start(
            out=wb[:].rearrange("p (c o) -> p c o", c=32),
            in_=t["w_out1"].rearrange("(c p) o -> p c o", p=128)[:, :, dc * 128:(dc + 1) * 128]),
            wait=_w(trk.gate(("w1", dc % 2))), inc="p6_wld", inc_by=16)
        for qs in range(2):
            pi = (dc * 2 + qs) % 4
            mm = None
            for hc in range(32):
                w0 = ()
                if hc == 0:
                    w0 = _w(("p6_wld", wld), ("p6_ld", lda) if dc == 0 else None,
                            trk.gate(("ps", pi)))
                mm = p.op("tensor", lambda te, hc=hc, qs=qs, pi=pi, wb=wb: te.matmul(
                    ps[pi][:, :], wb[:, hc * 128:(hc + 1) * 128],
                    avg_r[:, hc, qs * 512:(qs + 1) * 512],
                    start=(hc == 0), stop=(hc == 31)),
                    wait=w0, inc=("p6_mm" if hc == 31 else None))
            a1 = p.op("scalar", lambda s, dc=dc, qs=qs, pi=pi: s.activation(
                tmp[:], ps[pi][:, :], AF.Identity, bias=bo[:, dc:dc + 1]),
                wait=_w(("p6_mm", mm), ("p6_tab", bot) if dc == 0 else None,
                        ("p6_ld", ldx) if dc == 0 else None,
                        ("p6_a2", p.count("p6_a2"))), inc="p6_a1")
            trk.set(("ps", pi), "p6_a1", a1)
            a2 = p.op("vector", lambda v, dc=dc, qs=qs: v.tensor_tensor(
                tmp[:], tmp[:], xn3_r[:, dc, qs * 512:(qs + 1) * 512], OP.add),
                wait=_w(("p6_a1", a1)), inc="p6_a2")
            ev = p.op("vector", lambda v, dc=dc, qs=qs: v.tensor_tensor(
                x3t[qs][:], tmp[:], x2_r[:, dc, qs * 512:(qs + 1) * 512], OP.add),
                wait=_w(trk.gate(("x3t", qs))), inc="p6_a2")
            stt = p.op("scalar", lambda s, dc=dc, qs=qs: s.dma_start(
                out=x3_v[:, dc, qs * 512:(qs + 1) * 512], in_=x3t[qs][:]),
                wait=_w(("p6_a2", ev)), inc="p6_st", inc_by=16)
            trk.set(("x3t", qs), "p6_st", stt)
        trk.set(("w1", dc % 2), "p6_mm", p.count("p6_mm"))
    p.wait("sync", "p6_st", p.count("p6_st"))



def build_p7a(p, t):
    xn4_sb = p.sb("xn4_sb", [128, 16 * 1024], BF16)
    with p.scope():
        ln_done = ln_sweep(p, t["x3o"], xn4_sb, None, 1024)
    xn4_r = xn4_sb[:].rearrange("p (c t) -> p c t", c=16)

    wT_g = [p.sb(f"wT_g{i}", [128, 16 * 512], BF16) for i in range(2)]
    wT_u = [p.sb(f"wT_u{i}", [128, 16 * 512], BF16) for i in range(2)]
    gt = p.sb("gt", [128, 512], BF16)
    ut = p.sb("ut", [128, 512], BF16)
    ht = [p.sb(f"ht{i}", [128, 512], BF16) for i in range(2)]
    ps_u = [p.psb(f"ps_u{i}", [128, 512]) for i in range(4)]
    ps_g = [p.psb(f"ps_g{i}", [128, 512]) for i in range(4)]

    h4_v = t["h4"].rearrange("(c p) t -> p c t", p=128)
    hr_v = t["hr4"].rearrange("(c p) t -> p c t", p=128)

    trk = Tracker()
    for g in range(18):
        routed = g >= 16
        wsrc_g = t["w_rg"] if routed else t["w_shg"]
        wsrc_u = t["w_ru"] if routed else t["w_shu"]
        oc0 = (g - 16) * 4 if routed else 4 * g
        wtr_g = wt_stream(p, trk, wsrc_g, oc0, 4, wT_g[g % 2],
                          trk.gate(("wTg", g % 2)) or (ln_done if g < 2 else None))
        wtr_u = wt_stream(p, trk, wsrc_u, oc0, 4, wT_u[g % 2],
                          trk.gate(("wTu", g % 2)) or (ln_done if g < 2 else None))
        wg, wu = wT_g[g % 2], wT_u[g % 2]
        for j in range(4):
            oc = oc0 + j
            pu, pg = ps_u[j], ps_g[j]
            for ts in range(2):
                for c in range(16):
                    w0 = ()
                    if c == 0:
                        w0 = _w(wtr_u if (ts == 0 and j == 0) else None,
                                ln_done if (g == 0 and j == 0 and ts == 0) else None,
                                trk.gate(("ps", j)))
                    p.op("tensor", lambda te, j=j, c=c, ts=ts, pg=pg, wg=wg: te.matmul(
                        pg[:, :], wg[:, c * 512 + j * 128:c * 512 + (j + 1) * 128],
                        xn4_r[:, c, ts * 512:(ts + 1) * 512],
                        start=(c == 0), stop=(c == 15)), wait=w0)
                for c in range(16):
                    p.op("tensor", lambda te, j=j, c=c, ts=ts, pu=pu, wu=wu: te.matmul(
                        pu[:, :], wu[:, c * 512 + j * 128:c * 512 + (j + 1) * 128],
                        xn4_r[:, c, ts * 512:(ts + 1) * 512],
                        start=(c == 0), stop=(c == 15)),
                        inc=("p7_mm" if c == 15 else None))
                mmt = p.count("p7_mm")
                e1 = p.op("scalar", lambda s, pg=pg: s.activation(gt[:], pg[:, :], AF.Silu),
                          wait=_w(("p7_mm", mmt), ("p7_h", p.count("p7_h"))), inc="p7_ev")
                e2 = p.op("vector", lambda v, pu=pu: v.tensor_copy(ut[:], pu[:, :]),
                          wait=_w(("p7_mm", mmt), ("p7_h", p.count("p7_h"))), inc="p7_ev")
                hv = p.op("vector", lambda v, ts=ts: v.tensor_tensor(
                    ht[ts][:], gt[:], ut[:], OP.mult),
                    wait=_w(("p7_ev", p.count("p7_ev")), trk.gate(("ht", ts))),
                    inc="p7_h")
                trk.set(("ps", j), "p7_h", hv)
                dst = hr_v if routed else h4_v
                stt = p.op("scalar", lambda s, oc=oc, ts=ts, dst=dst: s.dma_start(
                    out=dst[:, oc, ts * 512:(ts + 1) * 512], in_=ht[ts][:]),
                    wait=_w(("p7_h", hv)), inc="p7_sth", inc_by=16)
                trk.set(("ht", ts), "p7_sth", stt)
        trk.set(("wTg", g % 2), "p7_mm", p.count("p7_mm"))
        trk.set(("wTu", g % 2), "p7_mm", p.count("p7_mm"))
    p.wait("sync", "p7_sth", p.count("p7_sth"))



def build_p7b(p, t):
    h4_sb = p.sb("h4_sb", [128, 64 * 1024], BF16)
    hr_sb = p.sb("hr_sb", [128, 8 * 1024], BF16)
    wT = [p.sb(f"wT{i}", [128, 64 * 128], BF16) for i in range(2)]
    wT_r = [p.sb(f"wT_r{i}", [128, 8 * 128], BF16) for i in range(2)]
    x3c = [p.sb(f"x3c{i}", [128, 512], F32) for i in range(2)]
    yt = [p.sb(f"yt{i}", [128, 512], F32) for i in range(2)]
    ps = [p.psb(f"ps{i}", [128, 512]) for i in range(4)]

    h4_r = h4_sb[:].rearrange("p (c t) -> p c t", c=64)
    hr_r = hr_sb[:].rearrange("p (c t) -> p c t", c=8)
    h4_v = t["h4"].rearrange("(c p) t -> p c t", p=128)
    hr_v = t["hr4"].rearrange("(c p) t -> p c t", p=128)
    x3_v = t["x3o"].rearrange("(c p) t -> p c t", p=128)
    y_v = t["yT"].rearrange("(c p) t -> p c t", p=128)

    ldh = p.op("gpsimd", lambda g: g.dma_start(out=h4_r[:, :, :], in_=h4_v[:, :, :]),
               inc="p7_ld", inc_by=16)
    ldr = p.op("gpsimd", lambda g: g.dma_start(out=hr_r[:, :, :], in_=hr_v[:, :, :]),
               inc="p7_ld", inc_by=16)
    trk = Tracker()
    for dc in range(16):
        wtr = wt_stream(p, trk, t["w_shd"], dc, 1, wT[dc % 2],
                        trk.gate(("wT", dc % 2)), ifc=64)
        wtr_r = wt_stream(p, trk, t["w_rd"], dc, 1, wT_r[dc % 2],
                          trk.gate(("wT", dc % 2)), ifc=8)
        wb, wbr = wT[dc % 2], wT_r[dc % 2]
        for qs in range(2):
            pi = (dc * 2 + qs) % 4
            mm = None
            for c in range(64):
                w0 = ()
                if c == 0:
                    w0 = _w(wtr_r if qs == 0 else None,
                            ("p7_ld", ldr) if dc == 0 else None,
                            trk.gate(("ps", pi)))
                p.op("tensor", lambda te, c=c, qs=qs, pi=pi, wb=wb: te.matmul(
                    ps[pi][:, :], wb[:, c * 128:(c + 1) * 128],
                    h4_r[:, c, qs * 512:(qs + 1) * 512],
                    start=(c == 0), stop=False), wait=w0)
            for c in range(8):
                mm = p.op("tensor", lambda te, c=c, qs=qs, pi=pi, wbr=wbr: te.matmul(
                    ps[pi][:, :], wbr[:, c * 128:(c + 1) * 128],
                    hr_r[:, c, qs * 512:(qs + 1) * 512],
                    start=False, stop=(c == 7)),
                    inc=("p7_mm" if c == 7 else None))
            xl = p.op("gpsimd", lambda g, dc=dc, qs=qs: g.dma_start(
                out=x3c[qs][:], in_=x3_v[:, dc, qs * 512:(qs + 1) * 512]),
                wait=_w(trk.gate(("x3c", qs))), inc="p7_xld", inc_by=16)
            ev = p.op("vector", lambda v, qs=qs, pi=pi: v.tensor_tensor(
                yt[qs][:], ps[pi][:, :], x3c[qs][:], OP.add),
                wait=_w(("p7_mm", mm), ("p7_xld", xl), trk.gate(("yt", qs))),
                inc="p7_ev")
            trk.set(("ps", pi), "p7_ev", ev)
            trk.set(("x3c", qs), "p7_ev", ev)
            stt = p.op("scalar", lambda s, dc=dc, qs=qs: s.dma_start(
                out=y_v[:, dc, qs * 512:(qs + 1) * 512], in_=yt[qs][:]),
                wait=_w(("p7_ev", ev)), inc="p7_st", inc_by=16)
            trk.set(("yt", qs), "p7_st", stt)
        trk.set(("wT", dc % 2), "p7_mm", p.count("p7_mm"))
    p.wait("sync", "p7_st", p.count("p7_st"))



CONST_NAMES = ["h128", "ddT_tab", "qd_tab", "kd_tab", "bd_tab"]

PHASES = {
    "p1": (build_p1, ["x0T", "w_qkv"], ["xn", "qk_fm", "kv_tm"]),
    "p2": (build_p2, ["qk_fm", "kv_tm", "ddT_tab", "qd_tab", "kd_tab", "bd_tab"], ["attn"]),
    "p3": (build_p3, ["attn", "nw_pp", "xn", "w_gate", "w_out0", "x0T"], ["x1"]),
    "p4a": (build_p4a, ["x1", "su_ug_pp", "h128"], ["x1h"]),
    "p4b": (build_p4b, ["x1h", "w_up", "w_wgate", "wsu_pp", "wsg_pp", "su_dp_pp"], ["hbuf"]),
    "p4c": (build_p4c, ["hbuf", "h128"], ["hpbuf"]),
    "p4d": (build_p4d, ["hpbuf", "w_down", "wsd_pp", "x1"], ["x2"]),
    "p5": (build_p5, ["x2", "w_hid", "b_hid_row", "w_qk", "b_qk_pp",
                      "gq_pp", "bq_pp", "gk_pp", "bk_pp"], ["xn3", "v3", "q3", "k3"]),
    "p6a": (build_p6a, ["xn3", "w_hid", "b_hidg_pp"], ["gate3"]),
    "p6b": (build_p6b, ["q3", "k3", "v3", "gate3"], ["avg"]),
    "p6c": (build_p6c, ["avg", "w_out1", "b_out1_pp", "x2", "xn3"], ["x3o"]),
    "p7a": (build_p7a, ["x3o", "w_shg", "w_shu", "w_rg", "w_ru"], ["h4", "hr4"]),
    "p7b": (build_p7b, ["h4", "hr4", "w_shd", "w_rd", "x3o"], ["yT"]),
}

PHASE_ORDER = ["p1", "p2", "p3", "p4a", "p4b", "p4c", "p4d",
               "p5", "p6a", "p6b", "p6c", "p7a", "p7b"]

TENSORS = {
    "x0T":       ((DM, TOK), F32),
    "w_qkv":     ((DM, 6144), F32),
    "w_gate":    ((DM, DM), F32),
    "w_out0":    ((DM, DM), F32),
    "nw_pp":     ((128, 16), F32),
    "su_ug_pp":  ((128, 16), F32),
    "wsu_pp":    ((128, 64), F32),
    "wsg_pp":    ((128, 64), F32),
    "su_dp_pp":  ((128, 64), F32),
    "wsd_pp":    ((128, 16), F32),
    "w_up":      ((DM, 8192), F32),
    "w_wgate":   ((DM, 8192), F32),
    "w_down":    ((8192, DM), F32),
    "w_hid":     ((DM, 8192), F32),
    "b_hid_row": ((1, 8192), F32),
    "b_hidg_pp": ((128, 32), F32),
    "w_qk":      ((DM, 128), F32),
    "b_qk_pp":   ((128, 1), F32),
    "gq_pp":     ((128, 1), F32),
    "bq_pp":     ((128, 1), F32),
    "gk_pp":     ((128, 1), F32),
    "bk_pp":     ((128, 1), F32),
    "w_out1":    ((4096, DM), F32),
    "b_out1_pp": ((128, 16), F32),
    "w_shg":     ((DM, 8192), F32),
    "w_shu":     ((DM, 8192), F32),
    "w_shd":     ((8192, DM), F32),
    "w_rg":      ((DM, 1024), F32),
    "w_ru":      ((DM, 1024), F32),
    "w_rd":      ((1024, DM), F32),
    "h128":      ((128, 128), BF16),
    "ddT_tab":   ((128, H * 2 * BS), F32),
    "qd_tab":    ((128, 16 * BS), F32),
    "kd_tab":    ((128, 2, DM), F32),
    "bd_tab":    ((128, 1024), F32),
    "xn":        ((DM, TOK), BF16),
    "qk_fm":     ((4096, TOK), BF16),
    "kv_tm":     ((TOK, 4096), BF16),
    "attn":      ((DM, TOK), BF16),
    "attnn":     ((DM, TOK), BF16),
    "x1":        ((DM, TOK), F32),
    "x1h":       ((DM, TOK), BF16),
    "hbuf":      ((8192, TOK), BF16),
    "hpbuf":     ((8192, TOK), BF16),
    "x2":        ((DM, TOK), F32),
    "xn3":       ((DM, TOK), BF16),
    "v3":        ((TOK, 4096), BF16),
    "q3":        ((128, TOK), BF16),
    "k3":        ((128, TOK), BF16),
    "gate3":     ((4096, TOK // 2), BF16),
    "avg":       ((4096, TOK // 2), BF16),
    "x3o":       ((DM, TOK // 2), F32),
    "h4":        ((8192, TOK // 2), BF16),
    "hr4":       ((1024, TOK // 2), BF16),
    "yT":        ((DM, TOK // 2), F32),
}



def check_fused_schedule(progs):
    streams = {}
    for e in ENGS:
        items = []
        for k, p in enumerate(progs):
            if k > 0:
                items.append(([("bar", 5 * k)], None, 0))
            items.extend(fn.meta for fn in p.items[e])
            items.append(([], "bar", 1))
        streams[e] = items
    pos = {e: 0 for e in ENGS}
    sems = {}
    progress = True
    while progress:
        progress = False
        for e in ENGS:
            while pos[e] < len(streams[e]):
                waits, inc, by = streams[e][pos[e]]
                if all(sems.get(s, 0) >= v for (s, v) in waits):
                    if inc:
                        sems[inc] = sems.get(inc, 0) + by
                    pos[e] += 1
                    progress = True
                else:
                    break
    stuck = {e: (pos[e], len(streams[e])) for e in ENGS if pos[e] < len(streams[e])}
    if stuck:
        lines = []
        for e, (i, n) in stuck.items():
            waits = streams[e][i][0]
            missing = [(s, v, sems.get(s, 0)) for (s, v) in waits if sems.get(s, 0) < v]
            lines.append(f"{e}: item {i}/{n} waiting {missing}")
        raise RuntimeError("DEADLOCK in fused schedule:\n" + "\n".join(lines))
    return True

MID_NAMES = ["xn", "qk_fm", "kv_tm", "attn", "attnn", "x1", "x1h", "hbuf",
             "hpbuf", "x2", "xn3", "v3", "q3", "k3", "gate3", "avg", "x3o",
             "h4", "hr4"]


def fused_input_names():
    names = []
    for name in PHASE_ORDER:
        _, ins, _ = PHASES[name]
        for n in ins:
            if n not in MID_NAMES and n not in names:
                names.append(n)
    return names


def build_fused_nc(debug_mids=False):
    nc = bass.Bass("TRN2", target_bir_lowering=False, debug=False)
    t = {}
    for n in fused_input_names():
        shape, dt = TENSORS[n]
        t[n] = nc.dram_tensor(n, list(shape), dt, kind="ExternalInput").ap()
    for n in MID_NAMES:
        shape, dt = TENSORS[n]
        kind = "ExternalOutput" if (debug_mids and n != "attnn") else "Internal"
        t[n] = nc.dram_tensor(n, list(shape), dt, kind=kind).ap()
    shape, dt = TENSORS["yT"]
    t["yT"] = nc.dram_tensor("yT", list(shape), dt, kind="ExternalOutput").ap()

    with contextlib.ExitStack() as sem_stack:
        sem_handles, sem_counts = {}, {}
        progs = []
        for name in PHASE_ORDER:
            build, ins, outs = PHASES[name]
            with contextlib.ExitStack() as sb_stack:
                p = Prog(nc, sb_stack, tag=name + "_", sem_stack=sem_stack,
                         sem_handles=sem_handles, sem_counts=sem_counts)
                build(p, t)
            progs.append(p)
        check_fused_schedule(progs)
        bar = sem_stack.enter_context(nc.semaphore("bar"))
        with nc.Block() as block:
            nphase = len(progs)

            def make_runner(eng):
                def runner(engine):
                    for k in range(nphase):
                        if k > 0:
                            engine.wait_ge(bar, 5 * k)
                        for fn in progs[k].items[eng]:
                            fn(engine)
                        engine.drain().then_inc(bar, 1)
                return runner
            for e in ENGS:
                getattr(block, e)(make_runner(e))
    return nc


def build_phase_nc(name):
    build, ins, outs = PHASES[name]
    nc = bass.Bass("TRN2", target_bir_lowering=False, debug=False)
    t = {}
    for n in ins:
        shape, dt = TENSORS[n]
        t[n] = nc.dram_tensor(n, list(shape), dt, kind="ExternalInput").ap()
    for n in outs:
        shape, dt = TENSORS[n]
        t[n] = nc.dram_tensor(n, list(shape), dt, kind="ExternalOutput").ap()
    with contextlib.ExitStack() as stack:
        p = Prog(nc, stack)
        build(p, t)
        p.emit()
    return nc, ins, outs



def _prep_weights(inputs):
    w = {}
    tr = lambda a: np.ascontiguousarray(np.asarray(a, np.float32).T)
    w["w_qkv"] = tr(np.asarray(inputs["L0_attn_qkv_w"]).reshape(H, 3, D, DM)
                    .transpose(1, 0, 2, 3).reshape(6144, DM))
    w["w_gate"] = tr(inputs["L0_attn_gate_w"])
    w["w_out0"] = tr(inputs["L0_attn_out_w"])
    w["nw_pp"] = _pp(inputs["L0_attn_norm_w"], 16)
    w["su_ug_pp"] = _pp(inputs["L0_ffn_SU_ug"], 16)
    w["wsu_pp"] = _pp(inputs["L0_ffn_Wscale_ug"][:8192], 64)
    w["wsg_pp"] = _pp(inputs["L0_ffn_Wscale_ug"][8192:], 64)
    w["su_dp_pp"] = _pp(inputs["L0_ffn_SU_dp"], 64)
    w["wsd_pp"] = _pp(inputs["L0_ffn_Wscale_dp"], 16)
    w["w_up"] = tr(inputs["L0_ffn_W_up"])
    w["w_wgate"] = tr(inputs["L0_ffn_W_gate"])
    w["w_down"] = tr(inputs["L0_ffn_W_down"])
    w["w_hid"] = inputs["L1_attn_W_hidden"]
    w["b_hid_row"] = np.asarray(inputs["L1_attn_b_hidden"], np.float32).reshape(1, 8192)
    w["b_hidg_pp"] = _pp(inputs["L1_attn_b_hidden"][4096:], 32)
    w["w_qk"] = inputs["L1_attn_W_qk"]
    for nm, key in (("b_qk_pp", "L1_attn_b_qk"), ("gq_pp", "L1_attn_gamma_q"),
                    ("bq_pp", "L1_attn_beta_q"), ("gk_pp", "L1_attn_gamma_k"),
                    ("bk_pp", "L1_attn_beta_k")):
        w[nm] = np.asarray(inputs[key], np.float32).reshape(128, 1)
    w["w_out1"] = inputs["L1_attn_W_out"]
    w["b_out1_pp"] = _pp(inputs["L1_attn_b_out"], 16)
    w["w_shg"] = tr(inputs["L1_ffn_sh_gate"])
    w["w_shu"] = tr(inputs["L1_ffn_sh_up"])
    w["w_shd"] = tr(inputs["L1_ffn_sh_down"])
    w["w_rg"] = tr(inputs["L1_ffn_r_gate"])
    w["w_ru"] = tr(inputs["L1_ffn_r_up"])
    w["w_rd"] = tr(inputs["L1_ffn_r_down"])
    w.update(_make_consts())
    return {k: np.ascontiguousarray(v) for k, v in w.items()}


def kernel(**inputs):
    x = inputs["L0_attn_hidden_states"]
    w = _prep_weights(inputs)

    state = [dict(w) for _ in range(N_CORES)]
    for c in range(N_CORES):
        state[c]["x0T"] = np.ascontiguousarray(np.asarray(x[c // 2], np.float32).T)

    if FUSED:
        nc = build_fused_nc()
        names = fused_input_names()
        in_maps = [{n: state[c][n] for n in names} for c in range(N_CORES)]
        res = run_bass_kernel_spmd(nc, in_maps, core_ids=list(range(N_CORES)))
        for c in range(N_CORES):
            state[c]["yT"] = res.results[c]["yT"]
    else:
        for name in PHASE_ORDER:
            nc, ins, outs = build_phase_nc(name)
            in_maps = [{n: state[c][n] for n in ins} for c in range(N_CORES)]
            res = run_bass_kernel_spmd(nc, in_maps, core_ids=list(range(N_CORES)))
            for c in range(N_CORES):
                for n in outs:
                    state[c][n] = res.results[c][n]

    out = np.empty((4, TOK, DM), dtype=np.float32)
    for c in range(N_CORES):
        b, half = c // 2, c % 2
        out[b, half * 1024:(half + 1) * 1024, :] = state[c]["yT"].T
    return out


# version 11
# speedup vs baseline: 1.0115x; 1.0115x over previous
import contextlib
import numpy as np
import ml_dtypes
import concourse.bass as bass
import concourse.mybir as mybir
from concourse.bass_utils import run_bass_kernel_spmd

F32 = mybir.dt.float32
BF16 = mybir.dt.bfloat16
AF = mybir.ActivationFunctionType
OP = mybir.AluOpType
BF = ml_dtypes.bfloat16

N_CORES = 8
TOK = 2048
DM = 2048
EPS = 1e-5
H = 32
D = 64
BS = 256
NB = TOK // BS

FUSED = True



def _hadamard(n):
    h = np.array([[1.0]], dtype=np.float32)
    while h.shape[0] < n:
        h = np.block([[h, h], [h, -h]])
    return h.astype(np.float32)


def _decay():
    base = 1.0 / 2.0 ** (8.0 / H)
    rate = base ** (np.arange(H, dtype=np.float32) + 1.0)
    slope = (rate * np.float32(1.0 + 1e-05)).astype(np.float32)
    r = np.arange(BS, dtype=np.float32) + 1.0
    qd = np.exp(-slope[:, None] * r[None, :]).astype(np.float32)
    kd = np.exp(-slope[:, None] * (BS - r[None, :])).astype(np.float32)
    de = slope[:, None, None] * (r[:, None] - r[None, :])[None, :, :]
    with np.errstate(over="ignore"):
        dd = np.where(de >= 0, np.exp(-de), 0.0).astype(np.float32)
    bd = np.exp(-slope * BS).astype(np.float32)
    return qd, kd, dd, bd


def _pp(vec, chunks):
    return np.ascontiguousarray(np.asarray(vec, dtype=np.float32).reshape(chunks, 128).T)


def _make_consts():
    qd, kd, dd, bd = _decay()
    ddT = dd.transpose(0, 2, 1).reshape(H, 2, 128, BS)
    ddT = np.ascontiguousarray(ddT.transpose(2, 0, 1, 3)).reshape(128, H * 2 * BS)
    qd_tab = np.empty((128, 16 * BS), dtype=np.float32)
    for c in range(16):
        for half in range(2):
            h = 2 * c + half
            qd_tab[half * 64:(half + 1) * 64, c * BS:(c + 1) * BS] = qd[h][None, :]
    kd_tab = np.empty((128, 2, DM), dtype=np.float32)
    for h in range(H):
        for par in range(2):
            kd_tab[:, par, h * 64:(h + 1) * 64] = kd[h, par * 128:(par + 1) * 128][:, None]
    bd_tab = np.empty((128, 16 * 64), dtype=np.float32)
    for ch in range(16):
        for half in range(2):
            bd_tab[half * 64:(half + 1) * 64, ch * 64:(ch + 1) * 64] = bd[2 * ch + half]
    return {
        "h128": _hadamard(128).astype(BF),
        "ddT_tab": np.ascontiguousarray(ddT),
        "qd_tab": qd_tab,
        "kd_tab": kd_tab,
        "bd_tab": np.ascontiguousarray(bd_tab),
    }



ENGS = ("sync", "gpsimd", "scalar", "vector", "tensor")


class Prog:

    def __init__(self, nc, stack, tag="", sem_stack=None, sem_handles=None,
                 sem_counts=None):
        self.nc = nc
        self.stack = stack
        self.sem_stack = sem_stack if sem_stack is not None else stack
        self.tag = tag
        self.items = {e: [] for e in ENGS}
        self.chain_prev = {}
        self.sem_handles = sem_handles if sem_handles is not None else {}
        self.sem_counts = sem_counts if sem_counts is not None else {}

    def sem(self, name):
        if name not in self.sem_handles:
            h = self.sem_stack.enter_context(self.nc.semaphore(name))
            self.sem_handles[name] = h
            self.sem_counts[name] = 0
        return self.sem_handles[name]

    def count(self, name):
        self.sem(name)
        return self.sem_counts[name]

    CHAIN_ENGS = ("vector", "scalar", "gpsimd")

    def op(self, eng, build, wait=(), inc=None, inc_by=1, dma=False):
        dma = dma or inc_by == 16
        chained = (eng in self.CHAIN_ENGS) and not dma
        ret_tok_none = inc is None
        if chained and inc is None:
            inc = "seq_" + eng
        wait_list = [(s, v) for (s, v) in wait if v > 0]
        if chained:
            prev = self.chain_prev.get(eng)
            if prev is not None:
                wait_list = wait_list + [prev]
        wait_vals = [(self.sem(s), v) for (s, v) in wait_list]
        tok = None
        inc_pair = None
        if inc is not None:
            self.sem(inc)
            self.sem_counts[inc] += inc_by
            tok = self.sem_counts[inc]
            inc_pair = (self.sem_handles[inc], inc_by)
        if chained:
            self.chain_prev[eng] = (inc, tok)

        def emit(engine):
            for (h, v) in wait_vals:
                engine.wait_ge(h, v)
            ins = build(engine)
            if inc_pair is not None:
                ins.then_inc(inc_pair[0], inc_pair[1])
        emit.meta = (wait_list, inc, inc_by if inc else 0)
        self.items[eng].append(emit)
        return None if ret_tok_none else tok

    def wait(self, eng, sem, val):
        if val <= 0:
            return
        h = self.sem(sem)
        fn = lambda engine: engine.wait_ge(h, val)
        fn.meta = ([(sem, val)], None, 0)
        self.items[eng].append(fn)

    @contextlib.contextmanager
    def scope(self):
        with contextlib.ExitStack() as st:
            outer = self.stack
            self.stack = st
            try:
                yield
            finally:
                self.stack = outer

    def sb(self, nm, shape, dt):
        return self.stack.enter_context(self.nc.sbuf_tensor(self.tag + nm, shape, dt))

    def psb(self, nm, shape, dt=F32):
        return self.stack.enter_context(self.nc.psum_tensor(self.tag + nm, shape, dt))

    def emit(self):
        with self.nc.Block() as block:
            for e in ENGS:
                items = self.items[e]
                if not items:
                    continue

                def runner(engine, items=items):
                    for fn in items:
                        fn(engine)
                getattr(block, e)(runner)


class Tracker:

    def __init__(self):
        self.t = {}

    def gate(self, key):
        return self.t.get(key, None)

    def set(self, key, sem, val):
        self.t[key] = (sem, val)


def _w(*pairs):
    return [pr for pr in pairs if pr is not None and pr[1] > 0]



def ln_sweep(p, src_dram, dst_sb, dst_dram, ntok, extra_scale_pp=None):
    nt = ntok // 512
    xz = [p.sb(f"ln_xz{i}", [128, 16 * 512], F32) for i in range(2)]
    xb = [p.sb(f"ln_xb{i}", [128, 16 * 512], BF16) for i in range(2)]
    sq = [p.sb(f"ln_sq{i}", [128, 16 * 512], BF16) for i in range(2)]
    ones_bf = p.sb("ln_ones_bf", [128, 1], BF16)
    ones32 = p.sb("ln_ones32", [1, 128], F32)
    st = [p.sb(f"ln_stat{i}", [1, 512], F32) for i in range(4)]
    out_tile = ([p.sb(f"ln_out{i}", [128, 16 * 512], BF16) for i in range(2)]
                if dst_sb is None else None)
    ps_sum = [p.psb(f"ln_ps_sum{i}", [1, 512]) for i in range(2)]
    ps_sq = [p.psb(f"ln_ps_sq{i}", [1, 512]) for i in range(2)]
    ps_b = [p.psb(f"ln_ps_b{i}", [128, 1024]) for i in range(2)]

    srcv = src_dram.rearrange("(c p) t -> p c t", p=128)
    dstv = dst_dram.rearrange("(c p) t -> p c t", p=128) if dst_dram is not None else None

    p.op("vector", lambda v: v.memset(ones_bf[:], 1.0))
    ini = p.op("vector", lambda v: v.memset(ones32[:], 1.0), inc="ln_init")

    apply_tok = {0: 0, 1: 0}
    mm_tok = {}
    state = {"last_apply": 0}

    def stage_a(tt):
        si = tt % 2
        t0 = tt * 512
        xb_r = xb[si][:].rearrange("p (c t) -> p c t", c=16)
        sq_r = sq[si][:].rearrange("p (c t) -> p c t", c=16)
        ld = p.op("gpsimd", lambda g, si=si, t0=t0: g.dma_start(
            out=xz[si][:], in_=srcv[:, :, t0:t0 + 512]),
            wait=_w(("ln_apply", apply_tok[si])), inc="ln_ld", inc_by=16)
        cp = p.op("vector", lambda v, si=si: v.tensor_copy(xb[si][:], xz[si][:]),
                  wait=_w(("ln_ld", ld), ("ln_init", ini)), inc="ln_v")
        sqk = p.op("scalar", lambda s, si=si: s.activation(sq[si][:], xb[si][:], AF.Square),
                   wait=_w(("ln_v", cp)), inc="ln_a")
        for c in range(16):
            p.op("tensor", lambda te, c=c, si=si, xb_r=xb_r: te.matmul(
                ps_sum[si][:, :], ones_bf[:], xb_r[:, c, :],
                start=(c == 0), stop=(c == 15)),
                wait=_w(("ln_a", sqk), ("ln_stat2", p.count("ln_stat2"))) if c == 0 else ())
        mm = None
        for c in range(16):
            mm = p.op("tensor", lambda te, c=c, si=si, sq_r=sq_r: te.matmul(
                ps_sq[si][:, :], ones_bf[:], sq_r[:, c, :],
                start=(c == 0), stop=(c == 15)),
                inc=("ln_mm" if c == 15 else None))
        mm_tok[tt] = mm

    def stage_b(tt):
        si = tt % 2
        t0 = tt * 512
        xz_r = xz[si][:].rearrange("p (c t) -> p c t", c=16)
        xb_r = xb[si][:].rearrange("p (c t) -> p c t", c=16)
        sq_r = sq[si][:].rearrange("p (c t) -> p c t", c=16)
        mm = p.count("ln_mm")
        p.op("vector", lambda v, si=si: v.tensor_scalar(
            st[0][:], ps_sum[si][0:1, :], 1.0 / DM, None, OP.mult),
            wait=_w(("ln_mm", mm)))
        p.op("vector", lambda v, si=si: v.tensor_scalar(
            st[1][:], ps_sum[si][0:1, :], -1.0 / DM, None, OP.mult))
        p.op("vector", lambda v: v.tensor_tensor(st[2][:], st[0][:], st[0][:], OP.mult))
        p.op("vector", lambda v, si=si: v.scalar_tensor_tensor(
            st[3][:], ps_sq[si][0:1, :], 1.0 / DM, st[2][:], OP.mult, OP.subtract))
        stat2 = p.op("vector", lambda v: v.tensor_scalar(
            st[2][:], st[3][:], EPS, None, OP.add), inc="ln_stat2")
        p.op("scalar", lambda s: s.activation(st[3][:], st[2][:], AF.Ln),
             wait=_w(("ln_stat2", stat2)))
        sqr = p.op("scalar", lambda s: s.activation(st[2][:], st[3][:], AF.Exp, scale=-0.5),
                   inc="ln_a")
        nmr = p.op("vector", lambda v: v.tensor_tensor(st[1][:], st[1][:], st[2][:], OP.mult),
                   wait=_w(("ln_a", sqr)), inc="ln_v")
        p.op("tensor", lambda te, si=si: te.matmul(
            ps_b[si][:, 0:512], ones32[:], st[2][:], start=True, stop=True),
            wait=_w(("ln_v", nmr), ("ln_apply", apply_tok[si])))
        mm2 = p.op("tensor", lambda te, si=si: te.matmul(
            ps_b[si][:, 512:1024], ones32[:], st[1][:], start=True, stop=True),
            inc="ln_mm")
        for c in range(16):
            dst_r = (dst_sb[:].rearrange("p (c t) -> p c t", c=16)[:, c, t0:t0 + 512]
                     if dst_sb is not None
                     else out_tile[si][:].rearrange("p (c t) -> p c t", c=16)[:, c, :])
            p.op("vector", lambda v, c=c, si=si, xb_r=xb_r, xz_r=xz_r: v.tensor_tensor(
                xb_r[:, c, :], xz_r[:, c, :], ps_b[si][:, 0:512], OP.mult),
                wait=_w(("ln_mm", mm2)) if c == 0 else ())
            if extra_scale_pp is None:
                tk = p.op("vector", lambda v, c=c, si=si, dst_r=dst_r, xb_r=xb_r: v.tensor_tensor(
                    dst_r, xb_r[:, c, :], ps_b[si][:, 512:1024], OP.add),
                    inc=("ln_apply" if c == 15 else None))
            else:
                p.op("vector", lambda v, c=c, si=si, xb_r=xb_r, sq_r=sq_r: v.tensor_tensor(
                    sq_r[:, c, :], xb_r[:, c, :], ps_b[si][:, 512:1024], OP.add))
                tk = p.op("vector", lambda v, c=c, dst_r=dst_r, sq_r=sq_r: v.tensor_scalar(
                    dst_r, sq_r[:, c, :], extra_scale_pp[:, c:c + 1], None, OP.mult),
                    inc=("ln_apply" if c == 15 else None))
            if tk is not None:
                apply_tok[si] = tk
                state["last_apply"] = tk
        if dst_dram is not None:
            src_t = (dst_sb[:].rearrange("p (c t) -> p c t", c=16)[:, :, t0:t0 + 512]
                     if dst_sb is not None
                     else out_tile[si][:].rearrange("p (c t) -> p c t", c=16))
            p.op("scalar", lambda s, t0=t0, src_t=src_t: s.dma_start(
                out=dstv[:, :, t0:t0 + 512], in_=src_t),
                wait=_w(("ln_apply", apply_tok[si])), inc="ln_st", inc_by=16)

    stage_a(0)
    for tt in range(nt):
        if tt + 1 < nt:
            stage_a(tt + 1)
        stage_b(tt)
    if dst_dram is not None:
        p.wait("scalar", "ln_st", p.count("ln_st"))
        return ("ln_st", p.count("ln_st"))
    return ("ln_apply", state["last_apply"])


def wt_stream(p, trk, wT_dram, oc0, nchunk, wT, prev_use_tok, ifc=16,
              eng="gpsimd"):
    wT_r = wT[:].rearrange("p (c o) -> p c o", c=ifc)
    src = wT_dram.rearrange("(c p) o -> p c o", p=128)[:, :, oc0 * 128:(oc0 + nchunk) * 128]
    tok = p.op(eng, lambda g: g.dma_start(out=wT_r[:, :, :], in_=src),
               wait=_w(prev_use_tok), inc="wld", inc_by=16)
    return ("wld", tok)



def build_p1(p, t):
    xn_sb = p.sb("xn_sb", [128, 16 * TOK], BF16)
    with p.scope():
        ln_done = ln_sweep(p, t["x0T"], xn_sb, t["xn"], TOK)
    xn_r = xn_sb[:].rearrange("p (c t) -> p c t", c=16)

    wT = [p.sb(f"wT{i}", [128, 16 * 512], BF16) for i in range(2)]
    outA = [p.sb(f"outA{i}", [128, TOK], BF16) for i in range(2)]
    outB = [p.sb(f"outB{i}", [128, 512], BF16) for i in range(4)]
    ps_A = [p.psb(f"ps_A{i}", [128, 512]) for i in range(4)]
    ps_B = [p.psb(f"ps_B{i}", [128, 512]) for i in range(4)]

    qkv = t["w_qkv"]
    qkfm = t["qk_fm"].rearrange("(c p) t -> p c t", p=128)
    kvtm = t["kv_tm"].rearrange("(c p) f -> p c f", p=128)

    trk = Tracker()
    for g in range(12):
        a_side = g < 8
        b_side = g >= 4
        wtr = wt_stream(p, trk, qkv, 4 * g, 4, wT[g % 2],
                        trk.gate(("wT", g % 2)) or (ln_done if g < 2 else None))
        wb = wT[g % 2]
        if a_side:
            for j in range(4):
                oc = 4 * g + j
                pa = ps_A[j]
                oa = outA[j % 2]
                for tt in range(4):
                    for c in range(16):
                        w0 = ()
                        if c == 0:
                            w0 = _w(wtr if (tt == 0 and j == 0) else None,
                                    ln_done if (g == 0 and tt == 0) else None,
                                    trk.gate(("psA", j)))
                        p.op("tensor", lambda te, j=j, c=c, tt=tt, pa=pa, wb=wb: te.matmul(
                            pa[:, :], wb[:, c * 512 + j * 128:c * 512 + (j + 1) * 128],
                            xn_r[:, c, tt * 512:(tt + 1) * 512],
                            start=(c == 0), stop=(c == 15)),
                            wait=w0, inc=("mmA" if c == 15 else None))
                    last_mm = ("mmA", p.count("mmA"))
                    ev = p.op("scalar", lambda s, pa=pa, oa=oa, tt=tt: s.activation(
                        oa[:, tt * 512:(tt + 1) * 512], pa[:, :], AF.Silu),
                        wait=_w(last_mm, trk.gate(("outA", j % 2)) if tt == 0 else None),
                        inc="evA")
                    trk.set(("psA", j), "evA", ev)
                st = p.op("scalar", lambda s, oc=oc, oa=oa: s.dma_start(
                    out=qkfm[:, oc, :], in_=oa[:, :]),
                    wait=_w(("evA", p.count("evA"))), inc="stA", inc_by=16)
                trk.set(("outA", j % 2), "stA", st)
        if b_side:
            of0 = 4 * g * 128 - 2048
            for tc in range(16):
                pb = ps_B[tc % 4]
                ob = outB[tc % 4]
                for c in range(16):
                    w0 = ()
                    if c == 0:
                        w0 = _w(wtr if tc == 0 else None,
                                trk.gate(("psB", tc % 4)))
                    p.op("tensor", lambda te, c=c, tc=tc, pb=pb, wb=wb: te.matmul(
                        pb[:, :], xn_r[:, c, tc * 128:(tc + 1) * 128],
                        wb[:, c * 512:(c + 1) * 512],
                        start=(c == 0), stop=(c == 15)),
                        wait=w0, inc=("mmB" if c == 15 else None))
                last_mm = ("mmB", p.count("mmB"))
                ev = p.op("scalar", lambda s, pb=pb, ob=ob: s.activation(
                    ob[:, :], pb[:, :], AF.Silu),
                    wait=_w(last_mm, trk.gate(("outB", tc % 4))), inc="evB")
                trk.set(("psB", tc % 4), "evB", ev)
                st = p.op("scalar", lambda s, tc=tc, of0=of0, ob=ob: s.dma_start(
                    out=kvtm[:, tc, of0:of0 + 512], in_=ob[:, :]),
                    wait=_w(("evB", ev)), inc="stB", inc_by=16)
                trk.set(("outB", tc % 4), "stB", st)
        mm_gate = ("mmA", p.count("mmA")) if a_side else ("mmB", p.count("mmB"))
        if a_side and b_side:
            mm_gate = ("mmB", p.count("mmB"))
        trk.set(("wT", g % 2), *mm_gate)

    p.wait("sync", "stA", p.count("stA"))
    p.wait("sync", "stB", p.count("stB"))




def build_p2(p, t):
    ddT = p.sb("ddT", [128, H * 2 * BS], F32)
    qdt = p.sb("qdt", [128, 16 * BS], F32)
    kdt = p.sb("kdt", [128, 2 * DM], F32)
    bdt = p.sb("bdt", [128, 1024], F32)
    q_blk = p.sb("q_blk", [128, 16 * BS], BF16)
    k_blk = p.sb("k_blk", [128, 16 * BS], BF16)
    qqd = p.sb("qqd", [128, 16 * BS], BF16)
    kv_blk = p.sb("kv_blk", [128, 2 * 4096], BF16)
    kkd = p.sb("kkd", [128, 2 * DM], BF16)
    sT = [p.sb(f"sT{i}", [128, 512], BF16) for i in range(3)]
    attn_sb = p.sb("attn_sb", [128, 16 * BS], BF16)
    S_f32 = p.sb("S_f32", [128, 1024], F32)
    S_tmp = p.sb("S_tmp", [128, 1024], F32)
    S_bf = p.sb("S_bf", [128, 1024], BF16)
    ps_aw = [p.psb(f"ps_aw{i}", [128, 512]) for i in range(3)]
    ps_att = [p.psb(f"ps_att{i}", [128, 512]) for i in range(2)]
    ps_S = p.psb("ps_S", [128, 1024])

    qkv_v = t["qk_fm"].rearrange("(c p) t -> p c t", p=128)
    kv_v = t["kv_tm"].rearrange("(c p) f -> p c f", p=128)
    attn_v = t["attn"].rearrange("(c p) t -> p c t", p=128)

    q_r = q_blk[:].rearrange("p (c t) -> p c t", c=16)
    k_r = k_blk[:].rearrange("p (c t) -> p c t", c=16)
    qqd_r = qqd[:].rearrange("p (c t) -> p c t", c=16)
    kv_r = kv_blk[:].rearrange("p (kc f) -> p kc f", kc=2)
    kkd_r = kkd[:].rearrange("p (kc f) -> p kc f", kc=2)
    kdt_r = kdt[:].rearrange("p (kc f) -> p kc f", kc=2)
    at_r = attn_sb[:].rearrange("p (c t) -> p c t", c=16)

    tab = None
    for (dst, src) in ((ddT, t["ddT_tab"]), (qdt, t["qd_tab"]),
                       (kdt, t["kd_tab"].rearrange("p a f -> p (a f)")), (bdt, t["bd_tab"])):
        tab = p.op("sync", lambda s, dst=dst, src=src: s.dma_start(out=dst[:], in_=src),
                   inc="p2_tab", inc_by=16)
    p.op("vector", lambda v: v.memset(S_f32[:], 0.0))
    s_bf_tok = p.op("vector", lambda v: v.memset(S_bf[:], 0.0), inc="p2_sbf")

    blk_done = 0
    sT_pe_tok = {0: 0, 1: 0, 2: 0}
    ev_tok = 0
    st_tok = 0
    for tb in range(NB):
        t0 = tb * BS
        p.op("gpsimd", lambda g, t0=t0: g.dma_start(
            out=q_blk[:], in_=qkv_v[:, 0:16, t0:t0 + BS]),
            wait=_w(("p2_blk", blk_done)), inc="p2_ld", inc_by=16)
        p.op("gpsimd", lambda g, t0=t0: g.dma_start(
            out=k_blk[:], in_=qkv_v[:, 16:32, t0:t0 + BS]), inc="p2_ld", inc_by=16)
        ld = p.op("gpsimd", lambda g, tb=tb: g.dma_start(
            out=kv_blk[:].rearrange("p (kc f) -> p kc f", kc=2),
            in_=kv_v[:, 2 * tb:2 * tb + 2, :]), inc="p2_ld", inc_by=16)
        qq = p.op("vector", lambda v: v.tensor_tensor(qqd[:], q_blk[:], qdt[:], OP.mult),
                  wait=_w(("p2_ld", ld), ("p2_tab", tab)), inc="p2_v")
        kk = None
        for kc in range(2):
            kk = p.op("vector", lambda v, kc=kc: v.tensor_tensor(
                kkd_r[:, kc, 0:DM], kv_r[:, kc, 0:DM], kdt_r[:, kc, :], OP.mult),
                inc="p2_v")
        sT_dve = {0: 0, 1: 0, 2: 0}
        att_ev = {0: ev_tok, 1: ev_tok}
        for h in range(H):
            ch, hb = h // 2, (h % 2) * 64
            bi = h % 3
            stb = sT[bi]
            paw = ps_aw[bi]
            pat = ps_att[(h // 2) % 2]
            for kc in range(2):
                w0 = ()
                if kc == 0:
                    w0 = _w(("p2_ld", ld) if h == 0 else None,
                            ("p2_v", sT_dve[bi]))
                p.op("tensor", lambda te, ch=ch, hb=hb, kc=kc, paw=paw: te.matmul(
                    paw[:, kc * BS:(kc + 1) * BS],
                    k_r[hb:hb + 64, ch, kc * 128:(kc + 1) * 128],
                    q_r[hb:hb + 64, ch, :],
                    start=(kc == 0), stop=(kc == 1)),
                    wait=w0, inc=("p2_mm" if kc == 1 else None))
            aw_tok = p.count("p2_mm")
            sT_dve[bi] = p.op("vector", lambda v, h=h, stb=stb, paw=paw: v.tensor_tensor(
                stb[:], paw[:], ddT[:, h * 512:(h + 1) * 512], OP.mult),
                wait=_w(("p2_mm", aw_tok), ("p2_mm2", sT_pe_tok[bi])), inc="p2_v")
            for kc in range(2):
                w0 = ()
                if kc == 0:
                    w0 = _w(("p2_v", sT_dve[bi]),
                            ("p2_ev", att_ev[(h // 2) % 2]) if h % 2 == 0 else None,
                            ("p2_sbf", s_bf_tok))
                p.op("tensor", lambda te, h=h, hb=hb, kc=kc, stb=stb, pat=pat: te.matmul(
                    pat[hb:hb + 64, 0:BS],
                    kv_r[:, kc, 2048 + h * 64:2048 + (h + 1) * 64],
                    stb[:, kc * BS:(kc + 1) * BS],
                    start=(kc == 0), stop=False),
                    wait=w0)
            sT_pe_tok[bi] = p.op("tensor", lambda te, h=h, hb=hb, ch=ch, pat=pat: te.matmul(
                pat[hb:hb + 64, 0:BS],
                S_bf[hb:hb + 64, ch * 64:(ch + 1) * 64],
                qqd_r[hb:hb + 64, ch, :],
                start=False, stop=True),
                wait=_w(("p2_v", qq)) if h == 0 else (), inc="p2_mm2")
            if h % 2 == 1:
                att_ev[(h // 2) % 2] = ev_tok = p.op("vector", lambda v, ch=ch, pat=pat: v.tensor_copy(
                    at_r[:, ch, :], pat[:, 0:BS]),
                    wait=_w(("p2_mm2", sT_pe_tok[bi]), ("p2_st", st_tok)), inc="p2_ev")
        smm = None
        for h in range(H):
            for kc in range(2):
                w0 = ()
                if h == 0 and kc == 0:
                    w0 = _w(("p2_v", kk), ("p2_supd", p.count("p2_supd")))
                hb2, ch2 = (h % 2) * 64, h // 2
                smm = p.op("tensor", lambda te, h=h, kc=kc, hb2=hb2, ch2=ch2: te.matmul(
                    ps_S[hb2:hb2 + 64, ch2 * 64:(ch2 + 1) * 64],
                    kkd_r[:, kc, h * 64:(h + 1) * 64],
                    kv_r[:, kc, 2048 + h * 64:2048 + (h + 1) * 64],
                    start=(kc == 0), stop=(kc == 1)),
                    wait=w0, inc=("p2_mm" if (h == H - 1 and kc == 1) else None))
        p.op("vector", lambda v: v.tensor_tensor(S_tmp[:], S_f32[:], bdt[:], OP.mult),
             wait=_w(("p2_mm", smm)))
        up = p.op("vector", lambda v: v.tensor_tensor(S_f32[:], S_tmp[:], ps_S[:, :], OP.add),
                  inc="p2_supd")
        s_bf_tok = p.op("vector", lambda v: v.tensor_copy(S_bf[:], S_f32[:]),
                        inc="p2_sbf")
        st_tok = p.op("scalar", lambda s, t0=t0: s.dma_start(
            out=attn_v[:, :, t0:t0 + BS], in_=at_r[:, :, :]),
            wait=_w(("p2_ev", ev_tok)), inc="p2_st", inc_by=16)
        blk_done = p.op("vector", lambda v: v.engine_nop(),
                        wait=_w(("p2_mm", p.count("p2_mm")),
                                ("p2_mm2", p.count("p2_mm2")),
                                ("p2_sbf", s_bf_tok),
                                ("p2_st", st_tok)),
                        inc="p2_blk")
    p.wait("sync", "p2_st", p.count("p2_st"))




def build_p3(p, t):
    at = p.sb("at", [128, 16 * 512], BF16)
    sq = p.sb("sq", [128, 16 * 512], BF16)
    ones_bf = p.sb("ones_bf", [128, 1], BF16)
    ones32 = p.sb("ones32", [1, 128], F32)
    st0 = p.sb("st0", [1, 512], F32)
    st1 = p.sb("st1", [1, 512], F32)
    nw = p.sb("nw", [128, 16], F32)
    xn_sb = p.sb("xn_sb", [128, 16 * TOK], BF16)
    G = p.sb("G", [128, 16 * TOK], BF16)
    gt = p.sb("gt", [128, 512], BF16)
    wT = [p.sb(f"wT{i}", [128, 16 * 512], BF16) for i in range(2)]
    x0t = [p.sb(f"x0t{i}", [128, 512], F32) for i in range(2)]
    x1t = [p.sb(f"x1t{i}", [128, 512], F32) for i in range(2)]
    ps_sq = p.psb("ps_sq", [1, 512])
    ps_b = p.psb("ps_b", [128, 512])
    ps = [p.psb(f"ps{i}", [128, 512]) for i in range(4)]

    at_r = at[:].rearrange("p (c t) -> p c t", c=16)
    sq_r = sq[:].rearrange("p (c t) -> p c t", c=16)
    xn_r = xn_sb[:].rearrange("p (c t) -> p c t", c=16)
    G_r = G[:].rearrange("p (c t) -> p c t", c=16)
    attn_v = t["attn"].rearrange("(c p) t -> p c t", p=128)
    xn_v = t["xn"].rearrange("(c p) t -> p c t", p=128)
    x0_v = t["x0T"].rearrange("(c p) t -> p c t", p=128)
    x1_v = t["x1"].rearrange("(c p) t -> p c t", p=128)

    p.op("vector", lambda v: v.memset(ones_bf[:], 1.0))
    ini = p.op("vector", lambda v: v.memset(ones32[:], 1.0), inc="p3_init")
    nwt = p.op("sync", lambda s: s.dma_start(out=nw[:], in_=t["nw_pp"]), inc="p3_tab", inc_by=16)
    xnld = p.op("sync", lambda s: s.dma_start(out=xn_sb[:].rearrange("p (c t) -> p c t", c=16),
                                              in_=xn_v[:, :, :]), inc="p3_tab", inc_by=16)

    app_tok = 0
    for tt in range(4):
        t0 = tt * 512
        ld = p.op("gpsimd", lambda g, t0=t0: g.dma_start(
            out=at[:], in_=attn_v[:, :, t0:t0 + 512]),
            wait=_w(("p3_app", app_tok)), inc="p3_ld", inc_by=16)
        sqk = p.op("scalar", lambda s: s.activation(sq[:], at[:], AF.Square),
                   wait=_w(("p3_ld", ld)), inc="p3_a")
        mm = None
        for c in range(16):
            mm = p.op("tensor", lambda te, c=c: te.matmul(
                ps_sq[:, :], ones_bf[:], sq_r[:, c, :],
                start=(c == 0), stop=(c == 15)),
                wait=_w(("p3_a", sqk), ("p3_st2", p.count("p3_st2")), ("p3_init", ini)) if c == 0 else (),
                inc=("p3_mm" if c == 15 else None))
        v1 = p.op("vector", lambda v: v.tensor_scalar(
            st0[:], ps_sq[0:1, :], 1.0 / DM, EPS, OP.mult, OP.add),
            wait=_w(("p3_mm", mm)), inc="p3_v")
        a1 = p.op("scalar", lambda s: s.activation(st1[:], st0[:], AF.Sqrt),
                  wait=_w(("p3_v", v1)), inc="p3_a")
        st2 = p.op("vector", lambda v: v.reciprocal(st0[:], st1[:]),
                   wait=_w(("p3_a", a1)), inc="p3_st2")
        bmm = p.op("tensor", lambda te: te.matmul(ps_b[:, :], ones32[:], st0[:],
                                                  start=True, stop=True),
                   wait=_w(("p3_st2", st2), ("p3_app", app_tok)), inc="p3_mm")
        for c in range(16):
            tk = p.op("vector", lambda v, c=c, t0=t0: v.scalar_tensor_tensor(
                G_r[:, c, t0:t0 + 512], at_r[:, c, :], nw[:, c:c + 1], ps_b[:, :],
                OP.mult, OP.mult),
                wait=_w(("p3_mm", bmm), ("p3_tab", nwt)) if c == 0 else (),
                inc=("p3_app" if c == 15 else None))
            if tk is not None:
                app_tok = tk

    trk = Tracker()
    gdone = 0
    for g in range(4):
        wtr = wt_stream(p, trk, t["w_gate"], 4 * g, 4, wT[g % 2],
                        trk.gate(("wT", g % 2)))
        wb = wT[g % 2]
        for j in range(4):
            oc = 4 * g + j
            pa = ps[j]
            for tt in range(4):
                for c in range(16):
                    w0 = ()
                    if c == 0:
                        w0 = _w(wtr if (tt == 0 and j == 0) else None,
                                ("p3_tab", xnld) if (g == 0 and j == 0 and tt == 0) else None,
                                trk.gate(("ps", j)))
                    p.op("tensor", lambda te, j=j, c=c, tt=tt, pa=pa, wb=wb: te.matmul(
                        pa[:, :], wb[:, c * 512 + j * 128:c * 512 + (j + 1) * 128],
                        xn_r[:, c, tt * 512:(tt + 1) * 512],
                        start=(c == 0), stop=(c == 15)),
                        wait=w0, inc=("p3_mmg" if c == 15 else None))
                ev = p.op("scalar", lambda s, pa=pa: s.activation(gt[:], pa[:, :], AF.Sigmoid),
                          wait=_w(("p3_mmg", p.count("p3_mmg")), ("p3_gg", p.count("p3_gg")),
                                  ("p3_app", app_tok) if (g == 0 and j == 0 and tt == 0) else None),
                          inc="p3_ev")
                gdone = p.op("vector", lambda v, oc=oc, tt=tt: v.tensor_tensor(
                    G_r[:, oc, tt * 512:(tt + 1) * 512], gt[:],
                    G_r[:, oc, tt * 512:(tt + 1) * 512], OP.mult),
                    wait=_w(("p3_ev", ev)), inc="p3_gg")
                trk.set(("ps", j), "p3_gg", gdone)
        trk.set(("wT", g % 2), "p3_mmg", p.count("p3_mmg"))

    gg_all = ("p3_gg", p.count("p3_gg"))
    for g in range(4):
        wtr = wt_stream(p, trk, t["w_out0"], 4 * g, 4, wT[g % 2],
                        trk.gate(("wT", g % 2)))
        wb = wT[g % 2]
        for j in range(4):
            oc = 4 * g + j
            pa = ps[j]
            for tt in range(4):
                xl = p.op("gpsimd", lambda gp, oc=oc, tt=tt, j=j: gp.dma_start(
                    out=x0t[tt % 2][:], in_=x0_v[:, oc, tt * 512:(tt + 1) * 512]),
                    wait=_w(trk.gate(("x0t", tt % 2))), inc="p3_xld", inc_by=16)
                for c in range(16):
                    w0 = ()
                    if c == 0:
                        w0 = _w(wtr if (tt == 0 and j == 0) else None,
                                gg_all if (g == 0 and j == 0 and tt == 0) else None,
                                trk.gate(("ps", j)))
                    p.op("tensor", lambda te, j=j, c=c, tt=tt, pa=pa, wb=wb: te.matmul(
                        pa[:, :], wb[:, c * 512 + j * 128:c * 512 + (j + 1) * 128],
                        G_r[:, c, tt * 512:(tt + 1) * 512],
                        start=(c == 0), stop=(c == 15)),
                        wait=w0, inc=("p3_mmo" if c == 15 else None))
                ev = p.op("vector", lambda v, pa=pa, tt=tt: v.tensor_tensor(
                    x1t[tt % 2][:], pa[:, :], x0t[tt % 2][:], OP.add),
                    wait=_w(("p3_mmo", p.count("p3_mmo")), ("p3_xld", xl),
                            trk.gate(("x1t", tt % 2))), inc="p3_evo")
                trk.set(("ps", j), "p3_evo", ev)
                stt = p.op("scalar", lambda s, oc=oc, tt=tt: s.dma_start(
                    out=x1_v[:, oc, tt * 512:(tt + 1) * 512], in_=x1t[tt % 2][:]),
                    wait=_w(("p3_evo", ev)), inc="p3_sto", inc_by=16)
                trk.set(("x1t", tt % 2), "p3_sto", stt)
                trk.set(("x0t", tt % 2), "p3_evo", ev)
        trk.set(("wT", g % 2), "p3_mmo", p.count("p3_mmo"))

    p.wait("sync", "p3_sto", p.count("p3_sto"))




def _bfly(p, ping, pong, nch, span, first_wait, sems=("p4_cp", "p4_g"),
          final_nxt=None, final_wait=None):
    import math
    nst = int(math.log2(nch))
    v_tok, g_tok = first_wait
    bufs = [ping, pong]
    for s in range(nst):
        dist = nch >> (s + 1)
        cur = bufs[s % 2][:].rearrange("p (g two d) -> p g two d", two=2, d=dist * span)
        if s == nst - 1 and final_nxt is not None:
            nxt = final_nxt
        else:
            nxt = bufs[(s + 1) % 2][:].rearrange("p (g two d) -> p g two d", two=2, d=dist * span)
        a, b = cur[:, :, 0, :], cur[:, :, 1, :]
        w = _w((sems[0], v_tok), (sems[1], g_tok))
        if s == nst - 1 and final_wait is not None:
            w = w + _w(final_wait)
        v_tok = p.op("vector", lambda v, a=a, b=b, nxt=nxt: v.tensor_tensor(
            nxt[:, :, 0, :], a, b, OP.add), wait=w, inc=sems[0])
        v_tok = p.op("vector", lambda v, a=a, b=b, nxt=nxt: v.tensor_tensor(
            nxt[:, :, 1, :], a, b, OP.subtract), inc=sems[0])
    return nst % 2, v_tok, g_tok


def build_p4a(p, t):
    xsu = p.sb("xsu", [128, 16 * TOK], BF16)
    su = p.sb("su", [128, 16], F32)
    sut = p.op("sync", lambda s: s.dma_start(out=su[:], in_=t["su_ug_pp"]), inc="p4_tab", inc_by=16)
    p.wait("vector", "p4_tab", sut)
    with p.scope():
        ln_done = ln_sweep(p, t["x1"], xsu, None, TOK, extra_scale_pp=su)
    h128 = p.sb("h128s", [128, 128], BF16)
    h1t = p.op("sync", lambda s: s.dma_start(out=h128[:], in_=t["h128"]), wait=_w(ln_done),
               inc="p4_tab", inc_by=16)
    ping = p.sb("ping", [128, 16 * 512], BF16)
    pong = p.sb("pong", [128, 16 * 512], BF16)
    ps = [p.psb(f"ps{i}", [128, 512]) for i in range(4)]
    xsu_r = xsu[:].rearrange("p (c t) -> p c t", c=16)
    ping_r = ping[:].rearrange("p (c t) -> p c t", c=16)
    pong_r = pong[:].rearrange("p (c t) -> p c t", c=16)
    x1h_v = t["x1h"].rearrange("(c p) t -> p c t", p=128)

    st_tok = 0
    for tt in range(4):
        t0 = tt * 512
        cp = None
        for c in range(16):
            w0 = ()
            if c == 0:
                w0 = _w(ln_done, ("p4_tab", h1t), ("p4_st", st_tok))
            else:
                w0 = _w(("p4_cp", max(0, p.count("p4_cp") - 3)),)
            p.op("tensor", lambda te, c=c, t0=t0: te.matmul(
                ps[c % 4][:, :], h128[:], xsu_r[:, c, t0:t0 + 512],
                start=True, stop=True), wait=w0, inc="p4_mm")
            cp = p.op("vector", lambda v, c=c: v.tensor_copy(ping_r[:, c, :], ps[c % 4][:, :]),
                      wait=_w(("p4_mm", p.count("p4_mm"))), inc="p4_cp")
        fin, vt, gt = _bfly(p, ping, pong, 16, 512, (cp, p.count("p4_g")))
        assert fin == 0
        st_tok = p.op("scalar", lambda s, t0=t0: s.dma_start(
            out=x1h_v[:, :, t0:t0 + 512], in_=ping_r[:, :, :]),
            wait=_w(("p4_cp", vt), ("p4_g", gt)), inc="p4_st", inc_by=16)
    p.wait("sync", "p4_st", p.count("p4_st"))



def build_p4b(p, t):
    x1h_sb = p.sb("x1h_sb", [128, 16 * TOK], BF16)
    wsu = p.sb("wsu", [128, 64], F32)
    wsg = p.sb("wsg", [128, 64], F32)
    sud = p.sb("sud", [128, 64], F32)
    wT_u = [p.sb(f"wT_u{i}", [128, 16 * 512], BF16) for i in range(2)]
    wT_g = [p.sb(f"wT_g{i}", [128, 16 * 512], BF16) for i in range(2)]
    gt = p.sb("gt", [128, 512], BF16)
    ut = p.sb("ut", [128, 512], BF16)
    ht = [p.sb(f"ht{i}", [128, 512], BF16) for i in range(2)]
    ps_u = [p.psb(f"ps_u{i}", [128, 512]) for i in range(4)]
    ps_g = [p.psb(f"ps_g{i}", [128, 512]) for i in range(4)]

    x1h_v = t["x1h"].rearrange("(c p) t -> p c t", p=128)
    h_v = t["hbuf"].rearrange("(c p) t -> p c t", p=128)
    x1h_r = x1h_sb[:].rearrange("p (c t) -> p c t", c=16)

    for nm, dst in (("wsu_pp", wsu), ("wsg_pp", wsg), ("su_dp_pp", sud)):
        p.op("sync", lambda s, nm=nm, dst=dst: s.dma_start(out=dst[:], in_=t[nm]),
             inc="p4_tab", inc_by=16)
    ldx = p.op("sync", lambda s: s.dma_start(
        out=x1h_sb[:].rearrange("p (c t) -> p c t", c=16), in_=x1h_v[:, :, :]),
        inc="p4_tab", inc_by=16)
    tab4b = ldx
    rs = float(1.0 / np.sqrt(2048.0))
    p.op("vector", lambda v: v.tensor_tensor(wsu[:], wsu[:], sud[:], OP.mult),
         wait=_w(("p4_tab", tab4b)))
    p.op("vector", lambda v: v.tensor_scalar(wsu[:], wsu[:], rs, None, OP.mult))
    pr = p.op("vector", lambda v: v.tensor_scalar(wsg[:], wsg[:], rs, None, OP.mult),
              inc="p4_pr")

    trk = Tracker()
    for g in range(16):
        wtr_u = wt_stream(p, trk, t["w_up"], 4 * g, 4, wT_u[g % 2],
                          trk.gate(("wTu", g % 2)))
        wtr_g = wt_stream(p, trk, t["w_wgate"], 4 * g, 4, wT_g[g % 2],
                          trk.gate(("wTg", g % 2)))
        wu, wg = wT_u[g % 2], wT_g[g % 2]
        for j in range(4):
            oc = 4 * g + j
            pu, pg = ps_u[j], ps_g[j]
            for tt in range(4):
                for c in range(16):
                    w0 = ()
                    if c == 0:
                        w0 = _w(wtr_g if (tt == 0 and j == 0) else None,
                                ("p4_pr", pr) if (g == 0 and j == 0 and tt == 0) else None,
                                trk.gate(("ps", j)))
                    p.op("tensor", lambda te, j=j, c=c, tt=tt, pu=pu, wu=wu: te.matmul(
                        pu[:, :], wu[:, c * 512 + j * 128:c * 512 + (j + 1) * 128],
                        x1h_r[:, c, tt * 512:(tt + 1) * 512],
                        start=(c == 0), stop=(c == 15)), wait=w0)
                for c in range(16):
                    p.op("tensor", lambda te, j=j, c=c, tt=tt, pg=pg, wg=wg: te.matmul(
                        pg[:, :], wg[:, c * 512 + j * 128:c * 512 + (j + 1) * 128],
                        x1h_r[:, c, tt * 512:(tt + 1) * 512],
                        start=(c == 0), stop=(c == 15)),
                        inc=("p4_mm" if c == 15 else None))
                mmt = p.count("p4_mm")
                e1 = p.op("scalar", lambda s, pg=pg, oc=oc: s.activation(
                    gt[:], pg[:, :], AF.Silu, scale=wsg[:, oc:oc + 1]),
                    wait=_w(("p4_mm", mmt), ("p4_h", p.count("p4_h"))), inc="p4_ev")
                e2 = p.op("vector", lambda v, pu=pu, oc=oc: v.tensor_scalar(
                    ut[:], pu[:, :], wsu[:, oc:oc + 1], None, OP.mult),
                    wait=_w(("p4_mm", mmt), ("p4_h", p.count("p4_h"))), inc="p4_ev")
                hv = p.op("vector", lambda v, tt=tt: v.tensor_tensor(
                    ht[tt % 2][:], gt[:], ut[:], OP.mult),
                    wait=_w(("p4_ev", p.count("p4_ev")), trk.gate(("ht", tt % 2))),
                    inc="p4_h")
                trk.set(("ps", j), "p4_h", hv)
                stt = p.op("scalar", lambda s, oc=oc, tt=tt: s.dma_start(
                    out=h_v[:, oc, tt * 512:(tt + 1) * 512], in_=ht[tt % 2][:]),
                    wait=_w(("p4_h", hv)), inc="p4_sth", inc_by=16)
                trk.set(("ht", tt % 2), "p4_sth", stt)
        mm_gate = ("p4_mm", p.count("p4_mm"))
        trk.set(("wTu", g % 2), *mm_gate)
        trk.set(("wTg", g % 2), *mm_gate)
    p.wait("sync", "p4_sth", p.count("p4_sth"))



def build_p4c(p, t):
    h128 = p.sb("h128s", [128, 128], BF16)
    ping = p.sb("ping", [128, 64 * 512], BF16)
    pong = p.sb("pong", [128, 64 * 512], BF16)
    ps = [p.psb(f"ps{i}", [128, 512]) for i in range(4)]
    h1t = p.op("sync", lambda s: s.dma_start(out=h128[:], in_=t["h128"]), inc="p4_tab", inc_by=16)
    h_v = t["hbuf"].rearrange("(c p) t -> p c t", p=128)
    hp_v = t["hpbuf"].rearrange("(c p) t -> p c t", p=128)
    ping_r = ping[:].rearrange("p (c t) -> p c t", c=64)
    pong_r = pong[:].rearrange("p (c t) -> p c t", c=64)

    st_tok = 0
    for tt in range(4):
        t0 = tt * 512
        ld = p.op("gpsimd", lambda g, t0=t0: g.dma_start(
            out=ping_r[:, :, :], in_=h_v[:, :, t0:t0 + 512]),
            wait=_w(("p4_st", st_tok)), inc="p4_ld", inc_by=16)
        cp = None
        for c in range(64):
            w0 = _w(("p4_ld", ld), ("p4_tab", h1t)) if c == 0 else _w(("p4_cp", max(0, p.count("p4_cp") - 3)),)
            p.op("tensor", lambda te, c=c: te.matmul(
                ps[c % 4][:, :], h128[:], ping_r[:, c, :],
                start=True, stop=True), wait=w0, inc="p4_mm")
            cp = p.op("vector", lambda v, c=c: v.tensor_copy(ping_r[:, c, :], ps[c % 4][:, :]),
                      wait=_w(("p4_mm", p.count("p4_mm"))), inc="p4_cp")
        fin, vt, gt = _bfly(p, ping, pong, 64, 512, (cp, p.count("p4_g")))
        assert fin == 0
        st_tok = p.op("scalar", lambda s, t0=t0: s.dma_start(
            out=hp_v[:, :, t0:t0 + 512], in_=ping_r[:, :, :]),
            wait=_w(("p4_cp", vt), ("p4_g", gt)), inc="p4_st", inc_by=16)
    p.wait("sync", "p4_st", p.count("p4_st"))



def build_p4d(p, t):
    wsd = p.sb("wsd", [128, 16], F32)
    hp_sb = p.sb("hp_sb", [128, 64 * 1024], BF16)
    wT = [p.sb(f"wT{i}", [128, 64 * 128], BF16) for i in range(2)]
    x1t = [p.sb(f"x1t{i}", [128, 512], F32) for i in range(2)]
    x2t = [p.sb(f"x2t{i}", [128, 512], F32) for i in range(2)]
    ps = [p.psb(f"ps{i}", [128, 512]) for i in range(4)]

    hp_v = t["hpbuf"].rearrange("(c p) t -> p c t", p=128)
    x1_v = t["x1"].rearrange("(c p) t -> p c t", p=128)
    x2_v = t["x2"].rearrange("(c p) t -> p c t", p=128)
    hp_r = hp_sb[:].rearrange("p (c t) -> p c t", c=64)

    wsdt = p.op("sync", lambda s: s.dma_start(out=wsd[:], in_=t["wsd_pp"]), inc="p4_tab", inc_by=16)
    rs = float(1.0 / np.sqrt(8192.0))
    pr = p.op("vector", lambda v: v.tensor_scalar(wsd[:], wsd[:], rs, None, OP.mult),
              wait=_w(("p4_tab", wsdt)), inc="p4_pr")

    trk = Tracker()
    for sup in range(2):
        s0 = sup * 1024
        ldh = p.op("gpsimd", lambda g, s0=s0: g.dma_start(
            out=hp_r[:, :, :], in_=hp_v[:, :, s0:s0 + 1024]),
            wait=_w(trk.gate("hp")), inc="p4_ld", inc_by=16)
        for oc in range(16):
            wtr = wt_stream(p, trk, t["w_down"], oc, 1, wT[oc % 2],
                            trk.gate(("wT", oc % 2)), ifc=64)
            wb = wT[oc % 2]
            for sp in range(2):
                pi = (oc * 2 + sp) % 4
                for c in range(64):
                    w0 = ()
                    if c == 0:
                        w0 = _w(wtr if sp == 0 else None, ("p4_ld", ldh) if sp == 0 else None,
                                ("p4_pr", pr), trk.gate(("ps", pi)))
                    p.op("tensor", lambda te, c=c, sp=sp, pa=ps[pi], wb=wb: te.matmul(
                        pa[:, :], wb[:, c * 128:(c + 1) * 128],
                        hp_r[:, c, sp * 512:(sp + 1) * 512],
                        start=(c == 0), stop=(c == 63)),
                        wait=w0, inc=("p4_mm" if c == 63 else None))
                xl = p.op("gpsimd", lambda g, oc=oc, sp=sp, s0=s0: g.dma_start(
                    out=x1t[sp][:], in_=x1_v[:, oc, s0 + sp * 512:s0 + (sp + 1) * 512]),
                    wait=_w(trk.gate(("x1t", sp))), inc="p4_xld", inc_by=16)
                ev = p.op("vector", lambda v, oc=oc, sp=sp, pi=pi: v.scalar_tensor_tensor(
                    x2t[sp][:], ps[pi][:, :], wsd[:, oc:oc + 1], x1t[sp][:],
                    OP.mult, OP.add),
                    wait=_w(("p4_mm", p.count("p4_mm")), ("p4_xld", xl),
                            trk.gate(("x2t", sp))), inc="p4_ev")
                trk.set(("ps", pi), "p4_ev", ev)
                trk.set(("x1t", sp), "p4_ev", ev)
                stt = p.op("scalar", lambda s, oc=oc, sp=sp, s0=s0: s.dma_start(
                    out=x2_v[:, oc, s0 + sp * 512:s0 + (sp + 1) * 512], in_=x2t[sp][:]),
                    wait=_w(("p4_ev", ev)), inc="p4_stx", inc_by=16)
                trk.set(("x2t", sp), "p4_stx", stt)
            trk.set(("wT", oc % 2), "p4_mm", p.count("p4_mm"))
        trk.set("hp", "p4_mm", p.count("p4_mm"))
    p.wait("sync", "p4_stx", p.count("p4_stx"))




def plan_par(p, engs):
    store = {}
    for eng in engs:
        def f(e, eng=eng):
            pid = e.partition_id()
            store[eng] = e.snap(pid % 2, min_val=0, max_val=1)
            return e.nop()
        p.op(eng, f)
    return store


def _half_view(dram_ap, p128=128):
    return dram_ap.rearrange("(c p) (h u) -> h p c u", p=p128, h=2)




def build_p4cd(p, t):
    h128 = p.sb("h128s", [128, 128], BF16)
    ping = p.sb("ping", [128, 64 * 128], BF16)
    pong = p.sb("pong", [128, 64 * 128], BF16)
    hpO = p.sb("hpO", [128, 64 * 1024], BF16)
    wsd = p.sb("wsd", [128, 16], F32)
    wT = [p.sb(f"wT{i}", [128, 64 * 128], BF16) for i in range(2)]
    x1t = [p.sb(f"x1t{i}", [128, 512], F32) for i in range(2)]
    x2t = [p.sb(f"x2t{i}", [128, 512], F32) for i in range(2)]
    ps_c = [p.psb(f"ps_c{i}", [128, 512]) for i in range(2)]
    ps_d = [p.psb(f"ps_d{i}", [128, 512]) for i in range(4)]

    h_v = t["hbuf"].rearrange("(c p) t -> p c t", p=128)
    x1_v = t["x1"].rearrange("(c p) t -> p c t", p=128)
    x2_v = t["x2"].rearrange("(c p) t -> p c t", p=128)
    ping_r = ping[:].rearrange("p (c t) -> p c t", c=64)
    hpO_r = hpO[:].rearrange("p (c t) -> p c t", c=64)
    hpO_b = hpO[:].rearrange("p (g two u) -> p g two u", two=2, u=1024)

    h1t = p.op("sync", lambda s: s.dma_start(out=h128[:], in_=t["h128"]), inc="p4_tab", inc_by=16)
    wsdt = p.op("sync", lambda s: s.dma_start(out=wsd[:], in_=t["wsd_pp"]), inc="p4_tab", inc_by=16)
    rs = float(1.0 / np.sqrt(8192.0))
    pr = p.op("vector", lambda v: v.tensor_scalar(wsd[:], wsd[:], rs, None, OP.mult),
              wait=_w(("p4_tab", wsdt)), inc="p4_pr")

    trk = Tracker()
    st_tok = 0

    def c_tile(j, sup):
        t0 = sup * 1024 + j * 128
        ld = p.op("gpsimd", lambda g, t0=t0: g.dma_start(
            out=ping_r[:, :, :], in_=h_v[:, :, t0:t0 + 128]),
            wait=_w(trk.gate("pp")), inc="p4_ld", inc_by=16)
        cp = None
        for c in range(64):
            w0 = _w(("p4_ld", ld), ("p4_tab", h1t)) if c == 0 else _w(("p4_cp", max(0, p.count("p4_cp") - 1)))
            p.op("tensor", lambda te, c=c: te.matmul(
                ps_c[c % 2][:, 0:128], h128[:], ping_r[:, c, :],
                start=True, stop=True), wait=w0, inc="p4_mm")
            cp = p.op("vector", lambda v, c=c: v.tensor_copy(ping_r[:, c, :], ps_c[c % 2][:, 0:128]),
                      wait=_w(("p4_mm", p.count("p4_mm"))), inc="p4_cp")
        fin, vt, gt = _bfly(p, ping, pong, 64, 128, (cp, p.count("p4_g")),
                            final_nxt=hpO_b[:, :, :, j * 128:(j + 1) * 128],
                            final_wait=trk.gate("hpO"))
        trk.set("pp", "p4_cp", vt)
        return vt

    def d_super(sup, hp_tok):
        s0 = sup * 1024
        last_ev = None
        for oc in range(16):
            wtr = wt_stream(p, trk, t["w_down"], oc, 1, wT[oc % 2],
                            trk.gate(("wT", oc % 2)), ifc=64)
            wb = wT[oc % 2]
            for sp in range(2):
                pi = (oc * 2 + sp) % 4
                for c in range(64):
                    w0 = ()
                    if c == 0:
                        w0 = _w(wtr if sp == 0 else None,
                                ("p4_cp", hp_tok) if (oc == 0 and sp == 0) else None,
                                ("p4_pr", pr), trk.gate(("ps", pi)))
                    p.op("tensor", lambda te, c=c, sp=sp, pi=pi, wb=wb: te.matmul(
                        ps_d[pi][:, :], wb[:, c * 128:(c + 1) * 128],
                        hpO_r[:, c, sp * 512:(sp + 1) * 512],
                        start=(c == 0), stop=(c == 63)),
                        wait=w0, inc=("p4_mm2" if c == 63 else None))
                xl = p.op("gpsimd", lambda g, oc=oc, sp=sp, s0=s0: g.dma_start(
                    out=x1t[sp][:], in_=x1_v[:, oc, s0 + sp * 512:s0 + (sp + 1) * 512]),
                    wait=_w(trk.gate(("x1t", sp))), inc="p4_xld", inc_by=16)
                ev = p.op("vector", lambda v, oc=oc, sp=sp, pi=pi: v.scalar_tensor_tensor(
                    x2t[sp][:], ps_d[pi][:, :], wsd[:, oc:oc + 1], x1t[sp][:],
                    OP.mult, OP.add),
                    wait=_w(("p4_mm2", p.count("p4_mm2")), ("p4_xld", xl),
                            trk.gate(("x2t", sp))), inc="p4_ev")
                trk.set(("ps", pi), "p4_ev", ev)
                trk.set(("x1t", sp), "p4_ev", ev)
                stt = p.op("scalar", lambda s, oc=oc, sp=sp, s0=s0: s.dma_start(
                    out=x2_v[:, oc, s0 + sp * 512:s0 + (sp + 1) * 512], in_=x2t[sp][:]),
                    wait=_w(("p4_ev", ev)), inc="p4_stx", inc_by=16)
                trk.set(("x2t", sp), "p4_stx", stt)
            trk.set(("wT", oc % 2), "p4_mm2", p.count("p4_mm2"))
        trk.set("hpO", "p4_mm2", p.count("p4_mm2"))

    toks0 = [c_tile(j, 0) for j in range(8)]
    d_super(0, toks0[-1])
    toks1 = [c_tile(j, 1) for j in range(8)]
    d_super(1, toks1[-1])
    p.wait("sync", "p4_stx", p.count("p4_stx"))



def build_p5(p, t):
    xn3_sb = p.sb("xn3_sb", [128, 16 * TOK], BF16)
    with p.scope():
        ln_done = ln_sweep(p, t["x2"], xn3_sb, t["xn3"], TOK)
    xn3_r = xn3_sb[:].rearrange("p (c t) -> p c t", c=16)

    wv = [p.sb(f"wv{i}", [128, 16 * 512], BF16) for i in range(2)]
    ones1 = p.sb("ones1", [1, 128], BF16)
    bias_bf = p.sb("bias_bf", [1, 512], BF16)
    bias_f32 = p.sb("bias_f32", [1, 512], F32)
    vt = [p.sb(f"vt{i}", [128, 512], BF16) for i in range(2)]
    wqk = p.sb("wqk", [128, 16 * 128], BF16)
    bqk = p.sb("bqk", [128, 1], F32)
    gq = p.sb("gq", [128, 1], F32)
    bq = p.sb("bq", [128, 1], F32)
    gk = p.sb("gk", [128, 1], F32)
    bk = p.sb("bk", [128, 1], F32)
    qkt = p.sb("qkt", [128, 512], BF16)
    q3s = p.sb("q3s", [128, TOK], BF16)
    k3s = p.sb("k3s", [128, TOK], BF16)
    ps = [p.psb(f"ps{i}", [128, 512]) for i in range(4)]

    v3_v = t["v3"].rearrange("(c p) f -> p c f", p=128)
    trk = Tracker()

    init5 = p.op("vector", lambda v: v.memset(ones1[:], 1.0), inc="p5_ini")
    tab5 = None
    for nm, dst in (("b_qk_pp", bqk), ("gq_pp", gq), ("bq_pp", bq), ("gk_pp", gk), ("bk_pp", bk)):
        tab5 = p.op("sync", lambda s, nm=nm, dst=dst: s.dma_start(out=dst[:], in_=t[nm]),
             wait=_w(ln_done), inc="p5_tab", inc_by=16)

    wqk_tok = p.op("gpsimd", lambda g: g.dma_start(
        out=wqk[:].rearrange("p (c o) -> p c o", c=16),
        in_=t["w_qk"].rearrange("(c p) o -> p c o", p=128)),
        wait=_w(ln_done), inc="p5_wld", inc_by=16)
    for tt in range(4):
        mm = None
        for c in range(16):
            w0 = _w(("p5_wld", wqk_tok), ln_done, ("p5_tab", tab5),
                    ("p5_qaff", p.count("p5_qaff"))) if c == 0 else ()
            mm = p.op("tensor", lambda te, c=c, tt=tt: te.matmul(
                ps[0][:, :], wqk[:, c * 128:(c + 1) * 128],
                xn3_r[:, c, tt * 512:(tt + 1) * 512],
                start=(c == 0), stop=(c == 15)),
                wait=w0, inc=("p5_mm" if c == 15 else None))
        sl = p.op("scalar", lambda s: s.activation(qkt[:], ps[0][:, :], AF.Silu, bias=bqk[:, 0:1]),
                  wait=_w(("p5_mm", mm)), inc="p5_ev")
        p.op("vector", lambda v, tt=tt: v.tensor_scalar(
            q3s[:, tt * 512:(tt + 1) * 512], qkt[:], gq[:, 0:1], bq[:, 0:1], OP.mult, OP.add),
            wait=_w(("p5_ev", sl)))
        p.op("vector", lambda v, tt=tt: v.tensor_scalar(
            k3s[:, tt * 512:(tt + 1) * 512], qkt[:], gk[:, 0:1], bk[:, 0:1], OP.mult, OP.add),
            inc="p5_qaff")
    qa = p.count("p5_qaff")
    p.op("scalar", lambda s: s.dma_start(out=t["q3"], in_=q3s[:]),
         wait=_w(("p5_qaff", qa)), inc="p5_st", inc_by=16)
    p.op("scalar", lambda s: s.dma_start(out=t["k3"], in_=k3s[:]),
         wait=_w(("p5_qaff", qa)), inc="p5_st", inc_by=16)

    for ofs in range(8):
        wb = wv[ofs % 2]
        wld = p.op("gpsimd", lambda g, ofs=ofs, wb=wb: g.dma_start(
            out=wb[:].rearrange("p (c o) -> p c o", c=16),
            in_=t["w_hid"].rearrange("(c p) o -> p c o", p=128)[:, :, ofs * 512:(ofs + 1) * 512]),
            wait=_w(trk.gate(("wv", ofs % 2)) or (ln_done if ofs < 2 else None)),
            inc="p5_wld", inc_by=16)
        bl = p.op("gpsimd", lambda g, ofs=ofs: g.dma_start(
            out=bias_f32[:], in_=t["b_hid_row"][0:1, ofs * 512:(ofs + 1) * 512]),
            wait=_w(("p5_bc", p.count("p5_bc"))), inc="p5_wld", inc_by=16)
        bc = p.op("vector", lambda v: v.tensor_copy(bias_bf[:], bias_f32[:]),
                  wait=_w(("p5_wld", p.count("p5_wld")), ("p5_ini", init5),
                          ("p5_vmm", p.count("p5_vmm"))), inc="p5_bc")
        for tc in range(16):
            pa = ps[tc % 4]
            for c in range(16):
                w0 = ()
                if c == 0:
                    w0 = _w(("p5_wld", p.count("p5_wld")), ln_done if ofs == 0 else None,
                            ("p5_bc", bc), trk.gate(("psv", tc % 4)))
                p.op("tensor", lambda te, c=c, tc=tc, pa=pa, wb=wb: te.matmul(
                    pa[:, :], xn3_r[:, c, tc * 128:(tc + 1) * 128],
                    wb[:, c * 512:(c + 1) * 512],
                    start=(c == 0), stop=False), wait=w0)
            mm = p.op("tensor", lambda te, pa=pa: te.matmul(
                pa[:, :], ones1[:], bias_bf[:], start=False, stop=True),
                inc="p5_vmm")
            ev = p.op("scalar", lambda s, pa=pa, tc=tc: s.activation(
                vt[tc % 2][:], pa[:, :], AF.Silu),
                wait=_w(("p5_vmm", mm), trk.gate(("vt", tc % 2))), inc="p5_ev")
            trk.set(("psv", tc % 4), "p5_ev", ev)
            stt = p.op("scalar", lambda s, tc=tc, ofs=ofs: s.dma_start(
                out=v3_v[:, tc, ofs * 512:(ofs + 1) * 512], in_=vt[tc % 2][:]),
                wait=_w(("p5_ev", ev)), inc="p5_st", inc_by=16)
            trk.set(("vt", tc % 2), "p5_st", stt)
        trk.set(("wv", ofs % 2), "p5_vmm", p.count("p5_vmm"))
    p.wait("sync", "p5_st", p.count("p5_st"))



def build_p6a(p, t):
    par = plan_par(p, ["gpsimd"])
    xn3o = p.sb("xn3o", [128, 16 * 1024], BF16)
    whg = [p.sb(f"whg{i}", [128, 16 * 128], BF16) for i in range(2)]
    bg = p.sb("bg", [128, 32], F32)
    gt = [p.sb(f"gt{i}", [128, 512], BF16) for i in range(2)]
    ps = [p.psb(f"ps{i}", [128, 512]) for i in range(4)]
    xn3_r = xn3o[:].rearrange("p (c t) -> p c t", c=16)
    g3_v = t["gate3"].rearrange("(c p) t -> p c t", p=128)
    xn3h = _half_view(t["xn3"])

    bgt = p.op("sync", lambda s: s.dma_start(out=bg[:], in_=t["b_hidg_pp"]), inc="p6_tab", inc_by=16)
    ldx = p.op("gpsimd", lambda g: g.dma_start(
        out=xn3o[:].rearrange("p (c t) -> p c t", c=16), in_=xn3h[par["gpsimd"]]),
        inc="p6_ld", inc_by=16)
    trk = Tracker()
    for oc in range(32):
        wb = whg[oc % 2]
        wld = p.op("gpsimd", lambda g, oc=oc, wb=wb: g.dma_start(
            out=wb[:].rearrange("p (c o) -> p c o", c=16),
            in_=t["w_hid"].rearrange("(c p) o -> p c o", p=128)[:, :, 4096 + oc * 128:4096 + (oc + 1) * 128]),
            wait=_w(trk.gate(("whg", oc % 2))), inc="p6_wld", inc_by=16)
        for ts in range(2):
            pi = (oc * 2 + ts) % 4
            mm = None
            for c in range(16):
                w0 = ()
                if c == 0:
                    w0 = _w(("p6_wld", wld), ("p6_ld", ldx) if oc == 0 else None,
                            ("p6_tab", bgt) if oc == 0 else None,
                            trk.gate(("ps", pi)))
                mm = p.op("tensor", lambda te, c=c, ts=ts, pi=pi, wb=wb: te.matmul(
                    ps[pi][:, :], wb[:, c * 128:(c + 1) * 128],
                    xn3_r[:, c, ts * 512:(ts + 1) * 512],
                    start=(c == 0), stop=(c == 15)),
                    wait=w0, inc=("p6_mm" if c == 15 else None))
            ev = p.op("scalar", lambda s, oc=oc, ts=ts, pi=pi: s.activation(
                gt[ts][:], ps[pi][:, :], AF.Silu, bias=bg[:, oc:oc + 1]),
                wait=_w(("p6_mm", mm), trk.gate(("gt", ts))), inc="p6_ev")
            trk.set(("ps", pi), "p6_ev", ev)
            stt = p.op("scalar", lambda s, oc=oc, ts=ts: s.dma_start(
                out=g3_v[:, oc, ts * 512:(ts + 1) * 512], in_=gt[ts][:]),
                wait=_w(("p6_ev", ev)), inc="p6_st", inc_by=16)
            trk.set(("gt", ts), "p6_st", stt)
        trk.set(("whg", oc % 2), "p6_mm", p.count("p6_mm"))
    p.wait("sync", "p6_st", p.count("p6_st"))



def build_p6b(p, t):
    par = plan_par(p, ["gpsimd"])
    v_sb = p.sb("v_sb", [128, 16 * 4096], BF16)
    k_sb = p.sb("k_sb", [128, TOK], BF16)
    q_sb = p.sb("q_sb", [128, 1024], BF16)
    sT = p.sb("sT", [128, 16 * 1024], BF16)
    rt = p.sb("rt", [128, 512], BF16)
    gc = [p.sb(f"gc{i}", [128, 1024], BF16) for i in range(2)]
    avt = [p.sb(f"avt{i}", [128, 512], BF16) for i in range(2)]
    ps = [p.psb(f"ps{i}", [128, 512]) for i in range(4)]

    v_r = v_sb[:].rearrange("p (c f) -> p c f", c=16)
    sT_r = sT[:].rearrange("p (c q) -> p c q", c=16)
    v3_v = t["v3"].rearrange("(c p) f -> p c f", p=128)
    g3_v = t["gate3"].rearrange("(c p) t -> p c t", p=128)
    avg_v = t["avg"].rearrange("(c p) t -> p c t", p=128)
    q3h = t["q3"].rearrange("p (h u) -> h p u", h=2)

    ldv = p.op("gpsimd", lambda g: g.dma_start(out=v_r[:, :, :], in_=v3_v[:, :, :]),
               inc="p6_ld", inc_by=16)
    ldk = p.op("gpsimd", lambda g: g.dma_start(out=k_sb[:], in_=t["k3"]),
               inc="p6_ld", inc_by=16)
    ldq = p.op("gpsimd", lambda g: g.dma_start(out=q_sb[:], in_=q3h[par["gpsimd"]]),
               inc="p6_ld", inc_by=16)

    rsc = float(1.0 / (2048.0 * 2048.0))
    sdone = 0
    for tc in range(16):
        for qs in range(2):
            pa = ps[(tc * 2 + qs) % 4]
            gate_sd = max(0, p.count("p6_sd") - 2)
            mm = p.op("tensor", lambda te, tc=tc, qs=qs, pa=pa: te.matmul(
                pa[:, :], k_sb[:, tc * 128:(tc + 1) * 128],
                q_sb[:, qs * 512:(qs + 1) * 512], start=True, stop=True),
                wait=_w(("p6_ld", ldq) if tc == 0 and qs == 0 else None,
                        ("p6_sd", gate_sd)),
                inc="p6_mm")
            rv = p.op("scalar", lambda s, pa=pa: s.activation(
                rt[:], pa[:, :], AF.Relu, scale=rsc),
                wait=_w(("p6_mm", mm), ("p6_sd", p.count("p6_sd"))), inc="p6_rv")
            sdone = p.op("vector", lambda v, tc=tc, qs=qs, pa=pa: v.tensor_tensor(
                sT_r[:, tc, qs * 512:(qs + 1) * 512], pa[:, :], rt[:], OP.mult),
                wait=_w(("p6_rv", rv)), inc="p6_sd")
    trk = Tracker()
    for ec in range(32):
        gl = p.op("gpsimd", lambda g, ec=ec: g.dma_start(
            out=gc[ec % 2][:], in_=g3_v[:, ec, :]),
            wait=_w(trk.gate(("gc", ec % 2))), inc="p6_gld", inc_by=16)
        for qs in range(2):
            pi = (ec * 2 + qs) % 4
            pa = ps[pi]
            mm = None
            for tc in range(16):
                w0 = ()
                if tc == 0:
                    w0 = _w(("p6_sd", p.count("p6_sd")) if ec == 0 else None,
                            trk.gate(("psb", pi)))
                mm = p.op("tensor", lambda te, tc=tc, qs=qs, ec=ec, pa=pa: te.matmul(
                    pa[:, :], v_r[:, tc, ec * 128:(ec + 1) * 128],
                    sT_r[:, tc, qs * 512:(qs + 1) * 512],
                    start=(tc == 0), stop=(tc == 15)),
                    wait=w0, inc=("p6_mm" if tc == 15 else None))
            ev = p.op("vector", lambda v, ec=ec, qs=qs, pa=pa: v.tensor_tensor(
                avt[qs][:], pa[:, :], gc[ec % 2][:, qs * 512:(qs + 1) * 512], OP.mult),
                wait=_w(("p6_mm", mm), ("p6_gld", gl), trk.gate(("avt", qs))),
                inc="p6_ev")
            trk.set(("psb", pi), "p6_ev", ev)
            stt = p.op("scalar", lambda s, ec=ec, qs=qs: s.dma_start(
                out=avg_v[:, ec, qs * 512:(qs + 1) * 512], in_=avt[qs][:]),
                wait=_w(("p6_ev", ev)), inc="p6_st", inc_by=16)
            trk.set(("avt", qs), "p6_st", stt)
        trk.set(("gc", ec % 2), "p6_ev", p.count("p6_ev"))
    p.wait("sync", "p6_st", p.count("p6_st"))



def build_p6c(p, t):
    par = plan_par(p, ["gpsimd"])
    avg_sb = p.sb("avg_sb", [128, 32 * 1024], BF16)
    w1 = [p.sb(f"w1{i}", [128, 32 * 128], BF16) for i in range(2)]
    bo = p.sb("bo", [128, 16], F32)
    xn3o = p.sb("xn3o", [128, 16 * 1024], BF16)
    x2o = p.sb("x2o", [128, 16 * 1024], F32)
    tmp = p.sb("tmp", [128, 512], F32)
    x3t = [p.sb(f"x3t{i}", [128, 512], F32) for i in range(2)]
    ps = [p.psb(f"ps{i}", [128, 512]) for i in range(4)]

    avg_r = avg_sb[:].rearrange("p (c t) -> p c t", c=32)
    avg_v = t["avg"].rearrange("(c p) t -> p c t", p=128)
    x3_v = t["x3o"].rearrange("(c p) t -> p c t", p=128)
    xn3_r = xn3o[:].rearrange("p (c t) -> p c t", c=16)
    x2_r = x2o[:].rearrange("p (c t) -> p c t", c=16)
    xn3h = _half_view(t["xn3"])
    x2h = _half_view(t["x2"])

    bot = p.op("sync", lambda s: s.dma_start(out=bo[:], in_=t["b_out1_pp"]), inc="p6_tab", inc_by=16)
    lda = p.op("gpsimd", lambda g: g.dma_start(out=avg_r[:, :, :], in_=avg_v[:, :, :]),
               inc="p6_ld", inc_by=16)
    p.op("gpsimd", lambda g: g.dma_start(
        out=xn3o[:].rearrange("p (c t) -> p c t", c=16), in_=xn3h[par["gpsimd"]]),
        inc="p6_ld", inc_by=16)
    ldx = p.op("gpsimd", lambda g: g.dma_start(
        out=x2o[:].rearrange("p (c t) -> p c t", c=16), in_=x2h[par["gpsimd"]]),
        inc="p6_ld", inc_by=16)
    trk = Tracker()
    for dc in range(16):
        wb = w1[dc % 2]
        wld = p.op("gpsimd", lambda g, dc=dc, wb=wb: g.dma_start(
            out=wb[:].rearrange("p (c o) -> p c o", c=32),
            in_=t["w_out1"].rearrange("(c p) o -> p c o", p=128)[:, :, dc * 128:(dc + 1) * 128]),
            wait=_w(trk.gate(("w1", dc % 2))), inc="p6_wld", inc_by=16)
        for qs in range(2):
            pi = (dc * 2 + qs) % 4
            mm = None
            for hc in range(32):
                w0 = ()
                if hc == 0:
                    w0 = _w(("p6_wld", wld), ("p6_ld", lda) if dc == 0 else None,
                            trk.gate(("ps", pi)))
                mm = p.op("tensor", lambda te, hc=hc, qs=qs, pi=pi, wb=wb: te.matmul(
                    ps[pi][:, :], wb[:, hc * 128:(hc + 1) * 128],
                    avg_r[:, hc, qs * 512:(qs + 1) * 512],
                    start=(hc == 0), stop=(hc == 31)),
                    wait=w0, inc=("p6_mm" if hc == 31 else None))
            a1 = p.op("scalar", lambda s, dc=dc, qs=qs, pi=pi: s.activation(
                tmp[:], ps[pi][:, :], AF.Identity, bias=bo[:, dc:dc + 1]),
                wait=_w(("p6_mm", mm), ("p6_tab", bot) if dc == 0 else None,
                        ("p6_ld", ldx) if dc == 0 else None,
                        ("p6_a2", p.count("p6_a2"))), inc="p6_a1")
            trk.set(("ps", pi), "p6_a1", a1)
            a2 = p.op("vector", lambda v, dc=dc, qs=qs: v.tensor_tensor(
                tmp[:], tmp[:], xn3_r[:, dc, qs * 512:(qs + 1) * 512], OP.add),
                wait=_w(("p6_a1", a1)), inc="p6_a2")
            ev = p.op("vector", lambda v, dc=dc, qs=qs: v.tensor_tensor(
                x3t[qs][:], tmp[:], x2_r[:, dc, qs * 512:(qs + 1) * 512], OP.add),
                wait=_w(trk.gate(("x3t", qs))), inc="p6_a2")
            stt = p.op("scalar", lambda s, dc=dc, qs=qs: s.dma_start(
                out=x3_v[:, dc, qs * 512:(qs + 1) * 512], in_=x3t[qs][:]),
                wait=_w(("p6_a2", ev)), inc="p6_st", inc_by=16)
            trk.set(("x3t", qs), "p6_st", stt)
        trk.set(("w1", dc % 2), "p6_mm", p.count("p6_mm"))
    p.wait("sync", "p6_st", p.count("p6_st"))



def build_p7a(p, t):
    xn4_sb = p.sb("xn4_sb", [128, 16 * 1024], BF16)
    with p.scope():
        ln_done = ln_sweep(p, t["x3o"], xn4_sb, None, 1024)
    xn4_r = xn4_sb[:].rearrange("p (c t) -> p c t", c=16)

    wT_g = [p.sb(f"wT_g{i}", [128, 16 * 512], BF16) for i in range(2)]
    wT_u = [p.sb(f"wT_u{i}", [128, 16 * 512], BF16) for i in range(2)]
    gt = p.sb("gt", [128, 512], BF16)
    ut = p.sb("ut", [128, 512], BF16)
    ht = [p.sb(f"ht{i}", [128, 512], BF16) for i in range(2)]
    ps_u = [p.psb(f"ps_u{i}", [128, 512]) for i in range(4)]
    ps_g = [p.psb(f"ps_g{i}", [128, 512]) for i in range(4)]

    h4_v = t["h4"].rearrange("(c p) t -> p c t", p=128)
    hr_v = t["hr4"].rearrange("(c p) t -> p c t", p=128)

    trk = Tracker()
    for g in range(18):
        routed = g >= 16
        wsrc_g = t["w_rg"] if routed else t["w_shg"]
        wsrc_u = t["w_ru"] if routed else t["w_shu"]
        oc0 = (g - 16) * 4 if routed else 4 * g
        wtr_g = wt_stream(p, trk, wsrc_g, oc0, 4, wT_g[g % 2],
                          trk.gate(("wTg", g % 2)) or (ln_done if g < 2 else None))
        wtr_u = wt_stream(p, trk, wsrc_u, oc0, 4, wT_u[g % 2],
                          trk.gate(("wTu", g % 2)) or (ln_done if g < 2 else None))
        wg, wu = wT_g[g % 2], wT_u[g % 2]
        for j in range(4):
            oc = oc0 + j
            pu, pg = ps_u[j], ps_g[j]
            for ts in range(2):
                for c in range(16):
                    w0 = ()
                    if c == 0:
                        w0 = _w(wtr_u if (ts == 0 and j == 0) else None,
                                ln_done if (g == 0 and j == 0 and ts == 0) else None,
                                trk.gate(("ps", j)))
                    p.op("tensor", lambda te, j=j, c=c, ts=ts, pg=pg, wg=wg: te.matmul(
                        pg[:, :], wg[:, c * 512 + j * 128:c * 512 + (j + 1) * 128],
                        xn4_r[:, c, ts * 512:(ts + 1) * 512],
                        start=(c == 0), stop=(c == 15)), wait=w0)
                for c in range(16):
                    p.op("tensor", lambda te, j=j, c=c, ts=ts, pu=pu, wu=wu: te.matmul(
                        pu[:, :], wu[:, c * 512 + j * 128:c * 512 + (j + 1) * 128],
                        xn4_r[:, c, ts * 512:(ts + 1) * 512],
                        start=(c == 0), stop=(c == 15)),
                        inc=("p7_mm" if c == 15 else None))
                mmt = p.count("p7_mm")
                e1 = p.op("scalar", lambda s, pg=pg: s.activation(gt[:], pg[:, :], AF.Silu),
                          wait=_w(("p7_mm", mmt), ("p7_h", p.count("p7_h"))), inc="p7_ev")
                e2 = p.op("vector", lambda v, pu=pu: v.tensor_copy(ut[:], pu[:, :]),
                          wait=_w(("p7_mm", mmt), ("p7_h", p.count("p7_h"))), inc="p7_ev")
                hv = p.op("vector", lambda v, ts=ts: v.tensor_tensor(
                    ht[ts][:], gt[:], ut[:], OP.mult),
                    wait=_w(("p7_ev", p.count("p7_ev")), trk.gate(("ht", ts))),
                    inc="p7_h")
                trk.set(("ps", j), "p7_h", hv)
                dst = hr_v if routed else h4_v
                stt = p.op("scalar", lambda s, oc=oc, ts=ts, dst=dst: s.dma_start(
                    out=dst[:, oc, ts * 512:(ts + 1) * 512], in_=ht[ts][:]),
                    wait=_w(("p7_h", hv)), inc="p7_sth", inc_by=16)
                trk.set(("ht", ts), "p7_sth", stt)
        trk.set(("wTg", g % 2), "p7_mm", p.count("p7_mm"))
        trk.set(("wTu", g % 2), "p7_mm", p.count("p7_mm"))
    p.wait("sync", "p7_sth", p.count("p7_sth"))



def build_p7b(p, t):
    h4_sb = p.sb("h4_sb", [128, 64 * 1024], BF16)
    hr_sb = p.sb("hr_sb", [128, 8 * 1024], BF16)
    wT = [p.sb(f"wT{i}", [128, 64 * 128], BF16) for i in range(2)]
    wT_r = [p.sb(f"wT_r{i}", [128, 8 * 128], BF16) for i in range(2)]
    x3c = [p.sb(f"x3c{i}", [128, 512], F32) for i in range(2)]
    yt = [p.sb(f"yt{i}", [128, 512], F32) for i in range(2)]
    ps = [p.psb(f"ps{i}", [128, 512]) for i in range(4)]

    h4_r = h4_sb[:].rearrange("p (c t) -> p c t", c=64)
    hr_r = hr_sb[:].rearrange("p (c t) -> p c t", c=8)
    h4_v = t["h4"].rearrange("(c p) t -> p c t", p=128)
    hr_v = t["hr4"].rearrange("(c p) t -> p c t", p=128)
    x3_v = t["x3o"].rearrange("(c p) t -> p c t", p=128)
    y_v = t["yT"].rearrange("(c p) t -> p c t", p=128)

    ldh = p.op("gpsimd", lambda g: g.dma_start(out=h4_r[:, :, :], in_=h4_v[:, :, :]),
               inc="p7_ld", inc_by=16)
    ldr = p.op("gpsimd", lambda g: g.dma_start(out=hr_r[:, :, :], in_=hr_v[:, :, :]),
               inc="p7_ld", inc_by=16)
    trk = Tracker()
    for dc in range(16):
        wtr = wt_stream(p, trk, t["w_shd"], dc, 1, wT[dc % 2],
                        trk.gate(("wT", dc % 2)), ifc=64)
        wtr_r = wt_stream(p, trk, t["w_rd"], dc, 1, wT_r[dc % 2],
                          trk.gate(("wT", dc % 2)), ifc=8)
        wb, wbr = wT[dc % 2], wT_r[dc % 2]
        for qs in range(2):
            pi = (dc * 2 + qs) % 4
            mm = None
            for c in range(64):
                w0 = ()
                if c == 0:
                    w0 = _w(wtr_r if qs == 0 else None,
                            ("p7_ld", ldr) if dc == 0 else None,
                            trk.gate(("ps", pi)))
                p.op("tensor", lambda te, c=c, qs=qs, pi=pi, wb=wb: te.matmul(
                    ps[pi][:, :], wb[:, c * 128:(c + 1) * 128],
                    h4_r[:, c, qs * 512:(qs + 1) * 512],
                    start=(c == 0), stop=False), wait=w0)
            for c in range(8):
                mm = p.op("tensor", lambda te, c=c, qs=qs, pi=pi, wbr=wbr: te.matmul(
                    ps[pi][:, :], wbr[:, c * 128:(c + 1) * 128],
                    hr_r[:, c, qs * 512:(qs + 1) * 512],
                    start=False, stop=(c == 7)),
                    inc=("p7_mm" if c == 7 else None))
            xl = p.op("gpsimd", lambda g, dc=dc, qs=qs: g.dma_start(
                out=x3c[qs][:], in_=x3_v[:, dc, qs * 512:(qs + 1) * 512]),
                wait=_w(trk.gate(("x3c", qs))), inc="p7_xld", inc_by=16)
            ev = p.op("vector", lambda v, qs=qs, pi=pi: v.tensor_tensor(
                yt[qs][:], ps[pi][:, :], x3c[qs][:], OP.add),
                wait=_w(("p7_mm", mm), ("p7_xld", xl), trk.gate(("yt", qs))),
                inc="p7_ev")
            trk.set(("ps", pi), "p7_ev", ev)
            trk.set(("x3c", qs), "p7_ev", ev)
            stt = p.op("scalar", lambda s, dc=dc, qs=qs: s.dma_start(
                out=y_v[:, dc, qs * 512:(qs + 1) * 512], in_=yt[qs][:]),
                wait=_w(("p7_ev", ev)), inc="p7_st", inc_by=16)
            trk.set(("yt", qs), "p7_st", stt)
        trk.set(("wT", dc % 2), "p7_mm", p.count("p7_mm"))
    p.wait("sync", "p7_st", p.count("p7_st"))



CONST_NAMES = ["h128", "ddT_tab", "qd_tab", "kd_tab", "bd_tab"]

PHASES = {
    "p1": (build_p1, ["x0T", "w_qkv"], ["xn", "qk_fm", "kv_tm"]),
    "p2": (build_p2, ["qk_fm", "kv_tm", "ddT_tab", "qd_tab", "kd_tab", "bd_tab"], ["attn"]),
    "p3": (build_p3, ["attn", "nw_pp", "xn", "w_gate", "w_out0", "x0T"], ["x1"]),
    "p4a": (build_p4a, ["x1", "su_ug_pp", "h128"], ["x1h"]),
    "p4b": (build_p4b, ["x1h", "w_up", "w_wgate", "wsu_pp", "wsg_pp", "su_dp_pp"], ["hbuf"]),
    "p4cd": (build_p4cd, ["hbuf", "h128", "w_down", "wsd_pp", "x1"], ["x2"]),
    "p5": (build_p5, ["x2", "w_hid", "b_hid_row", "w_qk", "b_qk_pp",
                      "gq_pp", "bq_pp", "gk_pp", "bk_pp"], ["xn3", "v3", "q3", "k3"]),
    "p6a": (build_p6a, ["xn3", "w_hid", "b_hidg_pp"], ["gate3"]),
    "p6b": (build_p6b, ["q3", "k3", "v3", "gate3"], ["avg"]),
    "p6c": (build_p6c, ["avg", "w_out1", "b_out1_pp", "x2", "xn3"], ["x3o"]),
    "p7a": (build_p7a, ["x3o", "w_shg", "w_shu", "w_rg", "w_ru"], ["h4", "hr4"]),
    "p7b": (build_p7b, ["h4", "hr4", "w_shd", "w_rd", "x3o"], ["yT"]),
}

PHASE_ORDER = ["p1", "p2", "p3", "p4a", "p4b", "p4cd",
               "p5", "p6a", "p6b", "p6c", "p7a", "p7b"]

TENSORS = {
    "x0T":       ((DM, TOK), F32),
    "w_qkv":     ((DM, 6144), F32),
    "w_gate":    ((DM, DM), F32),
    "w_out0":    ((DM, DM), F32),
    "nw_pp":     ((128, 16), F32),
    "su_ug_pp":  ((128, 16), F32),
    "wsu_pp":    ((128, 64), F32),
    "wsg_pp":    ((128, 64), F32),
    "su_dp_pp":  ((128, 64), F32),
    "wsd_pp":    ((128, 16), F32),
    "w_up":      ((DM, 8192), F32),
    "w_wgate":   ((DM, 8192), F32),
    "w_down":    ((8192, DM), F32),
    "w_hid":     ((DM, 8192), F32),
    "b_hid_row": ((1, 8192), F32),
    "b_hidg_pp": ((128, 32), F32),
    "w_qk":      ((DM, 128), F32),
    "b_qk_pp":   ((128, 1), F32),
    "gq_pp":     ((128, 1), F32),
    "bq_pp":     ((128, 1), F32),
    "gk_pp":     ((128, 1), F32),
    "bk_pp":     ((128, 1), F32),
    "w_out1":    ((4096, DM), F32),
    "b_out1_pp": ((128, 16), F32),
    "w_shg":     ((DM, 8192), F32),
    "w_shu":     ((DM, 8192), F32),
    "w_shd":     ((8192, DM), F32),
    "w_rg":      ((DM, 1024), F32),
    "w_ru":      ((DM, 1024), F32),
    "w_rd":      ((1024, DM), F32),
    "h128":      ((128, 128), BF16),
    "ddT_tab":   ((128, H * 2 * BS), F32),
    "qd_tab":    ((128, 16 * BS), F32),
    "kd_tab":    ((128, 2, DM), F32),
    "bd_tab":    ((128, 1024), F32),
    "xn":        ((DM, TOK), BF16),
    "qk_fm":     ((4096, TOK), BF16),
    "kv_tm":     ((TOK, 4096), BF16),
    "attn":      ((DM, TOK), BF16),
    "attnn":     ((DM, TOK), BF16),
    "x1":        ((DM, TOK), F32),
    "x1h":       ((DM, TOK), BF16),
    "hbuf":      ((8192, TOK), BF16),
    "hpbuf":     ((8192, TOK), BF16),
    "x2":        ((DM, TOK), F32),
    "xn3":       ((DM, TOK), BF16),
    "v3":        ((TOK, 4096), BF16),
    "q3":        ((128, TOK), BF16),
    "k3":        ((128, TOK), BF16),
    "gate3":     ((4096, TOK // 2), BF16),
    "avg":       ((4096, TOK // 2), BF16),
    "x3o":       ((DM, TOK // 2), F32),
    "h4":        ((8192, TOK // 2), BF16),
    "hr4":       ((1024, TOK // 2), BF16),
    "yT":        ((DM, TOK // 2), F32),
}



def check_fused_schedule(progs):
    streams = {}
    for e in ENGS:
        items = []
        for k, p in enumerate(progs):
            if k > 0:
                items.append(([("bar", 5 * k)], None, 0))
            items.extend(fn.meta for fn in p.items[e])
            items.append(([], "bar", 1))
        streams[e] = items
    pos = {e: 0 for e in ENGS}
    sems = {}
    progress = True
    while progress:
        progress = False
        for e in ENGS:
            while pos[e] < len(streams[e]):
                waits, inc, by = streams[e][pos[e]]
                if all(sems.get(s, 0) >= v for (s, v) in waits):
                    if inc:
                        sems[inc] = sems.get(inc, 0) + by
                    pos[e] += 1
                    progress = True
                else:
                    break
    stuck = {e: (pos[e], len(streams[e])) for e in ENGS if pos[e] < len(streams[e])}
    if stuck:
        lines = []
        for e, (i, n) in stuck.items():
            waits = streams[e][i][0]
            missing = [(s, v, sems.get(s, 0)) for (s, v) in waits if sems.get(s, 0) < v]
            lines.append(f"{e}: item {i}/{n} waiting {missing}")
        raise RuntimeError("DEADLOCK in fused schedule:\n" + "\n".join(lines))
    return True

MID_NAMES = ["xn", "qk_fm", "kv_tm", "attn", "attnn", "x1", "x1h", "hbuf",
             "x2", "xn3", "v3", "q3", "k3", "gate3", "avg", "x3o",
             "h4", "hr4"]


def fused_input_names():
    names = []
    for name in PHASE_ORDER:
        _, ins, _ = PHASES[name]
        for n in ins:
            if n not in MID_NAMES and n not in names:
                names.append(n)
    return names


def build_fused_nc(debug_mids=False):
    nc = bass.Bass("TRN2", target_bir_lowering=False, debug=False)
    t = {}
    for n in fused_input_names():
        shape, dt = TENSORS[n]
        t[n] = nc.dram_tensor(n, list(shape), dt, kind="ExternalInput").ap()
    for n in MID_NAMES:
        shape, dt = TENSORS[n]
        kind = "ExternalOutput" if (debug_mids and n != "attnn") else "Internal"
        t[n] = nc.dram_tensor(n, list(shape), dt, kind=kind).ap()
    shape, dt = TENSORS["yT"]
    t["yT"] = nc.dram_tensor("yT", list(shape), dt, kind="ExternalOutput").ap()

    with contextlib.ExitStack() as sem_stack:
        sem_handles, sem_counts = {}, {}
        progs = []
        for name in PHASE_ORDER:
            build, ins, outs = PHASES[name]
            with contextlib.ExitStack() as sb_stack:
                p = Prog(nc, sb_stack, tag=name + "_", sem_stack=sem_stack,
                         sem_handles=sem_handles, sem_counts=sem_counts)
                build(p, t)
            progs.append(p)
        check_fused_schedule(progs)
        bar = sem_stack.enter_context(nc.semaphore("bar"))
        with nc.Block() as block:
            nphase = len(progs)

            def make_runner(eng):
                def runner(engine):
                    for k in range(nphase):
                        if k > 0:
                            engine.wait_ge(bar, 5 * k)
                        for fn in progs[k].items[eng]:
                            fn(engine)
                        engine.drain().then_inc(bar, 1)
                return runner
            for e in ENGS:
                getattr(block, e)(make_runner(e))
    return nc


def build_phase_nc(name):
    build, ins, outs = PHASES[name]
    nc = bass.Bass("TRN2", target_bir_lowering=False, debug=False)
    t = {}
    for n in ins:
        shape, dt = TENSORS[n]
        t[n] = nc.dram_tensor(n, list(shape), dt, kind="ExternalInput").ap()
    for n in outs:
        shape, dt = TENSORS[n]
        t[n] = nc.dram_tensor(n, list(shape), dt, kind="ExternalOutput").ap()
    with contextlib.ExitStack() as stack:
        p = Prog(nc, stack)
        build(p, t)
        p.emit()
    return nc, ins, outs



def _prep_weights(inputs):
    w = {}
    tr = lambda a: np.ascontiguousarray(np.asarray(a, np.float32).T)
    w["w_qkv"] = tr(np.asarray(inputs["L0_attn_qkv_w"]).reshape(H, 3, D, DM)
                    .transpose(1, 0, 2, 3).reshape(6144, DM))
    w["w_gate"] = tr(inputs["L0_attn_gate_w"])
    w["w_out0"] = tr(inputs["L0_attn_out_w"])
    w["nw_pp"] = _pp(inputs["L0_attn_norm_w"], 16)
    w["su_ug_pp"] = _pp(inputs["L0_ffn_SU_ug"], 16)
    w["wsu_pp"] = _pp(inputs["L0_ffn_Wscale_ug"][:8192], 64)
    w["wsg_pp"] = _pp(inputs["L0_ffn_Wscale_ug"][8192:], 64)
    w["su_dp_pp"] = _pp(inputs["L0_ffn_SU_dp"], 64)
    w["wsd_pp"] = _pp(inputs["L0_ffn_Wscale_dp"], 16)
    w["w_up"] = tr(inputs["L0_ffn_W_up"])
    w["w_wgate"] = tr(inputs["L0_ffn_W_gate"])
    w["w_down"] = tr(inputs["L0_ffn_W_down"])
    w["w_hid"] = inputs["L1_attn_W_hidden"]
    w["b_hid_row"] = np.asarray(inputs["L1_attn_b_hidden"], np.float32).reshape(1, 8192)
    w["b_hidg_pp"] = _pp(inputs["L1_attn_b_hidden"][4096:], 32)
    w["w_qk"] = inputs["L1_attn_W_qk"]
    for nm, key in (("b_qk_pp", "L1_attn_b_qk"), ("gq_pp", "L1_attn_gamma_q"),
                    ("bq_pp", "L1_attn_beta_q"), ("gk_pp", "L1_attn_gamma_k"),
                    ("bk_pp", "L1_attn_beta_k")):
        w[nm] = np.asarray(inputs[key], np.float32).reshape(128, 1)
    w["w_out1"] = inputs["L1_attn_W_out"]
    w["b_out1_pp"] = _pp(inputs["L1_attn_b_out"], 16)
    w["w_shg"] = tr(inputs["L1_ffn_sh_gate"])
    w["w_shu"] = tr(inputs["L1_ffn_sh_up"])
    w["w_shd"] = tr(inputs["L1_ffn_sh_down"])
    w["w_rg"] = tr(inputs["L1_ffn_r_gate"])
    w["w_ru"] = tr(inputs["L1_ffn_r_up"])
    w["w_rd"] = tr(inputs["L1_ffn_r_down"])
    w.update(_make_consts())
    return {k: np.ascontiguousarray(v) for k, v in w.items()}


def kernel(**inputs):
    x = inputs["L0_attn_hidden_states"]
    w = _prep_weights(inputs)

    state = [dict(w) for _ in range(N_CORES)]
    for c in range(N_CORES):
        state[c]["x0T"] = np.ascontiguousarray(np.asarray(x[c // 2], np.float32).T)

    if FUSED:
        nc = build_fused_nc()
        names = fused_input_names()
        in_maps = [{n: state[c][n] for n in names} for c in range(N_CORES)]
        res = run_bass_kernel_spmd(nc, in_maps, core_ids=list(range(N_CORES)))
        for c in range(N_CORES):
            state[c]["yT"] = res.results[c]["yT"]
    else:
        for name in PHASE_ORDER:
            nc, ins, outs = build_phase_nc(name)
            in_maps = [{n: state[c][n] for n in ins} for c in range(N_CORES)]
            res = run_bass_kernel_spmd(nc, in_maps, core_ids=list(range(N_CORES)))
            for c in range(N_CORES):
                for n in outs:
                    state[c][n] = res.results[c][n]

    out = np.empty((4, TOK, DM), dtype=np.float32)
    for c in range(N_CORES):
        b, half = c // 2, c % 2
        out[b, half * 1024:(half + 1) * 1024, :] = state[c]["yT"].T
    return out


# version 14
# speedup vs baseline: 1.1475x; 1.1344x over previous
import contextlib
import numpy as np
import ml_dtypes
import concourse.bass as bass
import concourse.mybir as mybir
from concourse.bass_utils import run_bass_kernel_spmd

F32 = mybir.dt.float32
BF16 = mybir.dt.bfloat16
AF = mybir.ActivationFunctionType
OP = mybir.AluOpType
BF = ml_dtypes.bfloat16

N_CORES = 8
TOK = 2048
DM = 2048
EPS = 1e-5
H = 32
D = 64
BS = 256
NB = TOK // BS

FUSED = True



def _hadamard(n):
    h = np.array([[1.0]], dtype=np.float32)
    while h.shape[0] < n:
        h = np.block([[h, h], [h, -h]])
    return h.astype(np.float32)


def _decay():
    base = 1.0 / 2.0 ** (8.0 / H)
    rate = base ** (np.arange(H, dtype=np.float32) + 1.0)
    slope = (rate * np.float32(1.0 + 1e-05)).astype(np.float32)
    r = np.arange(BS, dtype=np.float32) + 1.0
    qd = np.exp(-slope[:, None] * r[None, :]).astype(np.float32)
    kd = np.exp(-slope[:, None] * (BS - r[None, :])).astype(np.float32)
    de = slope[:, None, None] * (r[:, None] - r[None, :])[None, :, :]
    with np.errstate(over="ignore"):
        dd = np.where(de >= 0, np.exp(-de), 0.0).astype(np.float32)
    bd = np.exp(-slope * BS).astype(np.float32)
    return qd, kd, dd, bd


def _pp(vec, chunks):
    return np.ascontiguousarray(np.asarray(vec, dtype=np.float32).reshape(chunks, 128).T)


def _make_consts():
    qd, kd, dd, bd = _decay()
    ddT = dd.transpose(0, 2, 1).reshape(H, 2, 128, BS)
    ddT = np.ascontiguousarray(ddT.transpose(2, 0, 1, 3)).reshape(128, H * 2 * BS)
    qd_tab = np.empty((128, 16 * BS), dtype=np.float32)
    for c in range(16):
        for half in range(2):
            h = 2 * c + half
            qd_tab[half * 64:(half + 1) * 64, c * BS:(c + 1) * BS] = qd[h][None, :]
    kd_tab = np.empty((128, 2, DM), dtype=np.float32)
    for h in range(H):
        for par in range(2):
            kd_tab[:, par, h * 64:(h + 1) * 64] = kd[h, par * 128:(par + 1) * 128][:, None]
    bd_tab = np.empty((128, 16 * 64), dtype=np.float32)
    for ch in range(16):
        for half in range(2):
            bd_tab[half * 64:(half + 1) * 64, ch * 64:(ch + 1) * 64] = bd[2 * ch + half]
    return {
        "h128": _hadamard(128).astype(BF),
        "ddT_tab": np.ascontiguousarray(ddT),
        "qd_tab": qd_tab,
        "kd_tab": kd_tab,
        "bd_tab": np.ascontiguousarray(bd_tab),
    }



ENGS = ("sync", "gpsimd", "scalar", "vector", "tensor")


class Prog:

    def __init__(self, nc, stack, tag="", sem_stack=None, sem_handles=None,
                 sem_counts=None):
        self.nc = nc
        self.stack = stack
        self.sem_stack = sem_stack if sem_stack is not None else stack
        self.tag = tag
        self.items = {e: [] for e in ENGS}
        self.chain_prev = {}
        self.sem_handles = sem_handles if sem_handles is not None else {}
        self.sem_counts = sem_counts if sem_counts is not None else {}

    def sem(self, name):
        if name not in self.sem_handles:
            h = self.sem_stack.enter_context(self.nc.semaphore(name))
            self.sem_handles[name] = h
            self.sem_counts[name] = 0
        return self.sem_handles[name]

    def count(self, name):
        self.sem(name)
        return self.sem_counts[name]

    CHAIN_ENGS = ("vector", "scalar", "gpsimd")

    def op(self, eng, build, wait=(), inc=None, inc_by=1, dma=False):
        dma = dma or inc_by == 16
        chained = (eng in self.CHAIN_ENGS) and not dma
        ret_tok_none = inc is None
        if chained and inc is None:
            inc = "seq_" + eng
        wait_list = [(s, v) for (s, v) in wait if v > 0]
        if chained:
            prev = self.chain_prev.get(eng)
            if prev is not None:
                wait_list = wait_list + [prev]
        wait_vals = [(self.sem(s), v) for (s, v) in wait_list]
        tok = None
        inc_pair = None
        if inc is not None:
            self.sem(inc)
            self.sem_counts[inc] += inc_by
            tok = self.sem_counts[inc]
            inc_pair = (self.sem_handles[inc], inc_by)
        if chained:
            self.chain_prev[eng] = (inc, tok)

        def emit(engine):
            for (h, v) in wait_vals:
                engine.wait_ge(h, v)
            ins = build(engine)
            if inc_pair is not None:
                ins.then_inc(inc_pair[0], inc_pair[1])
        emit.meta = (wait_list, inc, inc_by if inc else 0)
        self.items[eng].append(emit)
        return None if ret_tok_none else tok

    def wait(self, eng, sem, val):
        if val <= 0:
            return
        h = self.sem(sem)
        fn = lambda engine: engine.wait_ge(h, val)
        fn.meta = ([(sem, val)], None, 0)
        self.items[eng].append(fn)

    @contextlib.contextmanager
    def scope(self):
        with contextlib.ExitStack() as st:
            outer = self.stack
            self.stack = st
            try:
                yield
            finally:
                self.stack = outer

    def sb(self, nm, shape, dt):
        return self.stack.enter_context(self.nc.sbuf_tensor(self.tag + nm, shape, dt))

    def psb(self, nm, shape, dt=F32):
        return self.stack.enter_context(self.nc.psum_tensor(self.tag + nm, shape, dt))

    def emit(self):
        with self.nc.Block() as block:
            for e in ENGS:
                items = self.items[e]
                if not items:
                    continue

                def runner(engine, items=items):
                    for fn in items:
                        fn(engine)
                getattr(block, e)(runner)


class Tracker:

    def __init__(self):
        self.t = {}

    def gate(self, key):
        return self.t.get(key, None)

    def set(self, key, sem, val):
        self.t[key] = (sem, val)


def _w(*pairs):
    return [pr for pr in pairs if pr is not None and pr[1] > 0]



def ln_sweep(p, src_dram, dst_sb, dst_dram, ntok, extra_scale_pp=None):
    nt = ntok // 512
    xz = [p.sb(f"ln_xz{i}", [128, 16 * 512], F32) for i in range(2)]
    xb = [p.sb(f"ln_xb{i}", [128, 16 * 512], BF16) for i in range(2)]
    sq = [p.sb(f"ln_sq{i}", [128, 16 * 512], BF16) for i in range(2)]
    ones_bf = p.sb("ln_ones_bf", [128, 1], BF16)
    ones32 = p.sb("ln_ones32", [1, 128], F32)
    st = [p.sb(f"ln_stat{i}", [1, 512], F32) for i in range(4)]
    out_tile = ([p.sb(f"ln_out{i}", [128, 16 * 512], BF16) for i in range(2)]
                if dst_sb is None else None)
    ps_sum = [p.psb(f"ln_ps_sum{i}", [1, 512]) for i in range(2)]
    ps_sq = [p.psb(f"ln_ps_sq{i}", [1, 512]) for i in range(2)]
    ps_b = [p.psb(f"ln_ps_b{i}", [128, 1024]) for i in range(2)]

    srcv = src_dram.rearrange("(c p) t -> p c t", p=128)
    dstv = dst_dram.rearrange("(c p) t -> p c t", p=128) if dst_dram is not None else None

    p.op("vector", lambda v: v.memset(ones_bf[:], 1.0))
    ini = p.op("vector", lambda v: v.memset(ones32[:], 1.0), inc="ln_init")

    apply_tok = {0: 0, 1: 0}
    mm_tok = {}
    state = {"last_apply": 0}

    def stage_a(tt):
        si = tt % 2
        t0 = tt * 512
        xb_r = xb[si][:].rearrange("p (c t) -> p c t", c=16)
        sq_r = sq[si][:].rearrange("p (c t) -> p c t", c=16)
        ld = p.op("gpsimd", lambda g, si=si, t0=t0: g.dma_start(
            out=xz[si][:], in_=srcv[:, :, t0:t0 + 512]),
            wait=_w(("ln_apply", apply_tok[si])), inc="ln_ld", inc_by=16)
        cp = p.op("vector", lambda v, si=si: v.tensor_copy(xb[si][:], xz[si][:]),
                  wait=_w(("ln_ld", ld), ("ln_init", ini)), inc="ln_v")
        sqk = p.op("scalar", lambda s, si=si: s.activation(sq[si][:], xb[si][:], AF.Square),
                   wait=_w(("ln_v", cp)), inc="ln_a")
        for c in range(16):
            p.op("tensor", lambda te, c=c, si=si, xb_r=xb_r: te.matmul(
                ps_sum[si][:, :], ones_bf[:], xb_r[:, c, :],
                start=(c == 0), stop=(c == 15)),
                wait=_w(("ln_a", sqk), ("ln_stat2", p.count("ln_stat2"))) if c == 0 else ())
        mm = None
        for c in range(16):
            mm = p.op("tensor", lambda te, c=c, si=si, sq_r=sq_r: te.matmul(
                ps_sq[si][:, :], ones_bf[:], sq_r[:, c, :],
                start=(c == 0), stop=(c == 15)),
                inc=("ln_mm" if c == 15 else None))
        mm_tok[tt] = mm

    def stage_b(tt):
        si = tt % 2
        t0 = tt * 512
        xz_r = xz[si][:].rearrange("p (c t) -> p c t", c=16)
        xb_r = xb[si][:].rearrange("p (c t) -> p c t", c=16)
        sq_r = sq[si][:].rearrange("p (c t) -> p c t", c=16)
        mm = p.count("ln_mm")
        p.op("vector", lambda v, si=si: v.tensor_scalar(
            st[0][:], ps_sum[si][0:1, :], 1.0 / DM, None, OP.mult),
            wait=_w(("ln_mm", mm)))
        p.op("vector", lambda v, si=si: v.tensor_scalar(
            st[1][:], ps_sum[si][0:1, :], -1.0 / DM, None, OP.mult))
        p.op("vector", lambda v: v.tensor_tensor(st[2][:], st[0][:], st[0][:], OP.mult))
        p.op("vector", lambda v, si=si: v.scalar_tensor_tensor(
            st[3][:], ps_sq[si][0:1, :], 1.0 / DM, st[2][:], OP.mult, OP.subtract))
        stat2 = p.op("vector", lambda v: v.tensor_scalar(
            st[2][:], st[3][:], EPS, None, OP.add), inc="ln_stat2")
        p.op("scalar", lambda s: s.activation(st[3][:], st[2][:], AF.Ln),
             wait=_w(("ln_stat2", stat2)))
        sqr = p.op("scalar", lambda s: s.activation(st[2][:], st[3][:], AF.Exp, scale=-0.5),
                   inc="ln_a")
        nmr = p.op("vector", lambda v: v.tensor_tensor(st[1][:], st[1][:], st[2][:], OP.mult),
                   wait=_w(("ln_a", sqr)), inc="ln_v")
        p.op("tensor", lambda te, si=si: te.matmul(
            ps_b[si][:, 0:512], ones32[:], st[2][:], start=True, stop=True),
            wait=_w(("ln_v", nmr), ("ln_apply", apply_tok[si])))
        mm2 = p.op("tensor", lambda te, si=si: te.matmul(
            ps_b[si][:, 512:1024], ones32[:], st[1][:], start=True, stop=True),
            inc="ln_mm")
        for c in range(16):
            dst_r = (dst_sb[:].rearrange("p (c t) -> p c t", c=16)[:, c, t0:t0 + 512]
                     if dst_sb is not None
                     else out_tile[si][:].rearrange("p (c t) -> p c t", c=16)[:, c, :])
            p.op("vector", lambda v, c=c, si=si, xb_r=xb_r, xz_r=xz_r: v.tensor_tensor(
                xb_r[:, c, :], xz_r[:, c, :], ps_b[si][:, 0:512], OP.mult),
                wait=_w(("ln_mm", mm2)) if c == 0 else ())
            if extra_scale_pp is None:
                tk = p.op("vector", lambda v, c=c, si=si, dst_r=dst_r, xb_r=xb_r: v.tensor_tensor(
                    dst_r, xb_r[:, c, :], ps_b[si][:, 512:1024], OP.add),
                    inc=("ln_apply" if c == 15 else None))
            else:
                p.op("vector", lambda v, c=c, si=si, xb_r=xb_r, sq_r=sq_r: v.tensor_tensor(
                    sq_r[:, c, :], xb_r[:, c, :], ps_b[si][:, 512:1024], OP.add))
                tk = p.op("vector", lambda v, c=c, dst_r=dst_r, sq_r=sq_r: v.tensor_scalar(
                    dst_r, sq_r[:, c, :], extra_scale_pp[:, c:c + 1], None, OP.mult),
                    inc=("ln_apply" if c == 15 else None))
            if tk is not None:
                apply_tok[si] = tk
                state["last_apply"] = tk
        if dst_dram is not None:
            src_t = (dst_sb[:].rearrange("p (c t) -> p c t", c=16)[:, :, t0:t0 + 512]
                     if dst_sb is not None
                     else out_tile[si][:].rearrange("p (c t) -> p c t", c=16))
            p.op("scalar", lambda s, t0=t0, src_t=src_t: s.dma_start(
                out=dstv[:, :, t0:t0 + 512], in_=src_t),
                wait=_w(("ln_apply", apply_tok[si])), inc="ln_st", inc_by=16)

    stage_a(0)
    for tt in range(nt):
        if tt + 1 < nt:
            stage_a(tt + 1)
        stage_b(tt)
    if dst_dram is not None:
        p.wait("scalar", "ln_st", p.count("ln_st"))
        return ("ln_st", p.count("ln_st"))
    return ("ln_apply", state["last_apply"])


def wt_stream(p, trk, wT_dram, oc0, nchunk, wT, prev_use_tok, ifc=16,
              eng="gpsimd"):
    wT_r = wT[:].rearrange("p (c o) -> p c o", c=ifc)
    src = wT_dram.rearrange("(c p) o -> p c o", p=128)[:, :, oc0 * 128:(oc0 + nchunk) * 128]
    tok = p.op(eng, lambda g: g.dma_start(out=wT_r[:, :, :], in_=src),
               wait=_w(prev_use_tok), inc="wld", inc_by=16)
    return ("wld", tok)



def build_p1(p, t):
    xn_sb = p.sb("xn_sb", [128, 16 * TOK], BF16)
    with p.scope():
        ln_done = ln_sweep(p, t["x0T"], xn_sb, t["xn"], TOK)
    xn_r = xn_sb[:].rearrange("p (c t) -> p c t", c=16)

    wT = [p.sb(f"wT{i}", [128, 16 * 512], BF16) for i in range(2)]
    outA = [p.sb(f"outA{i}", [128, TOK], BF16) for i in range(2)]
    outB = [p.sb(f"outB{i}", [128, 512], BF16) for i in range(4)]
    ps_A = [p.psb(f"ps_A{i}", [128, 512]) for i in range(4)]
    ps_B = [p.psb(f"ps_B{i}", [128, 512]) for i in range(4)]

    qkv = t["w_qkv"]
    qkfm = t["qk_fm"].rearrange("(c p) t -> p c t", p=128)
    kvtm = t["kv_tm"].rearrange("(c p) f -> p c f", p=128)

    trk = Tracker()
    for g in range(12):
        a_side = g < 8
        b_side = g >= 4
        wtr = wt_stream(p, trk, qkv, 4 * g, 4, wT[g % 2],
                        trk.gate(("wT", g % 2)) or (ln_done if g < 2 else None))
        wb = wT[g % 2]
        if a_side:
            for j in range(4):
                oc = 4 * g + j
                oa = outA[j % 2]
                for tt in range(4):
                    pa = ps_A[tt]
                    for c in range(16):
                        w0 = ()
                        if c == 0:
                            w0 = _w(wtr if (tt == 0 and j == 0) else None,
                                    ln_done if (g == 0 and tt == 0) else None,
                                    trk.gate(("psA", tt)))
                        p.op("tensor", lambda te, j=j, c=c, tt=tt, pa=pa, wb=wb: te.matmul(
                            pa[:, :], wb[:, c * 512 + j * 128:c * 512 + (j + 1) * 128],
                            xn_r[:, c, tt * 512:(tt + 1) * 512],
                            start=(c == 0), stop=(c == 15)),
                            wait=w0, inc=("mmA" if c == 15 else None))
                    last_mm = ("mmA", p.count("mmA"))
                    ev = p.op("scalar", lambda s, pa=pa, oa=oa, tt=tt: s.activation(
                        oa[:, tt * 512:(tt + 1) * 512], pa[:, :], AF.Silu),
                        wait=_w(last_mm, trk.gate(("outA", j % 2)) if tt == 0 else None),
                        inc="evA")
                    trk.set(("psA", tt), "evA", ev)
                st = p.op("scalar", lambda s, oc=oc, oa=oa: s.dma_start(
                    out=qkfm[:, oc, :], in_=oa[:, :]),
                    wait=_w(("evA", p.count("evA"))), inc="stA", inc_by=16)
                trk.set(("outA", j % 2), "stA", st)
        if b_side:
            of0 = 4 * g * 128 - 2048
            for tc in range(16):
                pb = ps_B[tc % 4]
                ob = outB[tc % 4]
                for c in range(16):
                    w0 = ()
                    if c == 0:
                        w0 = _w(wtr if tc == 0 else None,
                                trk.gate(("psB", tc % 4)))
                    p.op("tensor", lambda te, c=c, tc=tc, pb=pb, wb=wb: te.matmul(
                        pb[:, :], xn_r[:, c, tc * 128:(tc + 1) * 128],
                        wb[:, c * 512:(c + 1) * 512],
                        start=(c == 0), stop=(c == 15)),
                        wait=w0, inc=("mmB" if c == 15 else None))
                last_mm = ("mmB", p.count("mmB"))
                ev = p.op("scalar", lambda s, pb=pb, ob=ob: s.activation(
                    ob[:, :], pb[:, :], AF.Silu),
                    wait=_w(last_mm, trk.gate(("outB", tc % 4))), inc="evB")
                trk.set(("psB", tc % 4), "evB", ev)
                st = p.op("scalar", lambda s, tc=tc, of0=of0, ob=ob: s.dma_start(
                    out=kvtm[:, tc, of0:of0 + 512], in_=ob[:, :]),
                    wait=_w(("evB", ev)), inc="stB", inc_by=16)
                trk.set(("outB", tc % 4), "stB", st)
        mm_gate = ("mmA", p.count("mmA")) if a_side else ("mmB", p.count("mmB"))
        if a_side and b_side:
            mm_gate = ("mmB", p.count("mmB"))
        trk.set(("wT", g % 2), *mm_gate)

    p.wait("sync", "stA", p.count("stA"))
    p.wait("sync", "stB", p.count("stB"))




def build_p2(p, t):
    ddT = p.sb("ddT", [128, H * 2 * BS], F32)
    qdt = p.sb("qdt", [128, 16 * BS], F32)
    kdt = p.sb("kdt", [128, 2 * DM], F32)
    bdt = p.sb("bdt", [128, 1024], F32)
    q_blk = p.sb("q_blk", [128, 16 * BS], BF16)
    k_blk = p.sb("k_blk", [128, 16 * BS], BF16)
    qqd = p.sb("qqd", [128, 16 * BS], BF16)
    kv_blk = p.sb("kv_blk", [128, 2 * 4096], BF16)
    kkd = p.sb("kkd", [128, 2 * DM], BF16)
    sT = [p.sb(f"sT{i}", [128, 1024], BF16) for i in range(2)]
    attn_sb = p.sb("attn_sb", [128, 16 * BS], BF16)
    S_f32 = p.sb("S_f32", [128, 1024], F32)
    S_tmp = p.sb("S_tmp", [128, 1024], F32)
    S_bf = p.sb("S_bf", [128, 1024], BF16)
    ps_aw = [p.psb(f"ps_aw{i}", [128, 1024]) for i in range(2)]
    ps_att = [p.psb(f"ps_att{i}", [128, 512]) for i in range(2)]
    ps_S = p.psb("ps_S", [128, 1024])

    qkv_v = t["qk_fm"].rearrange("(c p) t -> p c t", p=128)
    kv_v = t["kv_tm"].rearrange("(c p) f -> p c f", p=128)
    attn_v = t["attn"].rearrange("(c p) t -> p c t", p=128)

    q_r = q_blk[:].rearrange("p (c t) -> p c t", c=16)
    k_r = k_blk[:].rearrange("p (c t) -> p c t", c=16)
    qqd_r = qqd[:].rearrange("p (c t) -> p c t", c=16)
    kv_r = kv_blk[:].rearrange("p (kc f) -> p kc f", kc=2)
    kkd_r = kkd[:].rearrange("p (kc f) -> p kc f", kc=2)
    kdt_r = kdt[:].rearrange("p (kc f) -> p kc f", kc=2)
    at_r = attn_sb[:].rearrange("p (c t) -> p c t", c=16)

    tab = None
    for (dst, src) in ((ddT, t["ddT_tab"]), (qdt, t["qd_tab"]),
                       (kdt, t["kd_tab"].rearrange("p a f -> p (a f)")), (bdt, t["bd_tab"])):
        tab = p.op("sync", lambda s, dst=dst, src=src: s.dma_start(out=dst[:], in_=src),
                   inc="p2_tab", inc_by=16)
    p.op("vector", lambda v: v.memset(S_f32[:], 0.0))
    s_bf_tok = p.op("vector", lambda v: v.memset(S_bf[:], 0.0), inc="p2_sbf")

    blk_done = 0
    sT_pe_tok = {0: 0, 1: 0}
    ev_tok = 0
    st_tok = 0
    for tb in range(NB):
        t0 = tb * BS
        p.op("gpsimd", lambda g, t0=t0: g.dma_start(
            out=q_blk[:], in_=qkv_v[:, 0:16, t0:t0 + BS]),
            wait=_w(("p2_blk", blk_done)), inc="p2_ld", inc_by=16)
        p.op("gpsimd", lambda g, t0=t0: g.dma_start(
            out=k_blk[:], in_=qkv_v[:, 16:32, t0:t0 + BS]), inc="p2_ld", inc_by=16)
        ld = p.op("gpsimd", lambda g, tb=tb: g.dma_start(
            out=kv_blk[:].rearrange("p (kc f) -> p kc f", kc=2),
            in_=kv_v[:, 2 * tb:2 * tb + 2, :]), inc="p2_ld", inc_by=16)
        qq = p.op("vector", lambda v: v.tensor_tensor(qqd[:], q_blk[:], qdt[:], OP.mult),
                  wait=_w(("p2_ld", ld), ("p2_tab", tab)), inc="p2_v")
        kk = None
        for kc in range(2):
            kk = p.op("vector", lambda v, kc=kc: v.tensor_tensor(
                kkd_r[:, kc, 0:DM], kv_r[:, kc, 0:DM], kdt_r[:, kc, :], OP.mult),
                inc="p2_v")
        sT_dve = {0: 0, 1: 0}
        att_ev = {0: ev_tok, 1: ev_tok}
        for hp in range(H // 2):
            bi = hp % 2
            stb = sT[bi]
            paw = ps_aw[bi]
            pat = ps_att[hp % 2]
            for hh in range(2):
                h = 2 * hp + hh
                ch, hb = h // 2, (h % 2) * 64
                for kc in range(2):
                    w0 = ()
                    if hh == 0 and kc == 0:
                        w0 = _w(("p2_ld", ld) if hp == 0 else None,
                                ("p2_v", sT_dve[bi]))
                    p.op("tensor", lambda te, ch=ch, hb=hb, hh=hh, kc=kc, paw=paw: te.matmul(
                        paw[:, hh * 512 + kc * BS:hh * 512 + (kc + 1) * BS],
                        k_r[hb:hb + 64, ch, kc * 128:(kc + 1) * 128],
                        q_r[hb:hb + 64, ch, :],
                        start=(kc == 0), stop=(kc == 1)),
                        wait=w0, inc=("p2_mm" if (hh == 1 and kc == 1) else None))
            aw_tok = p.count("p2_mm")
            sT_dve[bi] = p.op("vector", lambda v, hp=hp, stb=stb, paw=paw: v.tensor_tensor(
                stb[:], paw[:], ddT[:, hp * 1024:(hp + 1) * 1024], OP.mult),
                wait=_w(("p2_mm", aw_tok), ("p2_mm2", sT_pe_tok[bi])), inc="p2_v")
            for hh in range(2):
                h = 2 * hp + hh
                ch, hb = h // 2, (h % 2) * 64
                for kc in range(2):
                    w0 = ()
                    if hh == 0 and kc == 0:
                        w0 = _w(("p2_v", sT_dve[bi]),
                                ("p2_ev", att_ev[hp % 2]),
                                ("p2_sbf", s_bf_tok))
                    p.op("tensor", lambda te, h=h, hb=hb, hh=hh, kc=kc, stb=stb, pat=pat: te.matmul(
                        pat[hb:hb + 64, 0:BS],
                        kv_r[:, kc, 2048 + h * 64:2048 + (h + 1) * 64],
                        stb[:, hh * 512 + kc * BS:hh * 512 + (kc + 1) * BS],
                        start=(kc == 0), stop=False),
                        wait=w0)
                sT_pe_tok[bi] = p.op("tensor", lambda te, h=h, hb=hb, ch=ch, pat=pat: te.matmul(
                    pat[hb:hb + 64, 0:BS],
                    S_bf[hb:hb + 64, ch * 64:(ch + 1) * 64],
                    qqd_r[hb:hb + 64, ch, :],
                    start=False, stop=True),
                    wait=_w(("p2_v", qq)) if h == 0 else (), inc="p2_mm2")
            att_ev[hp % 2] = ev_tok = p.op("vector", lambda v, hp=hp, pat=pat: v.tensor_copy(
                at_r[:, hp, :], pat[:, 0:BS]),
                wait=_w(("p2_mm2", sT_pe_tok[bi]), ("p2_st", st_tok)), inc="p2_ev")
        smm = None
        for h in range(H):
            for kc in range(2):
                w0 = ()
                if h == 0 and kc == 0:
                    w0 = _w(("p2_v", kk), ("p2_supd", p.count("p2_supd")))
                hb2, ch2 = (h % 2) * 64, h // 2
                smm = p.op("tensor", lambda te, h=h, kc=kc, hb2=hb2, ch2=ch2: te.matmul(
                    ps_S[hb2:hb2 + 64, ch2 * 64:(ch2 + 1) * 64],
                    kkd_r[:, kc, h * 64:(h + 1) * 64],
                    kv_r[:, kc, 2048 + h * 64:2048 + (h + 1) * 64],
                    start=(kc == 0), stop=(kc == 1)),
                    wait=w0, inc=("p2_mm" if (h == H - 1 and kc == 1) else None))
        p.op("vector", lambda v: v.tensor_tensor(S_tmp[:], S_f32[:], bdt[:], OP.mult),
             wait=_w(("p2_mm", smm)))
        up = p.op("vector", lambda v: v.tensor_tensor(S_f32[:], S_tmp[:], ps_S[:, :], OP.add),
                  inc="p2_supd")
        s_bf_tok = p.op("vector", lambda v: v.tensor_copy(S_bf[:], S_f32[:]),
                        inc="p2_sbf")
        st_tok = p.op("scalar", lambda s, t0=t0: s.dma_start(
            out=attn_v[:, :, t0:t0 + BS], in_=at_r[:, :, :]),
            wait=_w(("p2_ev", ev_tok)), inc="p2_st", inc_by=16)
        blk_done = p.op("vector", lambda v: v.engine_nop(),
                        wait=_w(("p2_mm", p.count("p2_mm")),
                                ("p2_mm2", p.count("p2_mm2")),
                                ("p2_sbf", s_bf_tok),
                                ("p2_st", st_tok)),
                        inc="p2_blk")
    p.wait("sync", "p2_st", p.count("p2_st"))




def build_p3(p, t):
    at = p.sb("at", [128, 16 * 512], BF16)
    sq = p.sb("sq", [128, 16 * 512], BF16)
    ones_bf = p.sb("ones_bf", [128, 1], BF16)
    ones32 = p.sb("ones32", [1, 128], F32)
    st0 = p.sb("st0", [1, 512], F32)
    st1 = p.sb("st1", [1, 512], F32)
    nw = p.sb("nw", [128, 16], F32)
    xn_sb = p.sb("xn_sb", [128, 16 * TOK], BF16)
    G = p.sb("G", [128, 16 * TOK], BF16)
    gt = p.sb("gt", [128, 512], BF16)
    wT = [p.sb(f"wT{i}", [128, 16 * 512], BF16) for i in range(2)]
    x0t = [p.sb(f"x0t{i}", [128, 512], F32) for i in range(2)]
    x1t = [p.sb(f"x1t{i}", [128, 512], F32) for i in range(2)]
    ps_sq = p.psb("ps_sq", [1, 512])
    ps_b = p.psb("ps_b", [128, 512])
    ps = [p.psb(f"ps{i}", [128, 512]) for i in range(4)]

    at_r = at[:].rearrange("p (c t) -> p c t", c=16)
    sq_r = sq[:].rearrange("p (c t) -> p c t", c=16)
    xn_r = xn_sb[:].rearrange("p (c t) -> p c t", c=16)
    G_r = G[:].rearrange("p (c t) -> p c t", c=16)
    attn_v = t["attn"].rearrange("(c p) t -> p c t", p=128)
    xn_v = t["xn"].rearrange("(c p) t -> p c t", p=128)
    x0_v = t["x0T"].rearrange("(c p) t -> p c t", p=128)
    x1_v = t["x1"].rearrange("(c p) t -> p c t", p=128)

    p.op("vector", lambda v: v.memset(ones_bf[:], 1.0))
    ini = p.op("vector", lambda v: v.memset(ones32[:], 1.0), inc="p3_init")
    nwt = p.op("sync", lambda s: s.dma_start(out=nw[:], in_=t["nw_pp"]), inc="p3_tab", inc_by=16)
    xnld = p.op("sync", lambda s: s.dma_start(out=xn_sb[:].rearrange("p (c t) -> p c t", c=16),
                                              in_=xn_v[:, :, :]), inc="p3_tab", inc_by=16)

    app_tok = 0
    for tt in range(4):
        t0 = tt * 512
        ld = p.op("gpsimd", lambda g, t0=t0: g.dma_start(
            out=at[:], in_=attn_v[:, :, t0:t0 + 512]),
            wait=_w(("p3_app", app_tok)), inc="p3_ld", inc_by=16)
        sqk = p.op("scalar", lambda s: s.activation(sq[:], at[:], AF.Square),
                   wait=_w(("p3_ld", ld)), inc="p3_a")
        mm = None
        for c in range(16):
            mm = p.op("tensor", lambda te, c=c: te.matmul(
                ps_sq[:, :], ones_bf[:], sq_r[:, c, :],
                start=(c == 0), stop=(c == 15)),
                wait=_w(("p3_a", sqk), ("p3_st2", p.count("p3_st2")), ("p3_init", ini)) if c == 0 else (),
                inc=("p3_mm" if c == 15 else None))
        v1 = p.op("vector", lambda v: v.tensor_scalar(
            st0[:], ps_sq[0:1, :], 1.0 / DM, EPS, OP.mult, OP.add),
            wait=_w(("p3_mm", mm)), inc="p3_v")
        a1 = p.op("scalar", lambda s: s.activation(st1[:], st0[:], AF.Sqrt),
                  wait=_w(("p3_v", v1)), inc="p3_a")
        st2 = p.op("vector", lambda v: v.reciprocal(st0[:], st1[:]),
                   wait=_w(("p3_a", a1)), inc="p3_st2")
        bmm = p.op("tensor", lambda te: te.matmul(ps_b[:, :], ones32[:], st0[:],
                                                  start=True, stop=True),
                   wait=_w(("p3_st2", st2), ("p3_app", app_tok)), inc="p3_mm")
        for c in range(16):
            tk = p.op("vector", lambda v, c=c, t0=t0: v.scalar_tensor_tensor(
                G_r[:, c, t0:t0 + 512], at_r[:, c, :], nw[:, c:c + 1], ps_b[:, :],
                OP.mult, OP.mult),
                wait=_w(("p3_mm", bmm), ("p3_tab", nwt)) if c == 0 else (),
                inc=("p3_app" if c == 15 else None))
            if tk is not None:
                app_tok = tk

    trk = Tracker()
    gdone = 0
    for g in range(4):
        wtr = wt_stream(p, trk, t["w_gate"], 4 * g, 4, wT[g % 2],
                        trk.gate(("wT", g % 2)))
        wb = wT[g % 2]
        for j in range(4):
            oc = 4 * g + j
            for tt in range(4):
                pa = ps[tt]
                for c in range(16):
                    w0 = ()
                    if c == 0:
                        w0 = _w(wtr if (tt == 0 and j == 0) else None,
                                ("p3_tab", xnld) if (g == 0 and j == 0 and tt == 0) else None,
                                trk.gate(("ps", tt)))
                    p.op("tensor", lambda te, j=j, c=c, tt=tt, pa=pa, wb=wb: te.matmul(
                        pa[:, :], wb[:, c * 512 + j * 128:c * 512 + (j + 1) * 128],
                        xn_r[:, c, tt * 512:(tt + 1) * 512],
                        start=(c == 0), stop=(c == 15)),
                        wait=w0, inc=("p3_mmg" if c == 15 else None))
                ev = p.op("scalar", lambda s, pa=pa: s.activation(gt[:], pa[:, :], AF.Sigmoid),
                          wait=_w(("p3_mmg", p.count("p3_mmg")), ("p3_gg", p.count("p3_gg")),
                                  ("p3_app", app_tok) if (g == 0 and j == 0 and tt == 0) else None),
                          inc="p3_ev")
                gdone = p.op("vector", lambda v, oc=oc, tt=tt: v.tensor_tensor(
                    G_r[:, oc, tt * 512:(tt + 1) * 512], gt[:],
                    G_r[:, oc, tt * 512:(tt + 1) * 512], OP.mult),
                    wait=_w(("p3_ev", ev)), inc="p3_gg")
                trk.set(("ps", tt), "p3_gg", gdone)
        trk.set(("wT", g % 2), "p3_mmg", p.count("p3_mmg"))

    gg_all = ("p3_gg", p.count("p3_gg"))
    for g in range(4):
        wtr = wt_stream(p, trk, t["w_out0"], 4 * g, 4, wT[g % 2],
                        trk.gate(("wT", g % 2)))
        wb = wT[g % 2]
        for j in range(4):
            oc = 4 * g + j
            for tt in range(4):
                pa = ps[tt]
                xl = p.op("gpsimd", lambda gp, oc=oc, tt=tt, j=j: gp.dma_start(
                    out=x0t[tt % 2][:], in_=x0_v[:, oc, tt * 512:(tt + 1) * 512]),
                    wait=_w(trk.gate(("x0t", tt % 2))), inc="p3_xld", inc_by=16)
                for c in range(16):
                    w0 = ()
                    if c == 0:
                        w0 = _w(wtr if (tt == 0 and j == 0) else None,
                                gg_all if (g == 0 and j == 0 and tt == 0) else None,
                                trk.gate(("ps", tt)))
                    p.op("tensor", lambda te, j=j, c=c, tt=tt, pa=pa, wb=wb: te.matmul(
                        pa[:, :], wb[:, c * 512 + j * 128:c * 512 + (j + 1) * 128],
                        G_r[:, c, tt * 512:(tt + 1) * 512],
                        start=(c == 0), stop=(c == 15)),
                        wait=w0, inc=("p3_mmo" if c == 15 else None))
                ev = p.op("vector", lambda v, pa=pa, tt=tt: v.tensor_tensor(
                    x1t[tt % 2][:], pa[:, :], x0t[tt % 2][:], OP.add),
                    wait=_w(("p3_mmo", p.count("p3_mmo")), ("p3_xld", xl),
                            trk.gate(("x1t", tt % 2))), inc="p3_evo")
                trk.set(("ps", tt), "p3_evo", ev)
                stt = p.op("scalar", lambda s, oc=oc, tt=tt: s.dma_start(
                    out=x1_v[:, oc, tt * 512:(tt + 1) * 512], in_=x1t[tt % 2][:]),
                    wait=_w(("p3_evo", ev)), inc="p3_sto", inc_by=16)
                trk.set(("x1t", tt % 2), "p3_sto", stt)
                trk.set(("x0t", tt % 2), "p3_evo", ev)
        trk.set(("wT", g % 2), "p3_mmo", p.count("p3_mmo"))

    p.wait("sync", "p3_sto", p.count("p3_sto"))




def _bfly(p, ping, pong, nch, span, first_wait, sems=("p4_cp", "p4_g"),
          final_nxt=None, final_wait=None):
    import math
    nst = int(math.log2(nch))
    v_tok, g_tok = first_wait
    bufs = [ping, pong]
    for s in range(nst):
        dist = nch >> (s + 1)
        cur = bufs[s % 2][:].rearrange("p (g two d) -> p g two d", two=2, d=dist * span)
        if s == nst - 1 and final_nxt is not None:
            nxt = final_nxt
        else:
            nxt = bufs[(s + 1) % 2][:].rearrange("p (g two d) -> p g two d", two=2, d=dist * span)
        a, b = cur[:, :, 0, :], cur[:, :, 1, :]
        w = _w((sems[0], v_tok), (sems[1], g_tok))
        if s == nst - 1 and final_wait is not None:
            w = w + _w(final_wait)
        v_tok = p.op("vector", lambda v, a=a, b=b, nxt=nxt: v.tensor_tensor(
            nxt[:, :, 0, :], a, b, OP.add), wait=w, inc=sems[0])
        v_tok = p.op("vector", lambda v, a=a, b=b, nxt=nxt: v.tensor_tensor(
            nxt[:, :, 1, :], a, b, OP.subtract), inc=sems[0])
    return nst % 2, v_tok, g_tok


def build_p4a(p, t):
    xsu = p.sb("xsu", [128, 16 * TOK], BF16)
    su = p.sb("su", [128, 16], F32)
    sut = p.op("sync", lambda s: s.dma_start(out=su[:], in_=t["su_ug_pp"]), inc="p4_tab", inc_by=16)
    p.wait("vector", "p4_tab", sut)
    with p.scope():
        ln_done = ln_sweep(p, t["x1"], xsu, None, TOK, extra_scale_pp=su)
    h128 = p.sb("h128s", [128, 128], BF16)
    h1t = p.op("sync", lambda s: s.dma_start(out=h128[:], in_=t["h128"]), wait=_w(ln_done),
               inc="p4_tab", inc_by=16)
    ping = p.sb("ping", [128, 16 * 512], BF16)
    pong = p.sb("pong", [128, 16 * 512], BF16)
    ps = [p.psb(f"ps{i}", [128, 512]) for i in range(4)]
    xsu_r = xsu[:].rearrange("p (c t) -> p c t", c=16)
    ping_r = ping[:].rearrange("p (c t) -> p c t", c=16)
    pong_r = pong[:].rearrange("p (c t) -> p c t", c=16)
    x1h_v = t["x1h"].rearrange("(c p) t -> p c t", p=128)

    st_tok = 0
    for tt in range(4):
        t0 = tt * 512
        cp = None
        for c in range(16):
            w0 = ()
            if c == 0:
                w0 = _w(ln_done, ("p4_tab", h1t), ("p4_st", st_tok))
            else:
                w0 = _w(("p4_cp", max(0, p.count("p4_cp") - 3)),)
            p.op("tensor", lambda te, c=c, t0=t0: te.matmul(
                ps[c % 4][:, :], h128[:], xsu_r[:, c, t0:t0 + 512],
                start=True, stop=True), wait=w0, inc="p4_mm")
            cp = p.op("vector", lambda v, c=c: v.tensor_copy(ping_r[:, c, :], ps[c % 4][:, :]),
                      wait=_w(("p4_mm", p.count("p4_mm"))), inc="p4_cp")
        fin, vt, gt = _bfly(p, ping, pong, 16, 512, (cp, p.count("p4_g")))
        assert fin == 0
        st_tok = p.op("scalar", lambda s, t0=t0: s.dma_start(
            out=x1h_v[:, :, t0:t0 + 512], in_=ping_r[:, :, :]),
            wait=_w(("p4_cp", vt), ("p4_g", gt)), inc="p4_st", inc_by=16)
    p.wait("sync", "p4_st", p.count("p4_st"))



def build_p4b(p, t):
    x1h_sb = p.sb("x1h_sb", [128, 16 * TOK], BF16)
    wsu = p.sb("wsu", [128, 64], F32)
    wsg = p.sb("wsg", [128, 64], F32)
    sud = p.sb("sud", [128, 64], F32)
    wT_u = [p.sb(f"wT_u{i}", [128, 16 * 512], BF16) for i in range(2)]
    wT_g = [p.sb(f"wT_g{i}", [128, 16 * 512], BF16) for i in range(2)]
    gt = p.sb("gt", [128, 512], BF16)
    ut = p.sb("ut", [128, 512], BF16)
    ht = [p.sb(f"ht{i}", [128, 512], BF16) for i in range(2)]
    ps_u = [p.psb(f"ps_u{i}", [128, 512]) for i in range(4)]
    ps_g = [p.psb(f"ps_g{i}", [128, 512]) for i in range(4)]

    x1h_v = t["x1h"].rearrange("(c p) t -> p c t", p=128)
    h_v = t["hbuf"].rearrange("(c p) t -> p c t", p=128)
    x1h_r = x1h_sb[:].rearrange("p (c t) -> p c t", c=16)

    for nm, dst in (("wsu_pp", wsu), ("wsg_pp", wsg), ("su_dp_pp", sud)):
        p.op("sync", lambda s, nm=nm, dst=dst: s.dma_start(out=dst[:], in_=t[nm]),
             inc="p4_tab", inc_by=16)
    ldx = p.op("sync", lambda s: s.dma_start(
        out=x1h_sb[:].rearrange("p (c t) -> p c t", c=16), in_=x1h_v[:, :, :]),
        inc="p4_tab", inc_by=16)
    tab4b = ldx
    rs = float(1.0 / np.sqrt(2048.0))
    p.op("vector", lambda v: v.tensor_tensor(wsu[:], wsu[:], sud[:], OP.mult),
         wait=_w(("p4_tab", tab4b)))
    p.op("vector", lambda v: v.tensor_scalar(wsu[:], wsu[:], rs, None, OP.mult))
    pr = p.op("vector", lambda v: v.tensor_scalar(wsg[:], wsg[:], rs, None, OP.mult),
              inc="p4_pr")

    trk = Tracker()
    for g in range(16):
        wtr_u = wt_stream(p, trk, t["w_up"], 4 * g, 4, wT_u[g % 2],
                          trk.gate(("wTu", g % 2)))
        wtr_g = wt_stream(p, trk, t["w_wgate"], 4 * g, 4, wT_g[g % 2],
                          trk.gate(("wTg", g % 2)))
        wu, wg = wT_u[g % 2], wT_g[g % 2]
        for j in range(4):
            oc = 4 * g + j
            for tt in range(4):
                pu, pg = ps_u[tt], ps_g[tt]
                for c in range(16):
                    w0 = ()
                    if c == 0:
                        w0 = _w(wtr_g if (tt == 0 and j == 0) else None,
                                ("p4_pr", pr) if (g == 0 and j == 0 and tt == 0) else None,
                                trk.gate(("ps", tt)))
                    p.op("tensor", lambda te, j=j, c=c, tt=tt, pu=pu, wu=wu: te.matmul(
                        pu[:, :], wu[:, c * 512 + j * 128:c * 512 + (j + 1) * 128],
                        x1h_r[:, c, tt * 512:(tt + 1) * 512],
                        start=(c == 0), stop=(c == 15)), wait=w0)
                for c in range(16):
                    p.op("tensor", lambda te, j=j, c=c, tt=tt, pg=pg, wg=wg: te.matmul(
                        pg[:, :], wg[:, c * 512 + j * 128:c * 512 + (j + 1) * 128],
                        x1h_r[:, c, tt * 512:(tt + 1) * 512],
                        start=(c == 0), stop=(c == 15)),
                        inc=("p4_mm" if c == 15 else None))
                mmt = p.count("p4_mm")
                e1 = p.op("scalar", lambda s, pg=pg, oc=oc: s.activation(
                    gt[:], pg[:, :], AF.Silu, scale=wsg[:, oc:oc + 1]),
                    wait=_w(("p4_mm", mmt), ("p4_h", p.count("p4_h"))), inc="p4_ev")
                e2 = p.op("vector", lambda v, pu=pu, oc=oc: v.tensor_scalar(
                    ut[:], pu[:, :], wsu[:, oc:oc + 1], None, OP.mult),
                    wait=_w(("p4_mm", mmt), ("p4_h", p.count("p4_h"))), inc="p4_ev")
                hv = p.op("vector", lambda v, tt=tt: v.tensor_tensor(
                    ht[tt % 2][:], gt[:], ut[:], OP.mult),
                    wait=_w(("p4_ev", p.count("p4_ev")), trk.gate(("ht", tt % 2))),
                    inc="p4_h")
                trk.set(("ps", tt), "p4_h", hv)
                stt = p.op("scalar", lambda s, oc=oc, tt=tt: s.dma_start(
                    out=h_v[:, oc, tt * 512:(tt + 1) * 512], in_=ht[tt % 2][:]),
                    wait=_w(("p4_h", hv)), inc="p4_sth", inc_by=16)
                trk.set(("ht", tt % 2), "p4_sth", stt)
        mm_gate = ("p4_mm", p.count("p4_mm"))
        trk.set(("wTu", g % 2), *mm_gate)
        trk.set(("wTg", g % 2), *mm_gate)
    p.wait("sync", "p4_sth", p.count("p4_sth"))



def build_p4c(p, t):
    h128 = p.sb("h128s", [128, 128], BF16)
    ping = p.sb("ping", [128, 64 * 512], BF16)
    pong = p.sb("pong", [128, 64 * 512], BF16)
    ps = [p.psb(f"ps{i}", [128, 512]) for i in range(4)]
    h1t = p.op("sync", lambda s: s.dma_start(out=h128[:], in_=t["h128"]), inc="p4_tab", inc_by=16)
    h_v = t["hbuf"].rearrange("(c p) t -> p c t", p=128)
    hp_v = t["hpbuf"].rearrange("(c p) t -> p c t", p=128)
    ping_r = ping[:].rearrange("p (c t) -> p c t", c=64)
    pong_r = pong[:].rearrange("p (c t) -> p c t", c=64)

    st_tok = 0
    for tt in range(4):
        t0 = tt * 512
        ld = p.op("gpsimd", lambda g, t0=t0: g.dma_start(
            out=ping_r[:, :, :], in_=h_v[:, :, t0:t0 + 512]),
            wait=_w(("p4_st", st_tok)), inc="p4_ld", inc_by=16)
        cp = None
        for c in range(64):
            w0 = _w(("p4_ld", ld), ("p4_tab", h1t)) if c == 0 else _w(("p4_cp", max(0, p.count("p4_cp") - 3)),)
            p.op("tensor", lambda te, c=c: te.matmul(
                ps[c % 4][:, :], h128[:], ping_r[:, c, :],
                start=True, stop=True), wait=w0, inc="p4_mm")
            cp = p.op("vector", lambda v, c=c: v.tensor_copy(ping_r[:, c, :], ps[c % 4][:, :]),
                      wait=_w(("p4_mm", p.count("p4_mm"))), inc="p4_cp")
        fin, vt, gt = _bfly(p, ping, pong, 64, 512, (cp, p.count("p4_g")))
        assert fin == 0
        st_tok = p.op("scalar", lambda s, t0=t0: s.dma_start(
            out=hp_v[:, :, t0:t0 + 512], in_=ping_r[:, :, :]),
            wait=_w(("p4_cp", vt), ("p4_g", gt)), inc="p4_st", inc_by=16)
    p.wait("sync", "p4_st", p.count("p4_st"))



def build_p4d(p, t):
    wsd = p.sb("wsd", [128, 16], F32)
    hp_sb = p.sb("hp_sb", [128, 64 * 1024], BF16)
    wT = [p.sb(f"wT{i}", [128, 64 * 128], BF16) for i in range(2)]
    x1t = [p.sb(f"x1t{i}", [128, 512], F32) for i in range(2)]
    x2t = [p.sb(f"x2t{i}", [128, 512], F32) for i in range(2)]
    ps = [p.psb(f"ps{i}", [128, 512]) for i in range(4)]

    hp_v = t["hpbuf"].rearrange("(c p) t -> p c t", p=128)
    x1_v = t["x1"].rearrange("(c p) t -> p c t", p=128)
    x2_v = t["x2"].rearrange("(c p) t -> p c t", p=128)
    hp_r = hp_sb[:].rearrange("p (c t) -> p c t", c=64)

    wsdt = p.op("sync", lambda s: s.dma_start(out=wsd[:], in_=t["wsd_pp"]), inc="p4_tab", inc_by=16)
    rs = float(1.0 / np.sqrt(8192.0))
    pr = p.op("vector", lambda v: v.tensor_scalar(wsd[:], wsd[:], rs, None, OP.mult),
              wait=_w(("p4_tab", wsdt)), inc="p4_pr")

    trk = Tracker()
    for sup in range(2):
        s0 = sup * 1024
        ldh = p.op("gpsimd", lambda g, s0=s0: g.dma_start(
            out=hp_r[:, :, :], in_=hp_v[:, :, s0:s0 + 1024]),
            wait=_w(trk.gate("hp")), inc="p4_ld", inc_by=16)
        for oc in range(16):
            wtr = wt_stream(p, trk, t["w_down"], oc, 1, wT[oc % 2],
                            trk.gate(("wT", oc % 2)), ifc=64)
            wb = wT[oc % 2]
            for sp in range(2):
                pi = (oc * 2 + sp) % 4
                for c in range(64):
                    w0 = ()
                    if c == 0:
                        w0 = _w(wtr if sp == 0 else None, ("p4_ld", ldh) if sp == 0 else None,
                                ("p4_pr", pr), trk.gate(("ps", pi)))
                    p.op("tensor", lambda te, c=c, sp=sp, pa=ps[pi], wb=wb: te.matmul(
                        pa[:, :], wb[:, c * 128:(c + 1) * 128],
                        hp_r[:, c, sp * 512:(sp + 1) * 512],
                        start=(c == 0), stop=(c == 63)),
                        wait=w0, inc=("p4_mm" if c == 63 else None))
                xl = p.op("gpsimd", lambda g, oc=oc, sp=sp, s0=s0: g.dma_start(
                    out=x1t[sp][:], in_=x1_v[:, oc, s0 + sp * 512:s0 + (sp + 1) * 512]),
                    wait=_w(trk.gate(("x1t", sp))), inc="p4_xld", inc_by=16)
                ev = p.op("vector", lambda v, oc=oc, sp=sp, pi=pi: v.scalar_tensor_tensor(
                    x2t[sp][:], ps[pi][:, :], wsd[:, oc:oc + 1], x1t[sp][:],
                    OP.mult, OP.add),
                    wait=_w(("p4_mm", p.count("p4_mm")), ("p4_xld", xl),
                            trk.gate(("x2t", sp))), inc="p4_ev")
                trk.set(("ps", pi), "p4_ev", ev)
                trk.set(("x1t", sp), "p4_ev", ev)
                stt = p.op("scalar", lambda s, oc=oc, sp=sp, s0=s0: s.dma_start(
                    out=x2_v[:, oc, s0 + sp * 512:s0 + (sp + 1) * 512], in_=x2t[sp][:]),
                    wait=_w(("p4_ev", ev)), inc="p4_stx", inc_by=16)
                trk.set(("x2t", sp), "p4_stx", stt)
            trk.set(("wT", oc % 2), "p4_mm", p.count("p4_mm"))
        trk.set("hp", "p4_mm", p.count("p4_mm"))
    p.wait("sync", "p4_stx", p.count("p4_stx"))




def plan_par(p, engs):
    store = {}
    for eng in engs:
        def f(e, eng=eng):
            pid = e.partition_id()
            store[eng] = e.snap(pid % 2, min_val=0, max_val=1)
            return e.nop()
        p.op(eng, f)
    return store


def _half_view(dram_ap, p128=128):
    return dram_ap.rearrange("(c p) (h u) -> h p c u", p=p128, h=2)




def build_p4cd(p, t):
    h128 = p.sb("h128s", [128, 128], BF16)
    ping = p.sb("ping", [128, 64 * 128], BF16)
    pong = p.sb("pong", [128, 64 * 128], BF16)
    hpO = p.sb("hpO", [128, 64 * 1024], BF16)
    wsd = p.sb("wsd", [128, 16], F32)
    wT = [p.sb(f"wT{i}", [128, 64 * 128], BF16) for i in range(2)]
    x1t = [p.sb(f"x1t{i}", [128, 512], F32) for i in range(2)]
    x2t = [p.sb(f"x2t{i}", [128, 512], F32) for i in range(2)]
    ps_c = [p.psb(f"ps_c{i}", [128, 512]) for i in range(2)]
    ps_d = [p.psb(f"ps_d{i}", [128, 512]) for i in range(4)]

    h_v = t["hbuf"].rearrange("(c p) t -> p c t", p=128)
    x1_v = t["x1"].rearrange("(c p) t -> p c t", p=128)
    x2_v = t["x2"].rearrange("(c p) t -> p c t", p=128)
    ping_r = ping[:].rearrange("p (c t) -> p c t", c=64)
    hpO_r = hpO[:].rearrange("p (c t) -> p c t", c=64)
    hpO_b = hpO[:].rearrange("p (g two u) -> p g two u", two=2, u=1024)

    h1t = p.op("sync", lambda s: s.dma_start(out=h128[:], in_=t["h128"]), inc="p4_tab", inc_by=16)
    wsdt = p.op("sync", lambda s: s.dma_start(out=wsd[:], in_=t["wsd_pp"]), inc="p4_tab", inc_by=16)
    rs = float(1.0 / np.sqrt(8192.0))
    pr = p.op("vector", lambda v: v.tensor_scalar(wsd[:], wsd[:], rs, None, OP.mult),
              wait=_w(("p4_tab", wsdt)), inc="p4_pr")

    trk = Tracker()
    st_tok = 0

    def c_tile(j, sup):
        t0 = sup * 1024 + j * 128
        ld = p.op("gpsimd", lambda g, t0=t0: g.dma_start(
            out=ping_r[:, :, :], in_=h_v[:, :, t0:t0 + 128]),
            wait=_w(trk.gate("pp")), inc="p4_ld", inc_by=16)
        cp = None
        for q in range(16):
            for i in range(4):
                c = 4 * q + i
                w0 = ()
                if q == 0 and i == 0:
                    w0 = _w(("p4_ld", ld), ("p4_tab", h1t))
                elif i == 0:
                    w0 = _w(("p4_cp", max(0, p.count("p4_cp") - 1)))
                p.op("tensor", lambda te, c=c, q=q, i=i: te.matmul(
                    ps_c[q % 2][:, i * 128:(i + 1) * 128], h128[:], ping_r[:, c, :],
                    start=True, stop=True), wait=w0,
                    inc=("p4_mm" if i == 3 else None))
            cp = p.op("vector", lambda v, q=q: v.tensor_copy(
                ping_r[:, 4 * q:4 * q + 4, :],
                ps_c[q % 2][:, :].rearrange("p (i t) -> p i t", i=4)),
                wait=_w(("p4_mm", p.count("p4_mm"))), inc="p4_cp")
        fin, vt, gt = _bfly(p, ping, pong, 64, 128, (cp, p.count("p4_g")),
                            final_nxt=hpO_b[:, :, :, j * 128:(j + 1) * 128],
                            final_wait=trk.gate("hpO"))
        trk.set("pp", "p4_cp", vt)
        return vt

    def d_super(sup, hp_tok):
        s0 = sup * 1024
        last_ev = None
        for oc in range(16):
            wtr = wt_stream(p, trk, t["w_down"], oc, 1, wT[oc % 2],
                            trk.gate(("wT", oc % 2)), ifc=64)
            wb = wT[oc % 2]
            for sp in range(2):
                pi = (oc * 2 + sp) % 4
                for c in range(64):
                    w0 = ()
                    if c == 0:
                        w0 = _w(wtr if sp == 0 else None,
                                ("p4_cp", hp_tok) if (oc == 0 and sp == 0) else None,
                                ("p4_pr", pr), trk.gate(("ps", pi)))
                    p.op("tensor", lambda te, c=c, sp=sp, pi=pi, wb=wb: te.matmul(
                        ps_d[pi][:, :], wb[:, c * 128:(c + 1) * 128],
                        hpO_r[:, c, sp * 512:(sp + 1) * 512],
                        start=(c == 0), stop=(c == 63)),
                        wait=w0, inc=("p4_mm2" if c == 63 else None))
                xl = p.op("gpsimd", lambda g, oc=oc, sp=sp, s0=s0: g.dma_start(
                    out=x1t[sp][:], in_=x1_v[:, oc, s0 + sp * 512:s0 + (sp + 1) * 512]),
                    wait=_w(trk.gate(("x1t", sp))), inc="p4_xld", inc_by=16)
                ev = p.op("vector", lambda v, oc=oc, sp=sp, pi=pi: v.scalar_tensor_tensor(
                    x2t[sp][:], ps_d[pi][:, :], wsd[:, oc:oc + 1], x1t[sp][:],
                    OP.mult, OP.add),
                    wait=_w(("p4_mm2", p.count("p4_mm2")), ("p4_xld", xl),
                            trk.gate(("x2t", sp))), inc="p4_ev")
                trk.set(("ps", pi), "p4_ev", ev)
                trk.set(("x1t", sp), "p4_ev", ev)
                stt = p.op("scalar", lambda s, oc=oc, sp=sp, s0=s0: s.dma_start(
                    out=x2_v[:, oc, s0 + sp * 512:s0 + (sp + 1) * 512], in_=x2t[sp][:]),
                    wait=_w(("p4_ev", ev)), inc="p4_stx", inc_by=16)
                trk.set(("x2t", sp), "p4_stx", stt)
            trk.set(("wT", oc % 2), "p4_mm2", p.count("p4_mm2"))
        trk.set("hpO", "p4_mm2", p.count("p4_mm2"))

    toks0 = [c_tile(j, 0) for j in range(8)]
    d_super(0, toks0[-1])
    toks1 = [c_tile(j, 1) for j in range(8)]
    d_super(1, toks1[-1])
    p.wait("sync", "p4_stx", p.count("p4_stx"))



def build_p5(p, t):
    xn3_sb = p.sb("xn3_sb", [128, 16 * TOK], BF16)
    with p.scope():
        ln_done = ln_sweep(p, t["x2"], xn3_sb, t["xn3"], TOK)
    xn3_r = xn3_sb[:].rearrange("p (c t) -> p c t", c=16)

    wv = [p.sb(f"wv{i}", [128, 16 * 512], BF16) for i in range(2)]
    ones1 = p.sb("ones1", [1, 128], BF16)
    bias_bf = p.sb("bias_bf", [1, 512], BF16)
    bias_f32 = p.sb("bias_f32", [1, 512], F32)
    vt = [p.sb(f"vt{i}", [128, 512], BF16) for i in range(2)]
    wqk = p.sb("wqk", [128, 16 * 128], BF16)
    bqk = p.sb("bqk", [128, 1], F32)
    gq = p.sb("gq", [128, 1], F32)
    bq = p.sb("bq", [128, 1], F32)
    gk = p.sb("gk", [128, 1], F32)
    bk = p.sb("bk", [128, 1], F32)
    qkt = p.sb("qkt", [128, 512], BF16)
    q3s = p.sb("q3s", [128, TOK], BF16)
    k3s = p.sb("k3s", [128, TOK], BF16)
    ps = [p.psb(f"ps{i}", [128, 512]) for i in range(4)]

    v3_v = t["v3"].rearrange("(c p) f -> p c f", p=128)
    trk = Tracker()

    init5 = p.op("vector", lambda v: v.memset(ones1[:], 1.0), inc="p5_ini")
    tab5 = None
    for nm, dst in (("b_qk_pp", bqk), ("gq_pp", gq), ("bq_pp", bq), ("gk_pp", gk), ("bk_pp", bk)):
        tab5 = p.op("sync", lambda s, nm=nm, dst=dst: s.dma_start(out=dst[:], in_=t[nm]),
             wait=_w(ln_done), inc="p5_tab", inc_by=16)

    wqk_tok = p.op("gpsimd", lambda g: g.dma_start(
        out=wqk[:].rearrange("p (c o) -> p c o", c=16),
        in_=t["w_qk"].rearrange("(c p) o -> p c o", p=128)),
        wait=_w(ln_done), inc="p5_wld", inc_by=16)
    for tt in range(4):
        mm = None
        for c in range(16):
            w0 = _w(("p5_wld", wqk_tok), ln_done, ("p5_tab", tab5),
                    ("p5_qaff", p.count("p5_qaff"))) if c == 0 else ()
            mm = p.op("tensor", lambda te, c=c, tt=tt: te.matmul(
                ps[0][:, :], wqk[:, c * 128:(c + 1) * 128],
                xn3_r[:, c, tt * 512:(tt + 1) * 512],
                start=(c == 0), stop=(c == 15)),
                wait=w0, inc=("p5_mm" if c == 15 else None))
        sl = p.op("scalar", lambda s: s.activation(qkt[:], ps[0][:, :], AF.Silu, bias=bqk[:, 0:1]),
                  wait=_w(("p5_mm", mm)), inc="p5_ev")
        p.op("vector", lambda v, tt=tt: v.tensor_scalar(
            q3s[:, tt * 512:(tt + 1) * 512], qkt[:], gq[:, 0:1], bq[:, 0:1], OP.mult, OP.add),
            wait=_w(("p5_ev", sl)))
        p.op("vector", lambda v, tt=tt: v.tensor_scalar(
            k3s[:, tt * 512:(tt + 1) * 512], qkt[:], gk[:, 0:1], bk[:, 0:1], OP.mult, OP.add),
            inc="p5_qaff")
    qa = p.count("p5_qaff")
    p.op("scalar", lambda s: s.dma_start(out=t["q3"], in_=q3s[:]),
         wait=_w(("p5_qaff", qa)), inc="p5_st", inc_by=16)
    p.op("scalar", lambda s: s.dma_start(out=t["k3"], in_=k3s[:]),
         wait=_w(("p5_qaff", qa)), inc="p5_st", inc_by=16)

    for ofs in range(8):
        wb = wv[ofs % 2]
        wld = p.op("gpsimd", lambda g, ofs=ofs, wb=wb: g.dma_start(
            out=wb[:].rearrange("p (c o) -> p c o", c=16),
            in_=t["w_hid"].rearrange("(c p) o -> p c o", p=128)[:, :, ofs * 512:(ofs + 1) * 512]),
            wait=_w(trk.gate(("wv", ofs % 2)) or (ln_done if ofs < 2 else None)),
            inc="p5_wld", inc_by=16)
        bl = p.op("gpsimd", lambda g, ofs=ofs: g.dma_start(
            out=bias_f32[:], in_=t["b_hid_row"][0:1, ofs * 512:(ofs + 1) * 512]),
            wait=_w(("p5_bc", p.count("p5_bc"))), inc="p5_wld", inc_by=16)
        bc = p.op("vector", lambda v: v.tensor_copy(bias_bf[:], bias_f32[:]),
                  wait=_w(("p5_wld", p.count("p5_wld")), ("p5_ini", init5),
                          ("p5_vmm", p.count("p5_vmm"))), inc="p5_bc")
        for tc in range(16):
            pa = ps[tc % 4]
            for c in range(16):
                w0 = ()
                if c == 0:
                    w0 = _w(("p5_wld", p.count("p5_wld")), ln_done if ofs == 0 else None,
                            ("p5_bc", bc), trk.gate(("psv", tc % 4)))
                p.op("tensor", lambda te, c=c, tc=tc, pa=pa, wb=wb: te.matmul(
                    pa[:, :], xn3_r[:, c, tc * 128:(tc + 1) * 128],
                    wb[:, c * 512:(c + 1) * 512],
                    start=(c == 0), stop=False), wait=w0)
            mm = p.op("tensor", lambda te, pa=pa: te.matmul(
                pa[:, :], ones1[:], bias_bf[:], start=False, stop=True),
                inc="p5_vmm")
            ev = p.op("scalar", lambda s, pa=pa, tc=tc: s.activation(
                vt[tc % 2][:], pa[:, :], AF.Silu),
                wait=_w(("p5_vmm", mm), trk.gate(("vt", tc % 2))), inc="p5_ev")
            trk.set(("psv", tc % 4), "p5_ev", ev)
            stt = p.op("scalar", lambda s, tc=tc, ofs=ofs: s.dma_start(
                out=v3_v[:, tc, ofs * 512:(ofs + 1) * 512], in_=vt[tc % 2][:]),
                wait=_w(("p5_ev", ev)), inc="p5_st", inc_by=16)
            trk.set(("vt", tc % 2), "p5_st", stt)
        trk.set(("wv", ofs % 2), "p5_vmm", p.count("p5_vmm"))
    p.wait("sync", "p5_st", p.count("p5_st"))



def build_p6a(p, t):
    par = plan_par(p, ["gpsimd"])
    xn3o = p.sb("xn3o", [128, 16 * 1024], BF16)
    whg = [p.sb(f"whg{i}", [128, 16 * 128], BF16) for i in range(2)]
    bg = p.sb("bg", [128, 32], F32)
    gt = [p.sb(f"gt{i}", [128, 512], BF16) for i in range(2)]
    ps = [p.psb(f"ps{i}", [128, 512]) for i in range(4)]
    xn3_r = xn3o[:].rearrange("p (c t) -> p c t", c=16)
    g3_v = t["gate3"].rearrange("(c p) t -> p c t", p=128)
    xn3h = _half_view(t["xn3"])

    bgt = p.op("sync", lambda s: s.dma_start(out=bg[:], in_=t["b_hidg_pp"]), inc="p6_tab", inc_by=16)
    ldx = p.op("gpsimd", lambda g: g.dma_start(
        out=xn3o[:].rearrange("p (c t) -> p c t", c=16), in_=xn3h[par["gpsimd"]]),
        inc="p6_ld", inc_by=16)
    trk = Tracker()
    for oc in range(32):
        wb = whg[oc % 2]
        wld = p.op("gpsimd", lambda g, oc=oc, wb=wb: g.dma_start(
            out=wb[:].rearrange("p (c o) -> p c o", c=16),
            in_=t["w_hid"].rearrange("(c p) o -> p c o", p=128)[:, :, 4096 + oc * 128:4096 + (oc + 1) * 128]),
            wait=_w(trk.gate(("whg", oc % 2))), inc="p6_wld", inc_by=16)
        for ts in range(2):
            pi = (oc * 2 + ts) % 4
            mm = None
            for c in range(16):
                w0 = ()
                if c == 0:
                    w0 = _w(("p6_wld", wld), ("p6_ld", ldx) if oc == 0 else None,
                            ("p6_tab", bgt) if oc == 0 else None,
                            trk.gate(("ps", pi)))
                mm = p.op("tensor", lambda te, c=c, ts=ts, pi=pi, wb=wb: te.matmul(
                    ps[pi][:, :], wb[:, c * 128:(c + 1) * 128],
                    xn3_r[:, c, ts * 512:(ts + 1) * 512],
                    start=(c == 0), stop=(c == 15)),
                    wait=w0, inc=("p6_mm" if c == 15 else None))
            ev = p.op("scalar", lambda s, oc=oc, ts=ts, pi=pi: s.activation(
                gt[ts][:], ps[pi][:, :], AF.Silu, bias=bg[:, oc:oc + 1]),
                wait=_w(("p6_mm", mm), trk.gate(("gt", ts))), inc="p6_ev")
            trk.set(("ps", pi), "p6_ev", ev)
            stt = p.op("scalar", lambda s, oc=oc, ts=ts: s.dma_start(
                out=g3_v[:, oc, ts * 512:(ts + 1) * 512], in_=gt[ts][:]),
                wait=_w(("p6_ev", ev)), inc="p6_st", inc_by=16)
            trk.set(("gt", ts), "p6_st", stt)
        trk.set(("whg", oc % 2), "p6_mm", p.count("p6_mm"))
    p.wait("sync", "p6_st", p.count("p6_st"))



def build_p6b(p, t):
    par = plan_par(p, ["gpsimd"])
    v_sb = p.sb("v_sb", [128, 16 * 4096], BF16)
    k_sb = p.sb("k_sb", [128, TOK], BF16)
    q_sb = p.sb("q_sb", [128, 1024], BF16)
    sT = p.sb("sT", [128, 16 * 1024], BF16)
    rt = p.sb("rt", [128, 512], BF16)
    gc = [p.sb(f"gc{i}", [128, 1024], BF16) for i in range(2)]
    avt = [p.sb(f"avt{i}", [128, 512], BF16) for i in range(2)]
    ps = [p.psb(f"ps{i}", [128, 512]) for i in range(4)]

    v_r = v_sb[:].rearrange("p (c f) -> p c f", c=16)
    sT_r = sT[:].rearrange("p (c q) -> p c q", c=16)
    v3_v = t["v3"].rearrange("(c p) f -> p c f", p=128)
    g3_v = t["gate3"].rearrange("(c p) t -> p c t", p=128)
    avg_v = t["avg"].rearrange("(c p) t -> p c t", p=128)
    q3h = t["q3"].rearrange("p (h u) -> h p u", h=2)

    ldv = p.op("gpsimd", lambda g: g.dma_start(out=v_r[:, :, :], in_=v3_v[:, :, :]),
               inc="p6_ld", inc_by=16)
    ldk = p.op("gpsimd", lambda g: g.dma_start(out=k_sb[:], in_=t["k3"]),
               inc="p6_ld", inc_by=16)
    ldq = p.op("gpsimd", lambda g: g.dma_start(out=q_sb[:], in_=q3h[par["gpsimd"]]),
               inc="p6_ld", inc_by=16)

    rsc = float(1.0 / (2048.0 * 2048.0))
    sdone = 0
    for tc in range(16):
        for qs in range(2):
            pa = ps[(tc * 2 + qs) % 4]
            gate_sd = max(0, p.count("p6_sd") - 2)
            mm = p.op("tensor", lambda te, tc=tc, qs=qs, pa=pa: te.matmul(
                pa[:, :], k_sb[:, tc * 128:(tc + 1) * 128],
                q_sb[:, qs * 512:(qs + 1) * 512], start=True, stop=True),
                wait=_w(("p6_ld", ldq) if tc == 0 and qs == 0 else None,
                        ("p6_sd", gate_sd)),
                inc="p6_mm")
            rv = p.op("scalar", lambda s, pa=pa: s.activation(
                rt[:], pa[:, :], AF.Relu, scale=rsc),
                wait=_w(("p6_mm", mm), ("p6_sd", p.count("p6_sd"))), inc="p6_rv")
            sdone = p.op("vector", lambda v, tc=tc, qs=qs, pa=pa: v.tensor_tensor(
                sT_r[:, tc, qs * 512:(qs + 1) * 512], pa[:, :], rt[:], OP.mult),
                wait=_w(("p6_rv", rv)), inc="p6_sd")
    trk = Tracker()
    for ec in range(32):
        gl = p.op("gpsimd", lambda g, ec=ec: g.dma_start(
            out=gc[ec % 2][:], in_=g3_v[:, ec, :]),
            wait=_w(trk.gate(("gc", ec % 2))), inc="p6_gld", inc_by=16)
        for qs in range(2):
            pi = (ec * 2 + qs) % 4
            pa = ps[pi]
            mm = None
            for tc in range(16):
                w0 = ()
                if tc == 0:
                    w0 = _w(("p6_sd", p.count("p6_sd")) if ec == 0 else None,
                            trk.gate(("psb", pi)))
                mm = p.op("tensor", lambda te, tc=tc, qs=qs, ec=ec, pa=pa: te.matmul(
                    pa[:, :], v_r[:, tc, ec * 128:(ec + 1) * 128],
                    sT_r[:, tc, qs * 512:(qs + 1) * 512],
                    start=(tc == 0), stop=(tc == 15)),
                    wait=w0, inc=("p6_mm" if tc == 15 else None))
            ev = p.op("vector", lambda v, ec=ec, qs=qs, pa=pa: v.tensor_tensor(
                avt[qs][:], pa[:, :], gc[ec % 2][:, qs * 512:(qs + 1) * 512], OP.mult),
                wait=_w(("p6_mm", mm), ("p6_gld", gl), trk.gate(("avt", qs))),
                inc="p6_ev")
            trk.set(("psb", pi), "p6_ev", ev)
            stt = p.op("scalar", lambda s, ec=ec, qs=qs: s.dma_start(
                out=avg_v[:, ec, qs * 512:(qs + 1) * 512], in_=avt[qs][:]),
                wait=_w(("p6_ev", ev)), inc="p6_st", inc_by=16)
            trk.set(("avt", qs), "p6_st", stt)
        trk.set(("gc", ec % 2), "p6_ev", p.count("p6_ev"))
    p.wait("sync", "p6_st", p.count("p6_st"))



def build_p6c(p, t):
    par = plan_par(p, ["gpsimd"])
    avg_sb = p.sb("avg_sb", [128, 32 * 1024], BF16)
    w1 = [p.sb(f"w1{i}", [128, 32 * 128], BF16) for i in range(2)]
    bo = p.sb("bo", [128, 16], F32)
    xn3o = p.sb("xn3o", [128, 16 * 1024], BF16)
    x2o = p.sb("x2o", [128, 16 * 1024], F32)
    tmp = p.sb("tmp", [128, 512], F32)
    x3t = [p.sb(f"x3t{i}", [128, 512], F32) for i in range(2)]
    ps = [p.psb(f"ps{i}", [128, 512]) for i in range(4)]

    avg_r = avg_sb[:].rearrange("p (c t) -> p c t", c=32)
    avg_v = t["avg"].rearrange("(c p) t -> p c t", p=128)
    x3_v = t["x3o"].rearrange("(c p) t -> p c t", p=128)
    xn3_r = xn3o[:].rearrange("p (c t) -> p c t", c=16)
    x2_r = x2o[:].rearrange("p (c t) -> p c t", c=16)
    xn3h = _half_view(t["xn3"])
    x2h = _half_view(t["x2"])

    bot = p.op("sync", lambda s: s.dma_start(out=bo[:], in_=t["b_out1_pp"]), inc="p6_tab", inc_by=16)
    lda = p.op("gpsimd", lambda g: g.dma_start(out=avg_r[:, :, :], in_=avg_v[:, :, :]),
               inc="p6_ld", inc_by=16)
    p.op("gpsimd", lambda g: g.dma_start(
        out=xn3o[:].rearrange("p (c t) -> p c t", c=16), in_=xn3h[par["gpsimd"]]),
        inc="p6_ld", inc_by=16)
    ldx = p.op("gpsimd", lambda g: g.dma_start(
        out=x2o[:].rearrange("p (c t) -> p c t", c=16), in_=x2h[par["gpsimd"]]),
        inc="p6_ld", inc_by=16)
    trk = Tracker()
    for dc in range(16):
        wb = w1[dc % 2]
        wld = p.op("gpsimd", lambda g, dc=dc, wb=wb: g.dma_start(
            out=wb[:].rearrange("p (c o) -> p c o", c=32),
            in_=t["w_out1"].rearrange("(c p) o -> p c o", p=128)[:, :, dc * 128:(dc + 1) * 128]),
            wait=_w(trk.gate(("w1", dc % 2))), inc="p6_wld", inc_by=16)
        for qs in range(2):
            pi = (dc * 2 + qs) % 4
            mm = None
            for hc in range(32):
                w0 = ()
                if hc == 0:
                    w0 = _w(("p6_wld", wld), ("p6_ld", lda) if dc == 0 else None,
                            trk.gate(("ps", pi)))
                mm = p.op("tensor", lambda te, hc=hc, qs=qs, pi=pi, wb=wb: te.matmul(
                    ps[pi][:, :], wb[:, hc * 128:(hc + 1) * 128],
                    avg_r[:, hc, qs * 512:(qs + 1) * 512],
                    start=(hc == 0), stop=(hc == 31)),
                    wait=w0, inc=("p6_mm" if hc == 31 else None))
            a1 = p.op("scalar", lambda s, dc=dc, qs=qs, pi=pi: s.activation(
                tmp[:], ps[pi][:, :], AF.Identity, bias=bo[:, dc:dc + 1]),
                wait=_w(("p6_mm", mm), ("p6_tab", bot) if dc == 0 else None,
                        ("p6_ld", ldx) if dc == 0 else None,
                        ("p6_a2", p.count("p6_a2"))), inc="p6_a1")
            trk.set(("ps", pi), "p6_a1", a1)
            a2 = p.op("vector", lambda v, dc=dc, qs=qs: v.tensor_tensor(
                tmp[:], tmp[:], xn3_r[:, dc, qs * 512:(qs + 1) * 512], OP.add),
                wait=_w(("p6_a1", a1)), inc="p6_a2")
            ev = p.op("vector", lambda v, dc=dc, qs=qs: v.tensor_tensor(
                x3t[qs][:], tmp[:], x2_r[:, dc, qs * 512:(qs + 1) * 512], OP.add),
                wait=_w(trk.gate(("x3t", qs))), inc="p6_a2")
            stt = p.op("scalar", lambda s, dc=dc, qs=qs: s.dma_start(
                out=x3_v[:, dc, qs * 512:(qs + 1) * 512], in_=x3t[qs][:]),
                wait=_w(("p6_a2", ev)), inc="p6_st", inc_by=16)
            trk.set(("x3t", qs), "p6_st", stt)
        trk.set(("w1", dc % 2), "p6_mm", p.count("p6_mm"))
    p.wait("sync", "p6_st", p.count("p6_st"))



def build_p7a(p, t):
    xn4_sb = p.sb("xn4_sb", [128, 16 * 1024], BF16)
    with p.scope():
        ln_done = ln_sweep(p, t["x3o"], xn4_sb, None, 1024)
    xn4_r = xn4_sb[:].rearrange("p (c t) -> p c t", c=16)

    wT_g = [p.sb(f"wT_g{i}", [128, 16 * 512], BF16) for i in range(2)]
    wT_u = [p.sb(f"wT_u{i}", [128, 16 * 512], BF16) for i in range(2)]
    gt = p.sb("gt", [128, 512], BF16)
    ut = p.sb("ut", [128, 512], BF16)
    ht = [p.sb(f"ht{i}", [128, 512], BF16) for i in range(2)]
    ps_u = [p.psb(f"ps_u{i}", [128, 512]) for i in range(4)]
    ps_g = [p.psb(f"ps_g{i}", [128, 512]) for i in range(4)]

    h4_v = t["h4"].rearrange("(c p) t -> p c t", p=128)
    hr_v = t["hr4"].rearrange("(c p) t -> p c t", p=128)

    trk = Tracker()
    for g in range(18):
        routed = g >= 16
        wsrc_g = t["w_rg"] if routed else t["w_shg"]
        wsrc_u = t["w_ru"] if routed else t["w_shu"]
        oc0 = (g - 16) * 4 if routed else 4 * g
        wtr_g = wt_stream(p, trk, wsrc_g, oc0, 4, wT_g[g % 2],
                          trk.gate(("wTg", g % 2)) or (ln_done if g < 2 else None))
        wtr_u = wt_stream(p, trk, wsrc_u, oc0, 4, wT_u[g % 2],
                          trk.gate(("wTu", g % 2)) or (ln_done if g < 2 else None))
        wg, wu = wT_g[g % 2], wT_u[g % 2]
        for j in range(4):
            oc = oc0 + j
            for ts in range(2):
                pi = (j * 2 + ts) % 4
                pu, pg = ps_u[pi], ps_g[pi]
                for c in range(16):
                    w0 = ()
                    if c == 0:
                        w0 = _w(wtr_u if (ts == 0 and j == 0) else None,
                                ln_done if (g == 0 and j == 0 and ts == 0) else None,
                                trk.gate(("ps", pi)))
                    p.op("tensor", lambda te, j=j, c=c, ts=ts, pg=pg, wg=wg: te.matmul(
                        pg[:, :], wg[:, c * 512 + j * 128:c * 512 + (j + 1) * 128],
                        xn4_r[:, c, ts * 512:(ts + 1) * 512],
                        start=(c == 0), stop=(c == 15)), wait=w0)
                for c in range(16):
                    p.op("tensor", lambda te, j=j, c=c, ts=ts, pu=pu, wu=wu: te.matmul(
                        pu[:, :], wu[:, c * 512 + j * 128:c * 512 + (j + 1) * 128],
                        xn4_r[:, c, ts * 512:(ts + 1) * 512],
                        start=(c == 0), stop=(c == 15)),
                        inc=("p7_mm" if c == 15 else None))
                mmt = p.count("p7_mm")
                e1 = p.op("scalar", lambda s, pg=pg: s.activation(gt[:], pg[:, :], AF.Silu),
                          wait=_w(("p7_mm", mmt), ("p7_h", p.count("p7_h"))), inc="p7_ev")
                e2 = p.op("vector", lambda v, pu=pu: v.tensor_copy(ut[:], pu[:, :]),
                          wait=_w(("p7_mm", mmt), ("p7_h", p.count("p7_h"))), inc="p7_ev")
                hv = p.op("vector", lambda v, ts=ts: v.tensor_tensor(
                    ht[ts][:], gt[:], ut[:], OP.mult),
                    wait=_w(("p7_ev", p.count("p7_ev")), trk.gate(("ht", ts))),
                    inc="p7_h")
                trk.set(("ps", pi), "p7_h", hv)
                dst = hr_v if routed else h4_v
                stt = p.op("scalar", lambda s, oc=oc, ts=ts, dst=dst: s.dma_start(
                    out=dst[:, oc, ts * 512:(ts + 1) * 512], in_=ht[ts][:]),
                    wait=_w(("p7_h", hv)), inc="p7_sth", inc_by=16)
                trk.set(("ht", ts), "p7_sth", stt)
        trk.set(("wTg", g % 2), "p7_mm", p.count("p7_mm"))
        trk.set(("wTu", g % 2), "p7_mm", p.count("p7_mm"))
    p.wait("sync", "p7_sth", p.count("p7_sth"))



def build_p7b(p, t):
    h4_sb = p.sb("h4_sb", [128, 64 * 1024], BF16)
    hr_sb = p.sb("hr_sb", [128, 8 * 1024], BF16)
    wT = [p.sb(f"wT{i}", [128, 64 * 128], BF16) for i in range(2)]
    wT_r = [p.sb(f"wT_r{i}", [128, 8 * 128], BF16) for i in range(2)]
    x3c = [p.sb(f"x3c{i}", [128, 512], F32) for i in range(2)]
    yt = [p.sb(f"yt{i}", [128, 512], F32) for i in range(2)]
    ps = [p.psb(f"ps{i}", [128, 512]) for i in range(4)]

    h4_r = h4_sb[:].rearrange("p (c t) -> p c t", c=64)
    hr_r = hr_sb[:].rearrange("p (c t) -> p c t", c=8)
    h4_v = t["h4"].rearrange("(c p) t -> p c t", p=128)
    hr_v = t["hr4"].rearrange("(c p) t -> p c t", p=128)
    x3_v = t["x3o"].rearrange("(c p) t -> p c t", p=128)
    y_v = t["yT"].rearrange("(c p) t -> p c t", p=128)

    ldh = p.op("gpsimd", lambda g: g.dma_start(out=h4_r[:, :, :], in_=h4_v[:, :, :]),
               inc="p7_ld", inc_by=16)
    ldr = p.op("gpsimd", lambda g: g.dma_start(out=hr_r[:, :, :], in_=hr_v[:, :, :]),
               inc="p7_ld", inc_by=16)
    trk = Tracker()
    for dc in range(16):
        wtr = wt_stream(p, trk, t["w_shd"], dc, 1, wT[dc % 2],
                        trk.gate(("wT", dc % 2)), ifc=64)
        wtr_r = wt_stream(p, trk, t["w_rd"], dc, 1, wT_r[dc % 2],
                          trk.gate(("wT", dc % 2)), ifc=8)
        wb, wbr = wT[dc % 2], wT_r[dc % 2]
        for qs in range(2):
            pi = (dc * 2 + qs) % 4
            mm = None
            for c in range(64):
                w0 = ()
                if c == 0:
                    w0 = _w(wtr_r if qs == 0 else None,
                            ("p7_ld", ldr) if dc == 0 else None,
                            trk.gate(("ps", pi)))
                p.op("tensor", lambda te, c=c, qs=qs, pi=pi, wb=wb: te.matmul(
                    ps[pi][:, :], wb[:, c * 128:(c + 1) * 128],
                    h4_r[:, c, qs * 512:(qs + 1) * 512],
                    start=(c == 0), stop=False), wait=w0)
            for c in range(8):
                mm = p.op("tensor", lambda te, c=c, qs=qs, pi=pi, wbr=wbr: te.matmul(
                    ps[pi][:, :], wbr[:, c * 128:(c + 1) * 128],
                    hr_r[:, c, qs * 512:(qs + 1) * 512],
                    start=False, stop=(c == 7)),
                    inc=("p7_mm" if c == 7 else None))
            xl = p.op("gpsimd", lambda g, dc=dc, qs=qs: g.dma_start(
                out=x3c[qs][:], in_=x3_v[:, dc, qs * 512:(qs + 1) * 512]),
                wait=_w(trk.gate(("x3c", qs))), inc="p7_xld", inc_by=16)
            ev = p.op("vector", lambda v, qs=qs, pi=pi: v.tensor_tensor(
                yt[qs][:], ps[pi][:, :], x3c[qs][:], OP.add),
                wait=_w(("p7_mm", mm), ("p7_xld", xl), trk.gate(("yt", qs))),
                inc="p7_ev")
            trk.set(("ps", pi), "p7_ev", ev)
            trk.set(("x3c", qs), "p7_ev", ev)
            stt = p.op("scalar", lambda s, dc=dc, qs=qs: s.dma_start(
                out=y_v[:, dc, qs * 512:(qs + 1) * 512], in_=yt[qs][:]),
                wait=_w(("p7_ev", ev)), inc="p7_st", inc_by=16)
            trk.set(("yt", qs), "p7_st", stt)
        trk.set(("wT", dc % 2), "p7_mm", p.count("p7_mm"))
    p.wait("sync", "p7_st", p.count("p7_st"))



CONST_NAMES = ["h128", "ddT_tab", "qd_tab", "kd_tab", "bd_tab"]

PHASES = {
    "p1": (build_p1, ["x0T", "w_qkv"], ["xn", "qk_fm", "kv_tm"]),
    "p2": (build_p2, ["qk_fm", "kv_tm", "ddT_tab", "qd_tab", "kd_tab", "bd_tab"], ["attn"]),
    "p3": (build_p3, ["attn", "nw_pp", "xn", "w_gate", "w_out0", "x0T"], ["x1"]),
    "p4a": (build_p4a, ["x1", "su_ug_pp", "h128"], ["x1h"]),
    "p4b": (build_p4b, ["x1h", "w_up", "w_wgate", "wsu_pp", "wsg_pp", "su_dp_pp"], ["hbuf"]),
    "p4cd": (build_p4cd, ["hbuf", "h128", "w_down", "wsd_pp", "x1"], ["x2"]),
    "p5": (build_p5, ["x2", "w_hid", "b_hid_row", "w_qk", "b_qk_pp",
                      "gq_pp", "bq_pp", "gk_pp", "bk_pp"], ["xn3", "v3", "q3", "k3"]),
    "p6a": (build_p6a, ["xn3", "w_hid", "b_hidg_pp"], ["gate3"]),
    "p6b": (build_p6b, ["q3", "k3", "v3", "gate3"], ["avg"]),
    "p6c": (build_p6c, ["avg", "w_out1", "b_out1_pp", "x2", "xn3"], ["x3o"]),
    "p7a": (build_p7a, ["x3o", "w_shg", "w_shu", "w_rg", "w_ru"], ["h4", "hr4"]),
    "p7b": (build_p7b, ["h4", "hr4", "w_shd", "w_rd", "x3o"], ["yT"]),
}

PHASE_ORDER = ["p1", "p2", "p3", "p4a", "p4b", "p4cd",
               "p5", "p6a", "p6b", "p6c", "p7a", "p7b"]

TENSORS = {
    "x0T":       ((DM, TOK), F32),
    "w_qkv":     ((DM, 6144), F32),
    "w_gate":    ((DM, DM), F32),
    "w_out0":    ((DM, DM), F32),
    "nw_pp":     ((128, 16), F32),
    "su_ug_pp":  ((128, 16), F32),
    "wsu_pp":    ((128, 64), F32),
    "wsg_pp":    ((128, 64), F32),
    "su_dp_pp":  ((128, 64), F32),
    "wsd_pp":    ((128, 16), F32),
    "w_up":      ((DM, 8192), F32),
    "w_wgate":   ((DM, 8192), F32),
    "w_down":    ((8192, DM), F32),
    "w_hid":     ((DM, 8192), F32),
    "b_hid_row": ((1, 8192), F32),
    "b_hidg_pp": ((128, 32), F32),
    "w_qk":      ((DM, 128), F32),
    "b_qk_pp":   ((128, 1), F32),
    "gq_pp":     ((128, 1), F32),
    "bq_pp":     ((128, 1), F32),
    "gk_pp":     ((128, 1), F32),
    "bk_pp":     ((128, 1), F32),
    "w_out1":    ((4096, DM), F32),
    "b_out1_pp": ((128, 16), F32),
    "w_shg":     ((DM, 8192), F32),
    "w_shu":     ((DM, 8192), F32),
    "w_shd":     ((8192, DM), F32),
    "w_rg":      ((DM, 1024), F32),
    "w_ru":      ((DM, 1024), F32),
    "w_rd":      ((1024, DM), F32),
    "h128":      ((128, 128), BF16),
    "ddT_tab":   ((128, H * 2 * BS), F32),
    "qd_tab":    ((128, 16 * BS), F32),
    "kd_tab":    ((128, 2, DM), F32),
    "bd_tab":    ((128, 1024), F32),
    "xn":        ((DM, TOK), BF16),
    "qk_fm":     ((4096, TOK), BF16),
    "kv_tm":     ((TOK, 4096), BF16),
    "attn":      ((DM, TOK), BF16),
    "attnn":     ((DM, TOK), BF16),
    "x1":        ((DM, TOK), F32),
    "x1h":       ((DM, TOK), BF16),
    "hbuf":      ((8192, TOK), BF16),
    "hpbuf":     ((8192, TOK), BF16),
    "x2":        ((DM, TOK), F32),
    "xn3":       ((DM, TOK), BF16),
    "v3":        ((TOK, 4096), BF16),
    "q3":        ((128, TOK), BF16),
    "k3":        ((128, TOK), BF16),
    "gate3":     ((4096, TOK // 2), BF16),
    "avg":       ((4096, TOK // 2), BF16),
    "x3o":       ((DM, TOK // 2), F32),
    "h4":        ((8192, TOK // 2), BF16),
    "hr4":       ((1024, TOK // 2), BF16),
    "yT":        ((DM, TOK // 2), F32),
}



def check_fused_schedule(progs):
    streams = {}
    for e in ENGS:
        items = []
        for k, p in enumerate(progs):
            if k > 0:
                items.append(([("bar", 5 * k)], None, 0))
            items.extend(fn.meta for fn in p.items[e])
            items.append(([], "bar", 1))
        streams[e] = items
    pos = {e: 0 for e in ENGS}
    sems = {}
    progress = True
    while progress:
        progress = False
        for e in ENGS:
            while pos[e] < len(streams[e]):
                waits, inc, by = streams[e][pos[e]]
                if all(sems.get(s, 0) >= v for (s, v) in waits):
                    if inc:
                        sems[inc] = sems.get(inc, 0) + by
                    pos[e] += 1
                    progress = True
                else:
                    break
    stuck = {e: (pos[e], len(streams[e])) for e in ENGS if pos[e] < len(streams[e])}
    if stuck:
        lines = []
        for e, (i, n) in stuck.items():
            waits = streams[e][i][0]
            missing = [(s, v, sems.get(s, 0)) for (s, v) in waits if sems.get(s, 0) < v]
            lines.append(f"{e}: item {i}/{n} waiting {missing}")
        raise RuntimeError("DEADLOCK in fused schedule:\n" + "\n".join(lines))
    return True

MID_NAMES = ["xn", "qk_fm", "kv_tm", "attn", "attnn", "x1", "x1h", "hbuf",
             "x2", "xn3", "v3", "q3", "k3", "gate3", "avg", "x3o",
             "h4", "hr4"]


def fused_input_names():
    names = []
    for name in PHASE_ORDER:
        _, ins, _ = PHASES[name]
        for n in ins:
            if n not in MID_NAMES and n not in names:
                names.append(n)
    return names


def build_fused_nc(debug_mids=False):
    nc = bass.Bass("TRN2", target_bir_lowering=False, debug=False)
    t = {}
    for n in fused_input_names():
        shape, dt = TENSORS[n]
        t[n] = nc.dram_tensor(n, list(shape), dt, kind="ExternalInput").ap()
    for n in MID_NAMES:
        shape, dt = TENSORS[n]
        kind = "ExternalOutput" if (debug_mids and n != "attnn") else "Internal"
        t[n] = nc.dram_tensor(n, list(shape), dt, kind=kind).ap()
    shape, dt = TENSORS["yT"]
    t["yT"] = nc.dram_tensor("yT", list(shape), dt, kind="ExternalOutput").ap()

    with contextlib.ExitStack() as sem_stack:
        sem_handles, sem_counts = {}, {}
        progs = []
        for name in PHASE_ORDER:
            build, ins, outs = PHASES[name]
            with contextlib.ExitStack() as sb_stack:
                p = Prog(nc, sb_stack, tag=name + "_", sem_stack=sem_stack,
                         sem_handles=sem_handles, sem_counts=sem_counts)
                build(p, t)
            progs.append(p)
        check_fused_schedule(progs)
        bar = sem_stack.enter_context(nc.semaphore("bar"))
        with nc.Block() as block:
            nphase = len(progs)

            def make_runner(eng):
                def runner(engine):
                    for k in range(nphase):
                        if k > 0:
                            engine.wait_ge(bar, 5 * k)
                        for fn in progs[k].items[eng]:
                            fn(engine)
                        engine.drain().then_inc(bar, 1)
                return runner
            for e in ENGS:
                getattr(block, e)(make_runner(e))
    return nc


def build_phase_nc(name):
    build, ins, outs = PHASES[name]
    nc = bass.Bass("TRN2", target_bir_lowering=False, debug=False)
    t = {}
    for n in ins:
        shape, dt = TENSORS[n]
        t[n] = nc.dram_tensor(n, list(shape), dt, kind="ExternalInput").ap()
    for n in outs:
        shape, dt = TENSORS[n]
        t[n] = nc.dram_tensor(n, list(shape), dt, kind="ExternalOutput").ap()
    with contextlib.ExitStack() as stack:
        p = Prog(nc, stack)
        build(p, t)
        p.emit()
    return nc, ins, outs



def _prep_weights(inputs):
    w = {}
    tr = lambda a: np.ascontiguousarray(np.asarray(a, np.float32).T)
    w["w_qkv"] = tr(np.asarray(inputs["L0_attn_qkv_w"]).reshape(H, 3, D, DM)
                    .transpose(1, 0, 2, 3).reshape(6144, DM))
    w["w_gate"] = tr(inputs["L0_attn_gate_w"])
    w["w_out0"] = tr(inputs["L0_attn_out_w"])
    w["nw_pp"] = _pp(inputs["L0_attn_norm_w"], 16)
    w["su_ug_pp"] = _pp(inputs["L0_ffn_SU_ug"], 16)
    w["wsu_pp"] = _pp(inputs["L0_ffn_Wscale_ug"][:8192], 64)
    w["wsg_pp"] = _pp(inputs["L0_ffn_Wscale_ug"][8192:], 64)
    w["su_dp_pp"] = _pp(inputs["L0_ffn_SU_dp"], 64)
    w["wsd_pp"] = _pp(inputs["L0_ffn_Wscale_dp"], 16)
    w["w_up"] = tr(inputs["L0_ffn_W_up"])
    w["w_wgate"] = tr(inputs["L0_ffn_W_gate"])
    w["w_down"] = tr(inputs["L0_ffn_W_down"])
    w["w_hid"] = inputs["L1_attn_W_hidden"]
    w["b_hid_row"] = np.asarray(inputs["L1_attn_b_hidden"], np.float32).reshape(1, 8192)
    w["b_hidg_pp"] = _pp(inputs["L1_attn_b_hidden"][4096:], 32)
    w["w_qk"] = inputs["L1_attn_W_qk"]
    for nm, key in (("b_qk_pp", "L1_attn_b_qk"), ("gq_pp", "L1_attn_gamma_q"),
                    ("bq_pp", "L1_attn_beta_q"), ("gk_pp", "L1_attn_gamma_k"),
                    ("bk_pp", "L1_attn_beta_k")):
        w[nm] = np.asarray(inputs[key], np.float32).reshape(128, 1)
    w["w_out1"] = inputs["L1_attn_W_out"]
    w["b_out1_pp"] = _pp(inputs["L1_attn_b_out"], 16)
    w["w_shg"] = tr(inputs["L1_ffn_sh_gate"])
    w["w_shu"] = tr(inputs["L1_ffn_sh_up"])
    w["w_shd"] = tr(inputs["L1_ffn_sh_down"])
    w["w_rg"] = tr(inputs["L1_ffn_r_gate"])
    w["w_ru"] = tr(inputs["L1_ffn_r_up"])
    w["w_rd"] = tr(inputs["L1_ffn_r_down"])
    w.update(_make_consts())
    return {k: np.ascontiguousarray(v) for k, v in w.items()}


def kernel(**inputs):
    x = inputs["L0_attn_hidden_states"]
    w = _prep_weights(inputs)

    state = [dict(w) for _ in range(N_CORES)]
    for c in range(N_CORES):
        state[c]["x0T"] = np.ascontiguousarray(np.asarray(x[c // 2], np.float32).T)

    if FUSED:
        nc = build_fused_nc()
        names = fused_input_names()
        in_maps = [{n: state[c][n] for n in names} for c in range(N_CORES)]
        res = run_bass_kernel_spmd(nc, in_maps, core_ids=list(range(N_CORES)))
        for c in range(N_CORES):
            state[c]["yT"] = res.results[c]["yT"]
    else:
        for name in PHASE_ORDER:
            nc, ins, outs = build_phase_nc(name)
            in_maps = [{n: state[c][n] for n in ins} for c in range(N_CORES)]
            res = run_bass_kernel_spmd(nc, in_maps, core_ids=list(range(N_CORES)))
            for c in range(N_CORES):
                for n in outs:
                    state[c][n] = res.results[c][n]

    out = np.empty((4, TOK, DM), dtype=np.float32)
    for c in range(N_CORES):
        b, half = c // 2, c % 2
        out[b, half * 1024:(half + 1) * 1024, :] = state[c]["yT"].T
    return out
